# version 22
# baseline (speedup 1.0000x reference)
import math
from contextlib import ExitStack

import numpy as np
import ml_dtypes
import concourse.bass as bass
import concourse.mybir as mybir
from concourse.bass_utils import run_bass_kernel_spmd

F32 = mybir.dt.float32
BF16 = mybir.dt.bfloat16
AF = mybir.ActivationFunctionType
ALU = mybir.AluOpType
AX = mybir.AxisListType

H = 2048
SEQ = 8192
NCORE = 8
TOK = SEQ // NCORE
KT = H // 128
NTT = TOK // 128
HEADS = 8
HD = 256
FF = 8192
NE = 16
KTOP = 512
EPS = 1e-5
C1G = 1.5957691216057308
C3G = 0.07135481627338621
DENOM_B = math.sqrt(2048.0) / 8192.0
BISECT_ITERS = 20

GAMMAS = (1.0 - np.exp(np.linspace(np.log(1.0 / 32), np.log(1.0 / 512), HEADS))).astype(np.float32)
NH = HEADS


class P:

    ENGS = ("t", "v", "s", "p", "y")

    def __init__(self, nc, ctx):
        self.nc = nc
        self.ctx = ctx
        self.ops = {e: [] for e in self.ENGS}
        self.sems = {}
        self.cnt = {}
        self.lastw = {e: {} for e in self.ENGS}
        self.bank_guard = [[] for _ in range(8)]

    def sem(self, name):
        self.sems[name] = self.ctx.enter_context(self.nc.semaphore(name))
        self.cnt[name] = 0
        return name

    def emit(self, e, fn, waits=(), inc=None, inc_by=1):
        w2 = []
        for sn, v in waits:
            if not v:
                continue
            if self.lastw[e].get(sn, 0) < v:
                self.lastw[e][sn] = v
                w2.append((sn, v))
        val = None
        if inc is not None:
            self.cnt[inc] += inc_by
            val = self.cnt[inc]
        sems = self.sems

        def gfun(eng, w2=tuple(w2), fn=fn, inc=inc, inc_by=inc_by):
            for sn, v in w2:
                eng.wait_ge(sems[sn], v)
            r = fn(eng)
            if inc is not None:
                r.then_inc(sems[inc], inc_by)

        self.ops[e].append(gfun)
        return val

    def finalize(self, block):
        for e, sec in (("y", block.sync), ("p", block.gpsimd), ("t", block.tensor),
                       ("v", block.vector), ("s", block.scalar)):
            ops = self.ops[e]

            def runner(eng, ops=ops):
                for f in ops:
                    f(eng)

            sec(runner)


def _host_consts(core):
    c = {}
    ident = np.eye(128, dtype=np.float32)
    c["c_idf"] = ident
    c["c_idb"] = ident.astype(ml_dtypes.bfloat16)
    a = np.arange(128)
    diff = a[None, :] - a[:, None]
    msk = (diff >= 0).astype(np.float32)
    c["c_mask01"] = msk.astype(ml_dtypes.bfloat16)
    ng = np.stack([g ** (-a.astype(np.float32)) for g in GAMMAS], axis=1)
    c["c_negj"] = ng.astype(np.float32)
    qs = np.stack([np.broadcast_to(g ** a.astype(np.float32), (128, 128)) for g in GAMMAS])
    c["c_qscale"] = np.ascontiguousarray(qs).astype(ml_dtypes.bfloat16)
    vs = np.stack([g ** (128.0 - a.astype(np.float32)) for g in GAMMAS], axis=1)
    c["c_vscale"] = vs.astype(np.float32)
    gd = np.zeros((128, HEADS * 8), dtype=np.float32)
    for h in range(HEADS):
        for r in range(8):
            if r < core:
                gd[:, h * 8 + r] = np.float32(GAMMAS[h]) ** np.float32(1024 * (core - r - 1))
    c["c_gdecay"] = gd
    cm = np.zeros((128, 8), dtype=np.float32)
    cm[:, :core] = 1.0
    c["c_cmask"] = cm
    ind_a = np.zeros((128, 16), dtype=np.float32)
    for p2 in range(128):
        ind_a[p2, p2 % 16] = 1.0
    c["c_inda"] = ind_a
    c["c_indb"] = np.ascontiguousarray(ind_a.T)
    a2 = np.arange(128)
    c["c_indf"] = (a2[:, None] % 16 == a2[None, :] % 16).astype(np.float32)
    sel = np.zeros((128, 1), dtype=np.float32)
    sel[16 * core:16 * core + 16, 0] = 1.0
    c["c_selw"] = sel
    return c


def _col_layout(v):
    v = np.asarray(v, dtype=np.float32).reshape(-1, 128)
    return np.ascontiguousarray(v.T)


def _host_cols(inputs):
    c = {}
    c["c_lncol"] = np.concatenate([_col_layout(inputs["L0_attn_a_ln_gamma"]),
                                   _col_layout(inputs["L0_attn_a_ln_beta"])], axis=1)
    b1 = np.asarray(inputs["L0_attn_a_b1"], dtype=np.float32)
    b2 = np.asarray(inputs["L0_attn_a_b2"], dtype=np.float32)
    be = np.asarray(inputs["L0_ffn_expert_b"], dtype=np.float32)
    c["c_bcol"] = np.concatenate([_col_layout(b1[0]), _col_layout(b1[1]),
                                  _col_layout(b2[0]), _col_layout(b2[1]),
                                  _col_layout(be)], axis=1)
    return c


def _const_specs():
    cc = _host_consts(0)
    specs = {k: (list(v.shape), BF16 if v.dtype == ml_dtypes.bfloat16 else F32)
             for k, v in cc.items()}
    specs["c_lncol"] = ([128, 2 * KT], F32)
    specs["c_bcol"] = ([128, 2 * 64 + 2 * KT + KT], F32)
    return specs


def build_program(upto=99, tap=None):
    nc = bass.Bass("TRN2", target_bir_lowering=False, debug=False)
    g = {}

    def dram_in(name, shape, dt=F32):
        g[name] = nc.dram_tensor(name, list(shape), dt, kind="ExternalInput").ap()

    dram_in("x_in", (TOK, H))
    for nm, shp in (
        ("L0_attn_a_Wq", (2, H, H)), ("L0_attn_a_Wk", (2, H, H)),
        ("L0_attn_a_Wv", (2, H, H)), ("L0_attn_a_Wg", (2, H, H)),
        ("L0_attn_a_Wo", (2, H, H)),
        ("L0_attn_a_gn_gamma", (2, H)), ("L0_attn_a_gn_beta", (2, H)),
        ("L0_attn_a_W1", (2, H, FF)), ("L0_attn_a_b1", (2, FF)),
        ("L0_attn_a_W2", (2, FF, H)), ("L0_attn_a_b2", (2, H)),
        ("L0_attn_a_ln_gamma", (H,)), ("L0_attn_a_ln_beta", (H,)),
        ("L0_attn_b_Wq", (H, H)), ("L0_attn_b_Wk", (H, H)),
        ("L0_attn_b_Wv", (H, H)), ("L0_attn_b_Wo", (H, H)),
        ("L0_ffn_router_w", (H, NE)), ("L0_ffn_router_b", (NE,)),
        ("L0_ffn_expert_w", (H, H)), ("L0_ffn_expert_b", (H,)),
    ):
        dram_in(nm, shp)
    for nm, (shp, dt) in _const_specs().items():
        dram_in(nm, shp, dt)

    g["y_out"] = nc.dram_tensor("y_out", [TOK, H], F32, kind="ExternalOutput").ap()
    g["spill_x"] = nc.dram_tensor("spill_x", [128, KT * TOK], F32).ap()
    g["cc_st_in"] = nc.dram_tensor("cc_st_in", [128, 512], BF16).ap()
    g["cc_st_out"] = nc.dram_tensor("cc_st_out", [128 * NCORE, 512], BF16).ap()
    g["cc_s_in"] = nc.dram_tensor("cc_s_in", [16, TOK], F32).ap()
    g["cc_s_out"] = nc.dram_tensor("cc_s_out", [16 * NCORE, TOK], F32).ap()
    g["cc_cs_in"] = nc.dram_tensor("cc_cs_in", [128, KT], F32).ap()
    g["cc_cs_out"] = nc.dram_tensor("cc_cs_out", [128 * NCORE, KT], F32).ap()

    ctx = ExitStack()
    E = ctx.enter_context
    with ctx:
        xT = E(nc.sbuf_tensor("xT", [128, KT * TOK], F32))
        xnT = E(nc.sbuf_tensor("xnT", [128, KT * TOK], BF16))
        big = E(nc.sbuf_tensor("big", [128, 16384], BF16))
        aux = E(nc.sbuf_tensor("aux", [128, 11264], BF16))
        wbuf = E(nc.sbuf_tensor("wbuf", [128, 2 * 4096], BF16))
        kT = E(nc.sbuf_tensor("kT", [128, 2 * TOK], BF16))
        vtk = E(nc.sbuf_tensor("vtk", [128, NTT * HD], BF16))
        gtk = E(nc.sbuf_tensor("gtk", [128, NTT * HD], BF16))
        scr = E(nc.sbuf_tensor("scr", [128, 4096], F32))
        rows = E(nc.sbuf_tensor("rows", [128, 1024], F32))
        cst = E(nc.sbuf_tensor("cst", [128, 224], F32))
        cstb = E(nc.sbuf_tensor("cstb", [128, 272], BF16))
        cidf = E(nc.sbuf_tensor("cidf", [128, 128], F32))
        cmask01 = E(nc.sbuf_tensor("cmask01", [128, 128], BF16))
        cqs = E(nc.sbuf_tensor("cqs", [128, 8 * 128], BF16))
        cinda = E(nc.sbuf_tensor("cinda", [128, 16], F32))
        cindb = E(nc.sbuf_tensor("cindb", [16, 128], F32))
        cindf = E(nc.sbuf_tensor("cindf", [128, 128], F32))
        lncol = E(nc.sbuf_tensor("lncol", [128, 2 * KT], F32))
        bcol = E(nc.sbuf_tensor("bcol", [128, 2 * 64 + 3 * KT], F32))
        gnga = E(nc.sbuf_tensor("gnga", [1, HD], BF16))
        gnbb = E(nc.sbuf_tensor("gnbb", [1, HD], BF16))
        rbrow = E(nc.sbuf_tensor("rbrow", [1, NE], F32))
        bis = E(nc.sbuf_tensor("bis", [16, 8], F32))

        psum = [E(nc.psum_tensor("pb%d" % i, [128, 512], F32)) for i in range(8)]

        zT = big
        Rpfx = aux[:, 0:4096]
        Rball = aux[:, 4096:6144]
        yhv = aux[:, 6144:8192]
        Rst = aux[:, 8192:10240].bitcast(F32)
        Rgcb = aux[:, 10240:10752]
        chz = aux[:, 10752:11264]
        h2 = big
        Sall = big[:, 0:2048].bitcast(F32)
        Smine = big[:, 2048:2304].bitcast(F32)
        SmT = big[0:16, 2304:4352].bitcast(F32)
        cmpb = big[:, 4352:6400].bitcast(F32)
        wBsb = big[:, 6400:8448].bitcast(F32)
        moescr = big[:, 8448:10496].bitcast(F32)
        kvh = big
        wTb = aux[:, 0:8192]
        abkv = aux[:, 8192:10240].bitcast(F32)

        p = P(nc, ctx)
        for snm in ("T", "V", "S", "DW0", "DW1", "DX0", "DX1", "DXC", "DXG", "DCU", "DCG"):
            p.sem(snm)
        NCC = 2 * HEADS + 3
        for i in range(NCC):
            p.sem("CC%d" % i)
        ccn = [0]
        ccbuf_guard = {}

        def mm(out, lhsT, rhs, start, stop, waits=(), inc=None):
            if stop:
                inc = "T"
            return p.emit("t", lambda t, o=out, l=lhsT, r=rhs, a=start, b=stop:
                          t.matmul(o, l, r, start=a, stop=b), waits=waits, inc=inc)

        def dma(queue, out, in_, sem, waits=()):
            return p.emit(queue, lambda q, o=out, i=in_: q.dma_start(out=o, in_=i),
                          waits=waits, inc=sem, inc_by=16)

        def vcopy(dst, src, waits=(), eng="v"):
            sem = "V" if eng == "v" else "S"
            return p.emit(eng, lambda e, o=dst, i=src:
                          (e.tensor_copy(o, i) if eng == "v" else e.copy(o, i)),
                          waits=waits, inc=sem)

        def tr128(dst, src, bank, ident_b, t_waits=(), v_waits=(), eng="v"):
            if ident_b:
                pv = psum[bank][:, 0:64].bitcast(BF16)
                idt = cstb[:, 0:128]
            else:
                pv = psum[bank][:, 0:128]
                idt = cidf[:]
            tv = p.emit("t", lambda t, o=pv, i=src: t.transpose(o, i, idt),
                        waits=list(t_waits) + p.bank_guard[bank], inc="T")
            sem = "V" if eng == "v" else "S"
            vv = p.emit(eng, lambda e, o=dst, i=pv:
                        (e.tensor_copy(o, i) if eng == "v" else e.copy(o, i)),
                        waits=[("T", tv)] + list(v_waits), inc=sem)
            p.bank_guard[bank] = [(sem, vv)]
            return vv

        wslot_guard = [[], []]

        def wload(slot, wap, r0, rk, c0, cw, layer=None):
            view = wbuf[:, slot * 4096: slot * 4096 + rk * cw]
            src = wap if layer is None else wap[layer]
            src = src[r0:r0 + rk * 128, c0:c0 + cw].rearrange("(k p) n -> p k n", p=128)
            step = max(1, 2048 // 128)
            d = None
            sd = "DW%d" % slot
            for ks in range(0, rk, step):
                ke = min(rk, ks + step)
                d = dma("p", view.rearrange("p (k n) -> p k n", n=cw)[:, ks:ke, :],
                        src[:, ks:ke, :], sd,
                        waits=wslot_guard[slot] if ks == 0 else ())
            return view, (sd, d)

        def wdone(slot):
            wslot_guard[slot] = [("T", p.cnt["T"])]

        dma("y", cidf[:], g["c_idf"], "DXC")
        dma("y", cstb[:, 0:128], g["c_idb"], "DXC")
        p.emit("v", lambda v: v.memset(cst[:, 0:1], 1.0), inc="V")
        p.emit("v", lambda v: v.memset(cst[:, 90:91], float(EPS)), inc="V")
        nc.const_aps.register(F32, float(EPS), cst[:, 90:91])
        p.emit("v", lambda v: v.memset(cst[0:1, 96:224], 1.0), inc="V")
        p.emit("v", lambda v: v.memset(cstb[:, 128:129], 1.0), inc="V")
        p.emit("v", lambda v: v.memset(cstb[0:1, 144:272], 1.0), inc="V")
        dma("y", cst[:, 1:9], g["c_vscale"], "DXC")
        dma("y", cst[:, 9:17], g["c_cmask"], "DXC")
        dma("y", cst[:, 17:81], g["c_gdecay"], "DXC")
        dma("y", cst[:, 81:82], g["c_selw"], "DXC")
        dma("y", cmask01[:], g["c_mask01"], "DXC")
        dma("y", cst[:, 82:90], g["c_negj"], "DXC")
        dma("y", cqs[:].rearrange("p (h i) -> p h i", h=8),
            g["c_qscale"].rearrange("h p i -> p h i"), "DXC")
        dma("y", cinda[:], g["c_inda"], "DXC")
        dma("y", cindb[:], g["c_indb"], "DXC")
        dma("y", cindf[:], g["c_indf"], "DXC")
        dma("y", lncol[:], g["c_lncol"], "DXC")
        dma("y", bcol[:], g["c_bcol"], "DXC")
        dma("y", rbrow[:], g["L0_ffn_router_b"][None, :], "DXC")
        CONST_DX = [("DXC", p.cnt["DXC"]), ("V", p.cnt["V"])]

        scrhalf_guard = [[], []]

        def stage_in(src_dram, add=False, extra=()):
            for tt in range(NTT):
                hbuf = scr[:, (tt % 2) * 2048:(tt % 2) * 2048 + 2048]
                sd = "DX%d" % (tt % 2)
                d = dma("y", hbuf, src_dram[tt * 128:(tt + 1) * 128, :], sd,
                        waits=scrhalf_guard[tt % 2] + list(extra))
                for k in range(KT):
                    b = 2 + (k % 2)
                    sl = slice(k * TOK + tt * 128, k * TOK + tt * 128 + 128)
                    if not add:
                        tr128(xT[:, sl], hbuf[:, k * 128:(k + 1) * 128], b, False,
                              t_waits=[(sd, d)] + CONST_DX)
                    else:
                        pv = psum[b][:, 0:128]
                        tv = p.emit("t", lambda t, o=pv, i=hbuf[:, k * 128:(k + 1) * 128]:
                                    t.transpose(o, i, cidf[:]),
                                    waits=[(sd, d)] + p.bank_guard[b], inc="T")
                        vv = p.emit("v", lambda v, o=xT[:, sl], i=pv:
                                    v.tensor_tensor(o, o, i, ALU.add),
                                    waits=[("T", tv)] + xg(), inc="V")
                        p.bank_guard[b] = [("V", vv)]
                scrhalf_guard[tt % 2] = [("T", p.cnt["T"])]
            xT_guard[0] = [("V", p.cnt["V"])]
            for k in range(KT):
                xTk_guard[k] = [("V", p.cnt["V"])]

        def stage_out(dst_dram, waits=()):
            for tt in range(NTT):
                hbuf = scr[:, (tt % 2) * 2048:(tt % 2) * 2048 + 2048]
                sd = "DX%d" % (tt % 2)
                last = None
                for k in range(KT):
                    b = 2 + (k % 2)
                    sl = slice(k * TOK + tt * 128, k * TOK + tt * 128 + 128)
                    last = tr128(hbuf[:, k * 128:(k + 1) * 128], xT[:, sl], b, False,
                                 t_waits=list(waits), v_waits=scrhalf_guard[tt % 2])
                d = dma("y", dst_dram[tt * 128:(tt + 1) * 128, :], hbuf, sd,
                        waits=[("V", last)])
                scrhalf_guard[tt % 2] = [(sd, d)]
            p.emit("y", lambda q: q.nop(),
                   waits=[("DX0", p.cnt["DX0"]), ("DX1", p.cnt["DX1"])])

        xT_guard = [[]]
        xTk_guard = [[] for _ in range(KT)]
        gn_guard = [[]]
        gath_guard = [[]]

        def xg():
            return list(xT_guard[0])

        def xkw():
            return [list(xTk_guard[k]) for k in range(KT)]

        bcopy_guard = [[], []]
        norm_guard = [[], []]

        def layernorm(dst_b16, src, in_waits, gamma_col=None, beta_col=None, dst_f32=None,
                      dst_waits=(), in_waits_k=None):
            nv = ns = None
            for k in range(KT):
                bb = (vtk if k % 2 == 0 else kT)[:, 0:1024]
                sq = (vtk if k % 2 == 0 else kT)[:, 1024:2048]
                wk_ = list(in_waits)
                nv = p.emit("s", lambda s, o=bb, i=src[:, k * TOK:(k + 1) * TOK]:
                            s.copy(o, i),
                            waits=wk_ + bcopy_guard[k % 2], inc="S")
                ns = p.emit("s", lambda s, o=sq, i=bb: s.activation(o, i, AF.Square),
                            waits=[("S", nv)] + bcopy_guard[k % 2], inc="S")
                for half in range(2):
                    mm(psum[half][0:1, :], cstb[:, 128:129], bb[:, half * 512:(half + 1) * 512],
                       start=(k == 0), stop=(k == KT - 1),
                       waits=[("S", nv)] + (p.bank_guard[half] if k == 0 else []),
                       inc="T")
                    mm(psum[2 + half][0:1, :], cstb[:, 128:129], sq[:, half * 512:(half + 1) * 512],
                       start=(k == 0), stop=(k == KT - 1),
                       waits=[("S", ns)] + (p.bank_guard[2 + half] if k == 0 else []),
                       inc="T")
                bcopy_guard[k % 2] = [("T", p.cnt["T"])]
            tdone = p.cnt["T"]
            bmm = None
            for half in range(2):
                vv = p.emit("v", lambda v, hh=half: v.tensor_scalar(
                    rows[0:1, 0:512], psum[hh][0:1, :], 1.0 / H, None, ALU.mult),
                    waits=[("T", tdone), ("T", p.cnt["T"])], inc="V")
                vv = p.emit("v", lambda v, hh=half: v.tensor_scalar(
                    rows[0:1, 512:1024], psum[2 + hh][0:1, :], 1.0 / H, None, ALU.mult),
                    waits=[("V", vv)], inc="V")
                bmm = mm(psum[4 + half][:, :], cst[0:1, 96:224], rows[0:1, 0:512],
                         start=True, stop=True,
                         waits=[("V", vv)] + p.bank_guard[4 + half])
                vv = p.emit("v", lambda v: v.tensor_tensor(
                    rows[0:1, 0:512], rows[0:1, 0:512], rows[0:1, 0:512], ALU.mult),
                    waits=[("V", vv), ("T", bmm)], inc="V")
                vv = p.emit("v", lambda v: v.tensor_tensor(
                    rows[0:1, 0:512], rows[0:1, 512:1024], rows[0:1, 0:512], ALU.subtract),
                    waits=[("V", vv)], inc="V")
                sv = p.emit("s", lambda s: s.activation(rows[0:1, 0:512], rows[0:1, 0:512],
                                                        AF.Ln, bias=float(EPS)),
                            waits=[("V", vv)], inc="S")
                sv = p.emit("s", lambda s: s.activation(rows[0:1, 0:512], rows[0:1, 0:512],
                                                        AF.Exp, scale=-0.5),
                            waits=[("S", sv)], inc="S")
                bmm = mm(psum[6 + half][:, :], cst[0:1, 96:224], rows[0:1, 0:512],
                         start=True, stop=True,
                         waits=[("S", sv)] + p.bank_guard[6 + half])
            vfin = p.cnt["V"]
            lasts = lastv = None
            half_done = {}
            hcnt = {}
            for half in range(2):
                for k in range(KT):
                    sl = slice(k * TOK + half * 512, k * TOK + (half + 1) * 512)
                    if dst_f32 is None:
                        t1 = scr[:, (k % 2) * 1024 + half * 512:(k % 2) * 1024 + half * 512 + 512]
                        exw = norm_guard[k % 2] + scrhalf_guard[0] + scrhalf_guard[1]
                    else:
                        t1 = dst_f32[:, sl]
                        exw = list(dst_waits)
                    lastv = p.emit("v", lambda v, o=t1, i=src[:, sl], mb=psum[4 + half][:, :]:
                                   v.tensor_tensor(o, i, mb, ALU.subtract),
                                   waits=[("T", bmm)] + exw, inc="V")
                    lastv = p.emit("v", lambda v, o=t1, rb=psum[6 + half][:, :]:
                                   v.tensor_tensor(o, o, rb, ALU.mult),
                                   waits=[("V", lastv)], inc="V")
                    if gamma_col is not None:
                        lasts = p.emit("s", lambda s, o=dst_b16[:, sl], i=t1,
                                       gc=gamma_col[:, k:k + 1], bc=beta_col[:, k:k + 1]:
                                       s.activation(o, i, AF.Identity, scale=gc, bias=bc),
                                       waits=[("V", lastv)], inc="S")
                    else:
                        lasts = p.emit("s", lambda s, o=dst_b16[:, sl], i=t1: s.copy(o, i),
                                       waits=[("V", lastv)], inc="S")
                    norm_guard[k % 2] = [("S", p.cnt["S"])]
                    if dst_f32 is not None and half == 1:
                        xTk_guard[k] = [("V", lastv)]
                half_done[half] = [("S", p.cnt["S"]), ("V", p.cnt["V"])]
                hcnt[half] = (p.cnt["V"], p.cnt["S"])
            if dst_f32 is None:
                scrhalf_guard[0] = scrhalf_guard[0] + [("S", p.cnt["S"])]
                scrhalf_guard[1] = scrhalf_guard[1] + [("S", p.cnt["S"])]
            for b in range(4):
                p.bank_guard[b] = [("V", vfin)]
            for half in range(2):
                p.bank_guard[4 + half] = [("V", hcnt[half][0]), ("S", hcnt[half][1])]
                p.bank_guard[6 + half] = [("V", hcnt[half][0]), ("S", hcnt[half][1])]
            half_done["all"] = [("S", lasts), ("V", lastv)]
            return half_done

        def retention_layer(l):
            xnw = layernorm(xnT, xT, xg(), gamma_col=lncol[:, 0:KT], beta_col=lncol[:, KT:2 * KT],
                            in_waits_k=xkw())
            zT_guard = [[]]
            wq, dq = wload(0, g["L0_attn_a_Wq"], 0, KT, 0 * HD, HD, layer=l)
            pend = {"q": (wq, dq, 0)}
            fin = [None]
            for h in range(NH):
                retention_head(l, h, xnw, pend, zT_guard, fin)
            fin[0]()
            fin[0] = None
            wo_pass(l, zT_guard)
            ffn(l)

        yhv_guard = [[]]

        def retention_head(l, h, xnw, pend, zT_guard, fin):
            ccq = ccn[0]
            ccn[0] += 1
            gam128 = float(np.float32(GAMMAS[h]) ** 128)
            qT = big[:, (2 * h) * TOK:(2 * h + 2) * TOK]
            q2b = scr[:, 1024 + (h % 2) * 1024: 2048 + (h % 2) * 1024].bitcast(BF16)
            wq, dq, slq = pend["q"]
            wk, dk = wload(1 - slq, g["L0_attn_a_Wk"], 0, KT, h * HD, HD, layer=l)
            for dt in range(2):
                for half in range(2):
                    bk = dt * 2 + half
                    for k in range(KT):
                        mm(psum[bk][:, :], wq[:, k * HD + dt * 128: k * HD + dt * 128 + 128],
                           xnT[:, k * TOK + half * 512: k * TOK + (half + 1) * 512],
                           start=(k == 0), stop=(k == KT - 1),
                           waits=([dq] + list(xnw[half]) + p.bank_guard[bk]) if k == 0 else [])
                    vcopy(qT[:, dt * TOK + half * 512: dt * TOK + (half + 1) * 512],
                          psum[bk][:, :], waits=[("T", p.cnt["T"])], eng="s")
                    p.bank_guard[bk] = [("S", p.cnt["S"])]
            wdone(slq)
            wv, dv = wload(slq, g["L0_attn_a_Wv"], 0, KT, h * HD, HD, layer=l)
            for dt in range(2):
                for half in range(2):
                    bk = 4 + dt * 2 + half
                    for k in range(KT):
                        mm(psum[bk][:, :], wk[:, k * HD + dt * 128: k * HD + dt * 128 + 128],
                           xnT[:, k * TOK + half * 512: k * TOK + (half + 1) * 512],
                           start=(k == 0), stop=(k == KT - 1),
                           waits=([dk] + list(xnw[half]) + p.bank_guard[bk]) if k == 0 else [])
                    vcopy(kT[:, dt * TOK + half * 512: dt * TOK + (half + 1) * 512],
                          psum[bk][:, :], waits=[("T", p.cnt["T"])], eng="s")
                    p.bank_guard[bk] = [("S", p.cnt["S"])]
            wdone(1 - slq)
            wg_, dg = wload(1 - slq, g["L0_attn_a_Wg"], 0, KT, h * HD, HD, layer=l)
            if fin[0] is not None:
                fin[0]()
                fin[0] = None
            head_T0 = p.cnt["T"]
            rz = p.emit("v", lambda v: v.memset(Rst[:, 0:512], 0.0),
                        waits=[("V", p.cnt["V"]), ("S", p.cnt["S"])], inc="V")
            k_ready_S = p.cnt["S"]
            cons_T = [0, 0]
            for i in range(NTT):
                bk = i % 4
                for k in range(KT):
                    mm(psum[bk][:, 0:HD], xnT[:, k * TOK + i * 128: k * TOK + i * 128 + 128],
                       wv[:, k * HD:(k + 1) * HD],
                       start=(k == 0), stop=(k == KT - 1),
                       waits=([dv] + list(xnw[i // 4]) + p.bank_guard[bk]) if k == 0 else [])
                sv_ = vcopy(vtk[:, i * HD:(i + 1) * HD], psum[bk][:, 0:HD],
                            waits=[("T", p.cnt["T"])], eng="s")
                p.bank_guard[bk] = [("S", sv_)]
                bi = i % 2
                base = bi * 128
                ktok = rows[:, base:base + 128].bitcast(BF16)
                vp = rows[:, 512 + base:512 + base + 128].bitcast(BF16)
                pv = vcopy(Rpfx[:, i * 512:(i + 1) * 512], Rst[:, 0:512],
                           waits=[("V", p.cnt["V"]), ("T", head_T0)], eng="s")
                evtr = None
                for dt in range(2):
                    evtr = tr128(ktok[:, dt * 128:(dt + 1) * 128],
                                 kT[:, dt * TOK + i * 128: dt * TOK + i * 128 + 128],
                                 4 + bi, True,
                                 t_waits=[("S", k_ready_S)],
                                 v_waits=[("T", cons_T[bi])] if cons_T[bi] else [])
                vv2 = p.emit("v", lambda v, o=vp, i2=vtk[:, i * HD:(i + 1) * HD],
                             sc=cst[:, 1 + h:2 + h]:
                             v.tensor_scalar(o, i2, sc, None, ALU.mult),
                             waits=[("S", sv_), ("V", p.cnt["V"]),
                                    ("T", cons_T[bi])], inc="V")
                for dt in range(2):
                    mm(psum[6 + dt][:, 0:HD], ktok[:, dt * 128:(dt + 1) * 128], vp,
                       start=True, stop=True,
                       waits=[("V", vv2), ("V", evtr)] + p.bank_guard[6 + dt])
                tui = p.cnt["T"]
                cons_T[bi] = tui
                for dt in range(2):
                    uv = p.emit("v", lambda v, o=Rst[:, dt * 256:(dt + 1) * 256],
                                ps=psum[6 + dt][:, 0:HD], gm=gam128:
                                v.scalar_tensor_tensor(o, o, gm, ps, ALU.mult, ALU.add),
                                waits=[("T", tui), ("S", pv)], inc="V")
                    p.bank_guard[6 + dt] = [("V", uv)]
            pfx_S = p.cnt["S"]
            wdone(slq)
            if h + 1 < NH:
                wqn, dqn = wload(slq, g["L0_attn_a_Wq"], 0, KT, (h + 1) * HD, HD, layer=l)
                pend["q"] = (wqn, dqn, slq)
            bv = vcopy(Rgcb[:, 0:512], Rst[:, 0:512],
                       waits=[("V", p.cnt["V"])] + ccbuf_guard.get("st_gcb", []))
            dcc = dma("y", g["cc_st_in"], Rgcb[:, 0:512], "DCU",
                      waits=[("V", bv)] + ccbuf_guard.get("st_in", []))
            ccs = "CC%d" % ccq
            p.emit("p", lambda gp: gp.collective_compute(
                "AllGather", ALU.bypass, replica_groups=[list(range(NCORE))],
                ins=[g["cc_st_in"]], outs=[g["cc_st_out"]]),
                waits=[("DCU", dcc)] + ccbuf_guard.get("st_out", []), inc=ccs)
            ccbuf_guard["st_in"] = [(ccs, 1)]
            for tt in range(NTT):
                bk = tt % 4
                for k in range(KT):
                    mm(psum[bk][:, 0:HD], xnT[:, k * TOK + tt * 128: k * TOK + tt * 128 + 128],
                       wg_[:, k * HD:(k + 1) * HD],
                       start=(k == 0), stop=(k == KT - 1),
                       waits=([dg] + list(xnw[tt // 4]) + p.bank_guard[bk]) if k == 0 else [])
                sg = p.emit("s", lambda s, o=gtk[:, tt * HD:(tt + 1) * HD], i2=psum[bk][:, 0:HD]:
                            s.activation(o, i2, AF.Sigmoid),
                            waits=[("T", p.cnt["T"])] + yhv_guard[0], inc="S")
                gv = p.emit("v", lambda v, o=gtk[:, tt * HD:(tt + 1) * HD],
                            ps=psum[bk][:, 0:HD]:
                            v.tensor_tensor(o, o, ps, ALU.mult),
                            waits=[("S", sg)], inc="V")
                p.bank_guard[bk] = [("V", gv)]
            wdone(1 - slq)
            pend["slot_for_next"] = 1 - slq
            qsc = cqs[:, h * 128:(h + 1) * 128]
            consq_T = [0, 0]
            for i in range(NTT):
                bi = i % 2
                base = bi * 256
                q1 = rows[:, base:base + 128].bitcast(BF16)
                smT = rows[:, 576 + bi * 64:576 + bi * 64 + 64].bitcast(BF16)
                vq = None
                for dt in range(2):
                    vq = p.emit("v", lambda v, o=q1[:, dt * 128:(dt + 1) * 128],
                                i2=qT[:, dt * TOK + i * 128: dt * TOK + i * 128 + 128], qs=qsc:
                                v.tensor_tensor(o, i2, qs, ALU.mult),
                                waits=([("T", consq_T[bi])] if consq_T[bi] else []) +
                                      [("V", p.cnt["V"])], inc="V")
                bsc = 4 + bi * 2
                for dt in range(2):
                    mm(psum[bsc][:, 0:128], kT[:, dt * TOK + i * 128: dt * TOK + i * 128 + 128],
                       q1[:, dt * 128:(dt + 1) * 128],
                       start=(dt == 0), stop=(dt == 1),
                       waits=([("V", vq)] + p.bank_guard[bsc]) if dt == 0 else [])
                tsc = p.cnt["T"]
                for dt in range(2):
                    p.emit("v", lambda v, o=q2b[:, dt * TOK + i * 128: dt * TOK + i * 128 + 128],
                           i2=q1[:, dt * 128:(dt + 1) * 128],
                           gm=float(np.float32(GAMMAS[h]) ** (128 * i)):
                           v.tensor_scalar(o, i2, gm, None, ALU.mult),
                           waits=[("T", tsc), ("V", vq)], inc="V")
                sv2 = p.emit("v", lambda v, o=smT, ps=psum[bsc][:, 0:128],
                             nj=cst[:, 82 + h:83 + h], mk=cmask01[:]:
                             v.scalar_tensor_tensor(o, ps, nj, mk, ALU.mult, ALU.mult),
                             waits=[("T", tsc)], inc="V")
                p.bank_guard[bsc] = [("V", sv2)]
                bky = 5 + bi * 2
                mm(psum[bky][:, 0:HD], smT, vtk[:, i * HD:(i + 1) * HD],
                   start=True, stop=False,
                   waits=[("V", sv2), ("V", vq), ("S", pfx_S)] + p.bank_guard[bky])
                for dt in range(2):
                    mm(psum[bky][:, 0:HD], q1[:, dt * 128:(dt + 1) * 128],
                       Rpfx[:, i * 512 + dt * 256: i * 512 + (dt + 1) * 256],
                       start=False, stop=(dt == 1), waits=[])
                tyi = p.cnt["T"]
                yv = vcopy(yhv[:, i * HD:(i + 1) * HD], psum[bky][:, 0:HD],
                           waits=[("T", tyi)] + yhv_guard[0], eng="s")
                p.bank_guard[bky] = [("S", yv)]
                consq_T[bi] = tyi

            def stage2():
                Gc = Rst[:, 512:1024]
                gv = p.emit("v", lambda v: v.memset(Gc, 0.0), waits=[("V", p.cnt["V"])], inc="V")
                for halfq in range(2):
                    d2 = dma("y", Rball[:, 0:2048].rearrange("p (r n) -> p r n", r=4),
                             g["cc_st_out"].rearrange("(r p) n -> p r n", p=128)[:, halfq * 4:(halfq + 1) * 4, :],
                             "DCG", waits=[(ccs, 1), ("V", gv)] + gath_guard[0])
                    for rr in range(4):
                        r = halfq * 4 + rr
                        gv2 = p.emit("v", lambda v, o=Gc, i=Rball[:, rr * 512:(rr + 1) * 512],
                                     sc=cst[:, 17 + h * 8 + r:18 + h * 8 + r]:
                                     v.scalar_tensor_tensor(o, i, sc, o, ALU.mult, ALU.add),
                                     waits=[("DCG", d2), ("V", gv)], inc="V")
                        gv = gv2
                gath_guard[0] = [("V", gv)]
                ccbuf_guard["st_out"] = [("DCG", p.cnt["DCG"])]
                gcb = vcopy(Rgcb[:, 0:512], Gc, waits=[("V", gv), ("DCU", dcc)])
                q2done = p.cnt["V"]
                for i in range(NTT):
                    bi = i % 2
                    bky = 5 + bi * 2
                    for dt in range(2):
                        mm(psum[bky][:, 0:HD], q2b[:, dt * TOK + i * 128: dt * TOK + i * 128 + 128],
                           Rgcb[:, dt * 256:(dt + 1) * 256],
                           start=(dt == 0), stop=(dt == 1),
                           waits=([("V", gcb), ("V", q2done)] + p.bank_guard[bky]) if dt == 0 else [])
                    tyg = p.cnt["T"]
                    yg = p.emit("v", lambda v, o=yhv[:, i * HD:(i + 1) * HD],
                                ps=psum[bky][:, 0:HD]:
                                v.tensor_tensor(o, o, ps, ALU.add),
                                waits=[("T", tyg), ("S", p.cnt["S"])], inc="V")
                    p.bank_guard[bky] = [("V", yg)]
                ccbuf_guard["st_gcb"] = [("T", p.cnt["T"])]
                ydone = [("S", p.cnt["S"])]
                bn1 = None
                for cq in range(8):
                    bn1 = p.emit("v", lambda v, c2=cq: v.bn_stats(
                        rows[:, 640 + c2 * 6:646 + c2 * 6], yhv[:, c2 * 256:(c2 + 1) * 256]),
                        waits=ydone + [("V", p.cnt["V"])], inc="V")
                bn2 = None
                for cq in range(8):
                    bn2 = p.emit("v", lambda v, c2=cq: v.bn_aggr(
                        rows[:, 688 + c2 * 2:690 + c2 * 2], rows[:, 640 + c2 * 6:646 + c2 * 6]),
                        waits=[("V", bn1)], inc="V")
                sv = p.emit("s", lambda s: s.activation(rows[:, 704:712],
                            rows[:, 688:704].rearrange("p (c n) -> p c n", c=8)[:, :, 1:2],
                            AF.Ln, bias=float(EPS)), waits=[("V", bn2)], inc="S")
                sv = p.emit("s", lambda s: s.activation(rows[:, 704:712], rows[:, 704:712],
                                                        AF.Exp, scale=-0.5),
                            waits=[("S", sv)], inc="S")
                dgn = dma("p", gnga[:], g["L0_attn_a_gn_gamma"][l][None, h * HD:(h + 1) * HD],
                          "DXG", waits=gn_guard[0])
                dgn = dma("p", gnbb[:], g["L0_attn_a_gn_beta"][l][None, h * HD:(h + 1) * HD],
                          "DXG", waits=())
                gmm = mm(psum[4][:, 0:HD], cstb[0:1, 144:272], gnga[:], start=True, stop=True,
                         waits=[("DXG", dgn)] + p.bank_guard[4])
                gmm = mm(psum[5][:, 0:HD], cstb[0:1, 144:272], gnbb[:], start=True, stop=True,
                         waits=p.bank_guard[5])
                gn_guard[0] = [("T", p.cnt["T"])]
                lastz = None
                for i in range(NTT):
                    t1 = scr[:, 0:256]
                    zv = p.emit("v", lambda v, o=t1, i2=yhv[:, i * HD:(i + 1) * HD],
                                mc=rows[:, 688 + 2 * i:689 + 2 * i], rc=rows[:, 704 + i:705 + i]:
                                v.tensor_scalar(o, i2, mc, rc, ALU.subtract, op1=ALU.mult),
                                waits=[("S", sv), ("V", p.cnt["V"]), ("T", gmm)], inc="V")
                    zv = p.emit("v", lambda v, o=t1, gb=psum[4][:, 0:HD]:
                                v.tensor_tensor(o, o, gb, ALU.mult), waits=[("V", zv)], inc="V")
                    zv = p.emit("v", lambda v, o=t1, bb=psum[5][:, 0:HD]:
                                v.tensor_tensor(o, o, bb, ALU.add), waits=[("V", zv)], inc="V")
                    zv = p.emit("v", lambda v, o=chz[:, 0:256], i2=t1, gk=gtk[:, i * HD:(i + 1) * HD]:
                                v.tensor_tensor(o, i2, gk, ALU.mult), waits=[("V", zv)], inc="V")
                    for dt in range(2):
                        kz = h * 2 + dt
                        lastz = tr128(zT[:, kz * TOK + i * 128: kz * TOK + i * 128 + 128],
                                      chz[:, dt * 128:(dt + 1) * 128], 6 + dt, True,
                                      t_waits=[("V", zv)])
                p.bank_guard[4] = [("V", p.cnt["V"])]
                p.bank_guard[5] = [("V", p.cnt["V"])]
                zT_guard[0] = zT_guard[0] + [("V", p.cnt["V"])]
                yhv_guard[0] = [("V", p.cnt["V"])]

            fin[0] = stage2

        def wo_pass(l, zT_guard):
            slot = [0]
            wo, do = wload(0, g["L0_attn_a_Wo"], 0, KT, 0, HD, layer=l)
            for sl8 in range(8):
                won, don = (wload(1 - slot[0], g["L0_attn_a_Wo"], 0, KT, (sl8 + 1) * HD, HD, layer=l)
                            if sl8 + 1 < 8 else (None, None))
                for co in range(2):
                    ct = sl8 * 2 + co
                    for half in range(2):
                        bk = (ct * 2 + half) % 4
                        for k in range(KT):
                            mm(psum[bk][:, :], wo[:, k * HD + co * 128: k * HD + co * 128 + 128],
                               zT[:, k * TOK + half * 512: k * TOK + (half + 1) * 512],
                               start=(k == 0), stop=(k == KT - 1),
                               waits=([do] + zT_guard[0] + p.bank_guard[bk]) if k == 0 else [])
                        sl = slice(ct * TOK + half * 512, ct * TOK + (half + 1) * 512)
                        rv = p.emit("v", lambda v, o=xT[:, sl], ps=psum[bk][:, :]:
                                    v.tensor_tensor(o, o, ps, ALU.add),
                                    waits=[("T", p.cnt["T"])] + xg(), inc="V")
                        p.bank_guard[bk] = [("V", rv)]
                        if half == 1:
                            xTk_guard[ct] = [("V", rv)]
                wdone(slot[0])
                if won is not None:
                    wo, do = won, don
                    slot[0] = 1 - slot[0]
            xT_guard[0] = [("V", p.cnt["V"])]

        def ffn(l):
            rnw = layernorm(xnT, xT, xg(), gamma_col=lncol[:, 0:KT], beta_col=lncol[:, KT:2 * KT],
                            in_waits_k=xkw())
            nffs = 4
            cur = [0]
            wv_, dv_ = wload(0, g["L0_attn_a_W1"], 0, KT, 0, HD, layer=l)
            for ffs in range(nffs):
                for s8 in range(8):
                    if s8 < 7:
                        nxt = wload(1 - cur[0], g["L0_attn_a_W1"], 0, KT,
                                    (ffs * 16 + (s8 + 1) * 2) * 128, HD, layer=l)
                    else:
                        nxt = wload(1 - cur[0], g["L0_attn_a_W2"], ffs * 2048, KT, 0, HD, layer=l)
                    for fi in range(2):
                        ft = ffs * 16 + s8 * 2 + fi
                        for half in range(2):
                            bk = (fi * 2 + half) % 4
                            for k in range(KT):
                                mm(psum[bk][:, :], wv_[:, k * HD + fi * 128: k * HD + fi * 128 + 128],
                                   xnT[:, k * TOK + half * 512: k * TOK + (half + 1) * 512],
                                   start=(k == 0), stop=(k == KT - 1),
                                   waits=([dv_] + list(rnw[half]) + p.bank_guard[bk]) if k == 0 else [])
                            gelu_evac(psum[bk], h2[:, (ft % 16) * TOK + half * 512:
                                                   (ft % 16) * TOK + (half + 1) * 512],
                                      bcol[:, l * 64 + ft: l * 64 + ft + 1], bk)
                    wdone(cur[0])
                    wv_, dv_ = nxt
                    cur[0] = 1 - cur[0]
                h2done = [("V", p.cnt["V"])]
                for cs in range(8):
                    if cs < 7:
                        nxt = wload(1 - cur[0], g["L0_attn_a_W2"], ffs * 2048, KT,
                                    (cs + 1) * HD, HD, layer=l)
                    elif ffs + 1 < nffs:
                        nxt = wload(1 - cur[0], g["L0_attn_a_W1"], 0, KT,
                                    (ffs + 1) * 2048, HD, layer=l)
                    else:
                        nxt = None
                    for co in range(2):
                        ct = cs * 2 + co
                        for half in range(2):
                            bk = (co * 2 + half) % 4 + 4
                            for k in range(KT):
                                mm(psum[bk][:, :], wv_[:, k * HD + co * 128: k * HD + co * 128 + 128],
                                   h2[:, k * TOK + half * 512: k * TOK + (half + 1) * 512],
                                   start=(k == 0), stop=(k == KT - 1),
                                   waits=([dv_] + h2done + p.bank_guard[bk]) if k == 0 else [])
                            sl = slice(ct * TOK + half * 512, ct * TOK + (half + 1) * 512)
                            if ffs == 0:
                                rv = p.emit("v", lambda v, o=xT[:, sl], ps=psum[bk][:, :],
                                            bc=bcol[:, 128 + l * KT + ct:129 + l * KT + ct]:
                                            v.scalar_tensor_tensor(o, ps, bc, o, ALU.add, ALU.add),
                                            waits=[("T", p.cnt["T"])] + xg(), inc="V")
                            else:
                                rv = p.emit("v", lambda v, o=xT[:, sl], ps=psum[bk][:, :]:
                                            v.tensor_tensor(o, o, ps, ALU.add),
                                            waits=[("T", p.cnt["T"])] + xg(), inc="V")
                            p.bank_guard[bk] = [("V", rv)]
                            if ffs == nffs - 1 and half == 1:
                                xTk_guard[ct] = [("V", rv)]
                    wdone(cur[0])
                    if nxt is not None:
                        wv_, dv_ = nxt
                        cur[0] = 1 - cur[0]
            xT_guard[0] = [("V", p.cnt["V"])]
            sg2 = [("V", p.cnt["V"]), ("S", p.cnt["S"])]
            scrhalf_guard[0] = scrhalf_guard[0] + sg2
            scrhalf_guard[1] = scrhalf_guard[1] + sg2

        gelu_scr_guard = [[], [], [], []]

        def gelu_evac(ps, dst, b1c, bk):
            gi = bk
            xs = scr[:, gi * 1024: gi * 1024 + 512]
            t2 = scr[:, gi * 1024 + 512: gi * 1024 + 1024]
            sv = p.emit("s", lambda s, o=xs, i=ps[:, :], bc=b1c:
                        s.activation(o, i, AF.Identity, bias=bc),
                        waits=[("T", p.cnt["T"])] + gelu_scr_guard[gi] + scrhalf_guard[gi // 2], inc="S")
            sv2 = p.emit("s", lambda s, o=t2, i=xs: s.activation(o, i, AF.Square),
                         waits=[("S", sv)], inc="S")
            vv = p.emit("v", lambda v, o=t2: v.tensor_scalar(o, o, C3G, C1G, ALU.mult, op1=ALU.add),
                        waits=[("S", sv2)], inc="V")
            vv = p.emit("v", lambda v, o=t2, i=xs: v.tensor_tensor(o, o, i, ALU.mult),
                        waits=[("V", vv)], inc="V")
            sv3 = p.emit("s", lambda s, o=t2: s.activation(o, o, AF.Sigmoid),
                         waits=[("V", vv)], inc="S")
            vv = p.emit("v", lambda v, o=dst, a=xs, b=t2: v.tensor_tensor(o, a, b, ALU.mult),
                        waits=[("S", sv3)], inc="V")
            gelu_scr_guard[gi] = [("V", vv)]
            p.bank_guard[bk] = [("S", sv)]

        def moe():
            ccq = ccn[0]
            ccn[0] += 1
            moescr_guard = [[], []]
            xnw = layernorm(xnT, xT, xg(), in_waits_k=xkw())
            ln_T = p.cnt["T"]
            scratch = [big[:, 10496 + i * 1024: 11520 + i * 1024] for i in range(5)] \
                + [kT[:, 0:1024], kT[:, 1024:2048],
                   vtk[:, 0:1024], vtk[:, 1024:2048],
                   gtk[:, 0:1024], gtk[:, 1024:2048]]
            NPRE = len(scratch)
            we, de = wload(1, g["L0_ffn_expert_w"], 0, KT, 0, HD)
            res = {"wv": None}
            evS = {}

            def expert_mms():
                weL, deL = we, de
                cur = 1
                for cs in range(8):
                    for co in range(2):
                        ct = cs * 2 + co
                        for half in range(2):
                            bk = (co * 2 + half) % 4
                            for k in range(KT):
                                mm(psum[bk][:, :],
                                   weL[:, k * HD + co * 128: k * HD + co * 128 + 128],
                                   xnT[:, k * TOK + half * 512: k * TOK + (half + 1) * 512],
                                   start=(k == 0), stop=(k == KT - 1),
                                   waits=([deL] + list(xnw[half]) + p.bank_guard[bk])
                                         if k == 0 else [])
                                yield
                            if ct < NPRE:
                                xw = [("T", ln_T)] if ct >= 5 else []
                                ev = p.emit("s", lambda s,
                                            o=scratch[ct][:, half * 512:(half + 1) * 512],
                                            i=psum[bk][:, :],
                                            bc=bcol[:, 128 + 2 * KT + ct:129 + 2 * KT + ct]:
                                            s.activation(o, i, AF.Identity, bias=bc),
                                            waits=[("T", p.cnt["T"])] + xw, inc="S")
                                evS[ct] = ev
                                p.bank_guard[bk] = [("S", ev)]
                            else:
                                assert res["wv"] is not None
                                sl = slice(ct * TOK + half * 512, ct * TOK + (half + 1) * 512)
                                ms = moescr[:, half * 512:(half + 1) * 512]
                                ev = p.emit("s", lambda s, o=ms, i=psum[bk][:, :],
                                            bc=bcol[:, 128 + 2 * KT + ct:129 + 2 * KT + ct]:
                                            s.activation(o, i, AF.Identity, bias=bc),
                                            waits=[("T", p.cnt["T"])] + moescr_guard[half],
                                            inc="S")
                                ev2 = p.emit("v", lambda v, o=ms,
                                             wb=wBsb[:, half * 512:(half + 1) * 512]:
                                             v.tensor_tensor(o, o, wb, ALU.mult),
                                             waits=[("S", ev), ("V", res["wv"])], inc="V")
                                ev2 = p.emit("v", lambda v, o=xT[:, sl], i=ms:
                                             v.tensor_tensor(o, o, i, ALU.add),
                                             waits=[("V", ev2)] + xg(), inc="V")
                                moescr_guard[half] = [("V", ev2)]
                                p.bank_guard[bk] = [("S", ev), ("V", ev2)]
                                if half == 1:
                                    xTk_guard[ct] = [("V", ev2)]
                    nxt = (wload(1 - cur, g["L0_ffn_expert_w"], 0, KT, (cs + 1) * HD, HD)
                           if cs < 7 else None)
                    wdone(cur)
                    if nxt is not None:
                        weL, deL = nxt
                        cur = 1 - cur

            gen = expert_mms()

            def pump(n):
                for _ in range(n):
                    if next(gen, "END") == "END":
                        break

            rw = wbuf[:, 0:256]
            drw = None
            src = g["L0_ffn_router_w"].rearrange("(k p) n -> p k n", p=128)
            drw = dma("p", rw.rearrange("p (k n) -> p k n", n=16), src, "DW0",
                      waits=wslot_guard[0])
            drw = ("DW0", drw)
            smw = None
            for tt in range(NTT):
                bk = 4 + tt % 2
                for k in range(KT):
                    mm(psum[bk][:, 0:16], xnT[:, k * TOK + tt * 128: k * TOK + tt * 128 + 128],
                       rw[:, k * 16:(k + 1) * 16],
                       start=(k == 0), stop=False,
                       waits=([drw] + list(xnw[tt // 4]) + p.bank_guard[bk]) if k == 0 else [])
                mm(psum[bk][:, 0:16], cst[0:1, 96:224], rbrow[:],
                   start=False, stop=True, waits=CONST_DX, inc="T")
                tl = p.cnt["T"]
                blk = Smine[:, tt * 16:(tt + 1) * 16]
                nm = rows[:, 0:1]
                vv = p.emit("v", lambda v, o=nm, i=psum[bk][:, 0:16]:
                            v.reduce_max(o, i, axis=AX.X, negate=True),
                            waits=[("T", tl), ("V", p.cnt["V"])], inc="V")
                sv = p.emit("s", lambda s, o=blk, i=psum[bk][:, 0:16], b=nm,
                            ac=rows[:, 1:2]:
                            s.activation(o, i, AF.Exp, bias=b, accum_out=ac),
                            waits=[("V", vv), ("S", p.cnt["S"])], inc="S")
                vv = p.emit("v", lambda v, o=rows[:, 2:3], i=rows[:, 1:2]:
                            v.reciprocal(o, i), waits=[("S", sv)], inc="V")
                smw = p.emit("v", lambda v, o=blk, sc=rows[:, 2:3]:
                             v.tensor_scalar(o, o, sc, None, ALU.mult),
                             waits=[("V", vv)], inc="V")
                p.bank_guard[bk] = [("S", sv)]
                bkt = 2 + bk
                pvt = psum[bkt][0:16, 0:128]
                tvt = p.emit("t", lambda t, o=pvt, i=blk: t.transpose(o, i, cidf[:]),
                             waits=[("V", smw)] + p.bank_guard[bkt], inc="T")
                svt = p.emit("v", lambda v, o=SmT[:, tt * 128:(tt + 1) * 128], i=pvt:
                             v.tensor_copy(o, i), waits=[("T", tvt)], inc="V")
                p.bank_guard[bkt] = [("V", svt)]
                pump(4)
            wdone(0)
            stw = p.cnt["V"]
            dsm = dma("y", g["cc_s_in"], SmT, "DCU", waits=[("V", stw)] + ccbuf_guard.get("s", []))
            ccs = "CC%d" % ccq
            p.emit("p", lambda gp: gp.collective_compute(
                "AllGather", ALU.bypass, replica_groups=[list(range(NCORE))],
                ins=[g["cc_s_in"]], outs=[g["cc_s_out"]]),
                waits=[("DCU", dsm)], inc=ccs)
            d2 = dma("y", Sall, g["cc_s_out"], "DCG", waits=[(ccs, 1)])
            ccbuf_guard["s"] = [(ccs, 1), ("DCG", d2)]
            taucol = rows[:, 3:4]
            csum = rows[:, 4:5]
            locol = rows[:, 5:6]
            selcol = rows[:, 6:7]
            midcol = rows[:, 7:8]
            last = {"v": p.cnt["V"], "s": p.cnt["S"]}

            def _stepA(it):
                a = p.emit("v", lambda v, c=0.5 ** (it + 1): v.tensor_scalar(
                    midcol, locol, c, None, ALU.add),
                    waits=[("V", last["v"])], inc="V")
                a = p.emit("v", lambda v: v.tensor_scalar(cmpb[:], Sall, midcol, 0.0,
                                                          ALU.is_ge, op1=ALU.add,
                                                          accum_out=csum),
                           waits=[("V", a)] + ([("DCG", d2)] if it == 0 else []), inc="V")
                last["v"] = a

            def _stepB(it):
                tb = mm(psum[4][:, 0:1], cindf[:], csum, start=True, stop=True,
                        waits=[("V", last["v"])] + p.bank_guard[4]
                              + (CONST_DX if it == 0 else []),
                        inc="T")
                a = p.emit("v", lambda v: v.tensor_scalar(selcol, psum[4][:, 0:1],
                                                          float(KTOP), None, ALU.is_gt),
                           waits=[("T", tb), ("V", last["v"])], inc="V")
                p.bank_guard[4] = [("V", a)]
                a = p.emit("v", lambda v, c=0.5 ** (it + 1): v.tensor_scalar(
                    selcol, selcol, c, None, ALU.mult),
                    waits=[("V", a)], inc="V")
                a = p.emit("v", lambda v: v.tensor_tensor(locol, locol, selcol, ALU.add),
                           waits=[("V", a)], inc="V")
                last["v"] = a

            def _stepF():
                a = p.emit("v", lambda v, c=0.5 ** BISECT_ITERS: v.tensor_scalar(
                    taucol, locol, c, None, ALU.add),
                    waits=[("V", last["v"])], inc="V")
                a = p.emit("v", lambda v: v.tensor_scalar(cmpb[:], Sall, taucol, None,
                                                          ALU.is_ge),
                           waits=[("V", a)], inc="V")
                a = p.emit("v", lambda v: v.tensor_tensor(cmpb[:], cmpb[:], Sall, ALU.mult),
                           waits=[("V", a)], inc="V")
                last["v"] = a

            def _stepW(half):
                wmm = mm(psum[4][0:1, :], cst[:, 81:82], cmpb[:, half * 512:(half + 1) * 512],
                         start=True, stop=True,
                         waits=[("V", last["v"])] + p.bank_guard[4], inc="T")
                a = p.emit("v", lambda v, hh=half: v.tensor_copy(
                    rows[0:1, hh * 512:(hh + 1) * 512], psum[4][0:1, :]),
                    waits=[("T", wmm)], inc="V")
                p.bank_guard[4] = [("V", a)]
                wm2 = mm(psum[5 + half][:, :], cst[0:1, 96:224],
                         rows[0:1, half * 512:(half + 1) * 512],
                         start=True, stop=True, waits=[("V", a)] + p.bank_guard[5 + half],
                         inc="T")
                a = p.emit("v", lambda v, hh=half: v.tensor_copy(
                    wBsb[:, hh * 512:(hh + 1) * 512], psum[5 + hh][:, :]),
                    waits=[("T", wm2)], inc="V")
                p.bank_guard[5 + half] = [("V", a)]
                last["v"] = a
                if half == 1:
                    res["wv"] = a

            def _memsets():
                a = p.emit("v", lambda v: v.memset(locol, 0.0),
                           waits=[("V", last["v"])], inc="V")
                last["v"] = a
                _stepA(0)

            steps = [(4, _memsets), (110, lambda: _stepB(0))]
            for it in range(1, BISECT_ITERS):
                steps.append((3, lambda it=it: _stepA(it)))
                steps.append((3, lambda it=it: _stepB(it)))
            steps.append((3, _stepF))
            steps.append((3, lambda: _stepW(0)))
            steps.append((3, lambda: _stepW(1)))

            for gap, fn in steps:
                pump(gap)
                fn()
            pump(1 << 30)
            for ti in range(NPRE):
                for half in range(2):
                    tmp = cmpb[:, half * 512:(half + 1) * 512]
                    sl = slice(ti * TOK + half * 512, ti * TOK + (half + 1) * 512)
                    c1 = p.emit("v", lambda v, o=tmp,
                                a=scratch[ti][:, half * 512:(half + 1) * 512],
                                wb=wBsb[:, half * 512:(half + 1) * 512]:
                                v.tensor_tensor(o, a, wb, ALU.mult),
                                waits=[("S", evS[ti]), ("V", res["wv"])], inc="V")
                    c2 = p.emit("v", lambda v, o=xT[:, sl], i=tmp:
                                v.tensor_tensor(o, o, i, ALU.add),
                                waits=[("V", c1)] + xg(), inc="V")
                    if half == 1:
                        xTk_guard[ti] = [("V", c2)]
            xT_guard[0] = [("V", p.cnt["V"])]

        def attn_b():
            ccq = ccn[0]
            ccn[0] += 1
            wsl_guard = [[], []]
            wTb_guard = [0, 0]
            totals_done = []
            xnw = layernorm(xnT, xT, xg(), in_waits_k=xkw())
            wslA = [[[], []], [[], []]]

            def stg(wi, par):
                return scr[:, wi * 2048 + par * 1024: wi * 2048 + par * 1024 + 1024].bitcast(BF16)

            for et in range(KT):
                par = et % 2
                for wi, wnm in ((0, "L0_attn_b_Wk"), (1, "L0_attn_b_Wv")):
                    if True:
                        bufb = stg(wi, par)
                        sd = "DW%d" % wi
                        d = dma("p", bufb, g[wnm][et * 128:(et + 1) * 128, :], sd,
                                waits=wslA[wi][par] + (scrhalf_guard[0] + scrhalf_guard[1] if et < 2 else []))
                        for k in range(KT):
                            tr128(wTb[:, wi * 2048 + k * 128: wi * 2048 + (k + 1) * 128],
                                  bufb[:, k * 128:(k + 1) * 128], 2 + (k % 2), True,
                                  t_waits=[(sd, d)],
                                  v_waits=[("T", wTb_guard[wi])] if wTb_guard[wi] else [])
                        wslA[wi][par] = [("T", p.cnt["T"])]
                    wtr = p.cnt["V"]
                    wtr_s = p.cnt["S"]
                    for half in range(2):
                        bk = 4 + wi * 2 + half
                        for k in range(KT):
                            mm(psum[bk][:, :],
                               wTb[:, wi * 2048 + k * 128: wi * 2048 + (k + 1) * 128],
                               xnT[:, k * TOK + half * 512: k * TOK + (half + 1) * 512],
                               start=(k == 0), stop=(k == KT - 1),
                               waits=([("V", wtr)] + list(xnw[half]) + p.bank_guard[bk]) if k == 0 else [])
                        if wi == 0:
                            ev = vcopy(kT[:, half * 512:(half + 1) * 512], psum[bk][:, :],
                                       waits=[("T", p.cnt["T"])], eng="s")
                            p.bank_guard[bk] = [("S", ev)]
                        else:
                            ev = vcopy(vtk[:, half * 512:(half + 1) * 512], psum[bk][:, :],
                                       waits=[("T", p.cnt["T"]), ("V", p.cnt["V"])], eng="s")
                            p.bank_guard[bk] = [("S", ev)]
                            p.emit("v", lambda v, o=abkv[:, half * 512:(half + 1) * 512],
                                   vv_=vtk[:, half * 512:(half + 1) * 512],
                                   kk=kT[:, half * 512:(half + 1) * 512]:
                                   v.tensor_tensor(o, vv_, kk, ALU.mult),
                                   waits=[("S", ev), ("V", p.cnt["V"])], inc="V")
                wTb_guard[0] = p.cnt["T"]
                wTb_guard[1] = p.cnt["T"]
                kvw = p.cnt["V"]
                tv = p.emit("v", lambda v, o=rows[:, et:et + 1], i=abkv[:, 0:1024]:
                            v.tensor_reduce(o, i, AX.X, ALU.add),
                            waits=[("V", kvw)], inc="V")
                sc = p.emit("v", lambda v, o=kvh[:, et * TOK:(et + 1) * TOK], i=abkv[:, 0:1024]:
                            v.tensor_tensor_scan(o, i, i, 0.0, ALU.add, ALU.bypass),
                            waits=[("V", kvw)], inc="V")
                totals_done = [("V", p.cnt["V"])]
            dcc = dma("y", g["cc_cs_in"], rows[:, 0:KT], "DCU", waits=totals_done)
            ccs = "CC%d" % ccq
            dq0 = [None, None]
            for et0 in range(2):
                dq0[et0] = dma("p", stg(0, et0), g["L0_attn_b_Wq"][et0 * 128:(et0 + 1) * 128, :],
                               "DW0", waits=wslA[0][et0])
            p.emit("p", lambda gp: gp.collective_compute(
                "AllGather", ALU.bypass, replica_groups=[list(range(NCORE))],
                ins=[g["cc_cs_in"]], outs=[g["cc_cs_out"]]),
                waits=[("DCU", dcc)], inc=ccs)
            d2 = dma("y", rows[:, 32:160].rearrange("p (r n) -> p r n", r=8),
                     g["cc_cs_out"].rearrange("(r p) n -> p r n", p=128), "DCG",
                     waits=[(ccs, 1)])
            cv = p.emit("v", lambda v: v.memset(rows[:, 16:32], 0.0),
                        waits=[("DCG", d2)], inc="V")
            for r in range(NCORE):
                cv = p.emit("v", lambda v, rr=r: v.scalar_tensor_tensor(
                    rows[:, 16:32], rows[:, 32 + rr * 16:48 + rr * 16],
                    cst[:, 9 + rr:10 + rr],
                    rows[:, 16:32], ALU.mult, ALU.add),
                    waits=[("V", cv)], inc="V")
            for et in range(KT):
                par = et % 2
                bufq = stg(0, par)
                d = (dq0[et] if et < 2 else
                     dma("p", bufq, g["L0_attn_b_Wq"][et * 128:(et + 1) * 128, :], "DW0",
                         waits=wslA[0][par]))
                for k in range(KT):
                    tr128(wTb[:, k * 128:(k + 1) * 128], bufq[:, k * 128:(k + 1) * 128],
                          2 + (k % 2), True, t_waits=[("DW0", d)])
                wslA[0][par] = [("T", p.cnt["T"])]
                wtr = p.cnt["V"]
                for half in range(2):
                    bk = 4 + half
                    for k in range(KT):
                        mm(psum[bk][:, :], wTb[:, k * 128:(k + 1) * 128],
                           xnT[:, k * TOK + half * 512: k * TOK + (half + 1) * 512],
                           start=(k == 0), stop=(k == KT - 1),
                           waits=([("V", wtr)] + p.bank_guard[bk]) if k == 0 else [])
                    ev = vcopy(kT[:, half * 512:(half + 1) * 512], psum[bk][:, :],
                               waits=[("T", p.cnt["T"])], eng="s")
                    p.bank_guard[bk] = [("S", ev)]
                qw = p.cnt["S"]
                tv = p.emit("v", lambda v, o=abkv[:, 0:1024], i=kvh[:, et * TOK:(et + 1) * TOK],
                            cc=rows[:, 16 + et:17 + et]:
                            v.tensor_scalar(o, i, cc, DENOM_B, ALU.add, op1=ALU.mult),
                            waits=[("V", cv), ("V", p.cnt["V"]), ("S", qw)], inc="V")
                tv = p.emit("v", lambda v, o=kvh[:, et * TOK:(et + 1) * TOK],
                            a=abkv[:, 0:1024], b=kT[:, 0:1024]:
                            v.tensor_tensor(o, a, b, ALU.mult),
                            waits=[("V", tv)], inc="V")
            vhat_done = [("V", p.cnt["V"])]
            for ft in range(KT):
                par = ft % 2
                bufo = stg(0, par)
                d = dma("p", bufo, g["L0_attn_b_Wo"][ft * 128:(ft + 1) * 128, :], "DW0",
                        waits=wslA[0][par])
                for k in range(KT):
                    tr128(wTb[:, k * 128:(k + 1) * 128], bufo[:, k * 128:(k + 1) * 128],
                          2 + (k % 2), True, t_waits=[("DW0", d)])
                wslA[0][par] = [("T", p.cnt["T"])]
                wtr = p.cnt["V"]
                for half in range(2):
                    bk = 4 + half
                    for k in range(KT):
                        mm(psum[bk][:, :], wTb[:, k * 128:(k + 1) * 128],
                           kvh[:, k * TOK + half * 512: k * TOK + (half + 1) * 512],
                           start=(k == 0), stop=(k == KT - 1),
                           waits=([("V", wtr)] + vhat_done + p.bank_guard[bk]) if k == 0 else [])
                    sl = slice(ft * TOK + half * 512, ft * TOK + (half + 1) * 512)
                    rv = p.emit("v", lambda v, o=xT[:, sl], ps=psum[bk][:, :]:
                                v.tensor_tensor(o, o, ps, ALU.add),
                                waits=[("T", p.cnt["T"])] + xg(), inc="V")
                    p.bank_guard[bk] = [("V", rv)]
                    if half == 1:
                        xTk_guard[ft] = [("V", rv)]
            xT_guard[0] = [("V", p.cnt["V"])]

        stage_in(g["x_in"])

        if upto == 0:
            stage_out(g["y_out"], waits=xg())
        elif tap == "ln_affine":
            w = layernorm(xnT, xT, xg(), gamma_col=lncol[:, 0:KT], beta_col=lncol[:, KT:2 * KT])["all"]
            for k in range(KT):
                vcopy(xT[:, k * TOK:(k + 1) * TOK], xnT[:, k * TOK:(k + 1) * TOK], waits=w)
            stage_out(g["y_out"], waits=[("V", p.cnt["V"])])
        elif tap == "ln_plain":
            w = layernorm(xnT, xT, xg())["all"]
            for k in range(KT):
                vcopy(xT[:, k * TOK:(k + 1) * TOK], xnT[:, k * TOK:(k + 1) * TOK], waits=w)
            stage_out(g["y_out"], waits=[("V", p.cnt["V"])])
        else:
            for c in range(4):
                dma("y", g["spill_x"][:, c * 4096:(c + 1) * 4096],
                    xT[:, c * 4096:(c + 1) * 4096], "DX0",
                    waits=xg() if c == 0 else ())
            spill_done = [("DX0", p.cnt["DX0"])]
            lnw = layernorm(xnT, xT, xg(), dst_f32=xT, dst_waits=spill_done, in_waits_k=xkw())
            xT_guard[0] = lnw["all"]
            if tap == "msr0":
                xnw2 = layernorm(xnT, xT, xg(), gamma_col=lncol[:, 0:KT],
                                 beta_col=lncol[:, KT:2 * KT])
                wq, dq = wload(0, g["L0_attn_a_Wq"], 0, KT, 0, HD, layer=0)
                pend = {"q": (wq, dq, 0)}
                ztg = [[]]
                if NH < HEADS:
                    p.emit("v", lambda v: v.memset(big[:], 0.0), inc="V")
                fin2 = [None]
                for h in range(NH):
                    retention_head(0, h, xnw2, pend, ztg, fin2)
                fin2[0]()
                zv0 = p.emit("v", lambda v: v.memset(xT[:], 0.0),
                             waits=[("V", p.cnt["V"]), ("S", p.cnt["S"]),
                                    ("T", p.cnt["T"])], inc="V")
                xT_guard[0] = [("V", zv0)]
                wo_pass(0, ztg)
                stage_out(g["y_out"], waits=xg())
            else:
                for l in range(2):
                    retention_layer(l)
                for c in range(8):
                    hbuf = scr[:, (c % 2) * 2048:(c % 2) * 2048 + 2048]
                    dld = dma("y", hbuf, g["spill_x"][:, c * 2048:(c + 1) * 2048],
                              "DX1", waits=scrhalf_guard[c % 2] + (spill_done if c < 2 else []))
                    av = p.emit("v", lambda v, o=xT[:, c * 2048:(c + 1) * 2048], i=hbuf:
                                v.tensor_tensor(o, o, i, ALU.add),
                                waits=[("DX1", dld)] + xg(), inc="V")
                    scrhalf_guard[c % 2] = [("V", av)]
                    xTk_guard[2 * c] = [("V", av)]
                    xTk_guard[2 * c + 1] = [("V", av)]
                xT_guard[0] = [("V", p.cnt["V"])]
                if upto == 1:
                    stage_out(g["y_out"], waits=xg())
                else:
                    moe()
                    if upto == 2:
                        stage_out(g["y_out"], waits=xg())
                    else:
                        attn_b()
                        if upto == 3:
                            stage_out(g["y_out"], waits=xg())
                        else:
                            moe()
                            stage_out(g["y_out"], waits=xg())

        p.finalize(E(nc.Block()))
    return nc


_CACHE = {}


def kernel(**inputs):
    upto = int(inputs.pop("_upto", 99))
    tap = inputs.pop("_tap", None)
    key = (upto, tap)
    if key not in _CACHE:
        _CACHE[key] = build_program(upto, tap)
    nc = _CACHE[key]
    x = np.ascontiguousarray(np.asarray(inputs["L0_attn_a_x"], dtype=np.float32).reshape(SEQ, H))
    cols = _host_cols(inputs)
    in_maps = []
    for c in range(NCORE):
        m = {"x_in": x[c * TOK:(c + 1) * TOK]}
        for k, v in inputs.items():
            if k == "L0_attn_a_x":
                continue
            m[k] = np.ascontiguousarray(np.asarray(v, dtype=np.float32))
        m.update(_host_consts(c))
        m.update(cols)
        in_maps.append(m)
    res = run_bass_kernel_spmd(nc, in_maps, core_ids=list(range(NCORE)))
    out = np.concatenate([r["y_out"] for r in res.results], axis=0)
    return out.reshape(1, SEQ, H)


# version 23
# speedup vs baseline: 1.0043x; 1.0043x over previous
import math
from contextlib import ExitStack

import numpy as np
import ml_dtypes
import concourse.bass as bass
import concourse.mybir as mybir
from concourse.bass_utils import run_bass_kernel_spmd

F32 = mybir.dt.float32
BF16 = mybir.dt.bfloat16
AF = mybir.ActivationFunctionType
ALU = mybir.AluOpType
AX = mybir.AxisListType

H = 2048
SEQ = 8192
NCORE = 8
TOK = SEQ // NCORE
KT = H // 128
NTT = TOK // 128
HEADS = 8
HD = 256
FF = 8192
NE = 16
KTOP = 512
EPS = 1e-5
C1G = 1.5957691216057308
C3G = 0.07135481627338621
DENOM_B = math.sqrt(2048.0) / 8192.0
BISECT_ITERS = 20

GAMMAS = (1.0 - np.exp(np.linspace(np.log(1.0 / 32), np.log(1.0 / 512), HEADS))).astype(np.float32)
NH = HEADS


class P:

    ENGS = ("t", "v", "s", "p", "y")

    def __init__(self, nc, ctx):
        self.nc = nc
        self.ctx = ctx
        self.ops = {e: [] for e in self.ENGS}
        self.sems = {}
        self.cnt = {}
        self.lastw = {e: {} for e in self.ENGS}
        self.bank_guard = [[] for _ in range(8)]

    def sem(self, name):
        self.sems[name] = self.ctx.enter_context(self.nc.semaphore(name))
        self.cnt[name] = 0
        return name

    def emit(self, e, fn, waits=(), inc=None, inc_by=1):
        w2 = []
        for sn, v in waits:
            if not v:
                continue
            if self.lastw[e].get(sn, 0) < v:
                self.lastw[e][sn] = v
                w2.append((sn, v))
        val = None
        if inc is not None:
            self.cnt[inc] += inc_by
            val = self.cnt[inc]
        sems = self.sems

        def gfun(eng, w2=tuple(w2), fn=fn, inc=inc, inc_by=inc_by):
            for sn, v in w2:
                eng.wait_ge(sems[sn], v)
            r = fn(eng)
            if inc is not None:
                r.then_inc(sems[inc], inc_by)

        self.ops[e].append(gfun)
        return val

    def finalize(self, block):
        for e, sec in (("y", block.sync), ("p", block.gpsimd), ("t", block.tensor),
                       ("v", block.vector), ("s", block.scalar)):
            ops = self.ops[e]

            def runner(eng, ops=ops):
                for f in ops:
                    f(eng)

            sec(runner)


def _host_consts(core):
    c = {}
    ident = np.eye(128, dtype=np.float32)
    c["c_idf"] = ident
    c["c_idb"] = ident.astype(ml_dtypes.bfloat16)
    a = np.arange(128)
    diff = a[None, :] - a[:, None]
    msk = (diff >= 0).astype(np.float32)
    c["c_mask01"] = msk.astype(ml_dtypes.bfloat16)
    ng = np.stack([g ** (-a.astype(np.float32)) for g in GAMMAS], axis=1)
    c["c_negj"] = ng.astype(np.float32)
    qs = np.stack([np.broadcast_to(g ** a.astype(np.float32), (128, 128)) for g in GAMMAS])
    c["c_qscale"] = np.ascontiguousarray(qs).astype(ml_dtypes.bfloat16)
    vs = np.stack([g ** (128.0 - a.astype(np.float32)) for g in GAMMAS], axis=1)
    c["c_vscale"] = vs.astype(np.float32)
    gd = np.zeros((128, HEADS * 8), dtype=np.float32)
    for h in range(HEADS):
        for r in range(8):
            if r < core:
                gd[:, h * 8 + r] = np.float32(GAMMAS[h]) ** np.float32(1024 * (core - r - 1))
    c["c_gdecay"] = gd
    cm = np.zeros((128, 8), dtype=np.float32)
    cm[:, :core] = 1.0
    c["c_cmask"] = cm
    ind_a = np.zeros((128, 16), dtype=np.float32)
    for p2 in range(128):
        ind_a[p2, p2 % 16] = 1.0
    c["c_inda"] = ind_a
    c["c_indb"] = np.ascontiguousarray(ind_a.T)
    a2 = np.arange(128)
    c["c_indf"] = (a2[:, None] % 16 == a2[None, :] % 16).astype(np.float32)
    sel = np.zeros((128, 1), dtype=np.float32)
    sel[16 * core:16 * core + 16, 0] = 1.0
    c["c_selw"] = sel
    return c


def _col_layout(v):
    v = np.asarray(v, dtype=np.float32).reshape(-1, 128)
    return np.ascontiguousarray(v.T)


def _host_cols(inputs):
    c = {}
    c["c_lncol"] = np.concatenate([_col_layout(inputs["L0_attn_a_ln_gamma"]),
                                   _col_layout(inputs["L0_attn_a_ln_beta"])], axis=1)
    b1 = np.asarray(inputs["L0_attn_a_b1"], dtype=np.float32)
    b2 = np.asarray(inputs["L0_attn_a_b2"], dtype=np.float32)
    be = np.asarray(inputs["L0_ffn_expert_b"], dtype=np.float32)
    c["c_bcol"] = np.concatenate([_col_layout(b1[0]), _col_layout(b1[1]),
                                  _col_layout(b2[0]), _col_layout(b2[1]),
                                  _col_layout(be)], axis=1)
    return c


def _const_specs():
    cc = _host_consts(0)
    specs = {k: (list(v.shape), BF16 if v.dtype == ml_dtypes.bfloat16 else F32)
             for k, v in cc.items()}
    specs["c_lncol"] = ([128, 2 * KT], F32)
    specs["c_bcol"] = ([128, 2 * 64 + 2 * KT + KT], F32)
    return specs


def build_program(upto=99, tap=None):
    nc = bass.Bass("TRN2", target_bir_lowering=False, debug=False)
    g = {}

    def dram_in(name, shape, dt=F32):
        g[name] = nc.dram_tensor(name, list(shape), dt, kind="ExternalInput").ap()

    dram_in("x_in", (TOK, H))
    for nm, shp in (
        ("L0_attn_a_Wq", (2, H, H)), ("L0_attn_a_Wk", (2, H, H)),
        ("L0_attn_a_Wv", (2, H, H)), ("L0_attn_a_Wg", (2, H, H)),
        ("L0_attn_a_Wo", (2, H, H)),
        ("L0_attn_a_gn_gamma", (2, H)), ("L0_attn_a_gn_beta", (2, H)),
        ("L0_attn_a_W1", (2, H, FF)), ("L0_attn_a_b1", (2, FF)),
        ("L0_attn_a_W2", (2, FF, H)), ("L0_attn_a_b2", (2, H)),
        ("L0_attn_a_ln_gamma", (H,)), ("L0_attn_a_ln_beta", (H,)),
        ("L0_attn_b_Wq", (H, H)), ("L0_attn_b_Wk", (H, H)),
        ("L0_attn_b_Wv", (H, H)), ("L0_attn_b_Wo", (H, H)),
        ("L0_ffn_router_w", (H, NE)), ("L0_ffn_router_b", (NE,)),
        ("L0_ffn_expert_w", (H, H)), ("L0_ffn_expert_b", (H,)),
    ):
        dram_in(nm, shp)
    for nm, (shp, dt) in _const_specs().items():
        dram_in(nm, shp, dt)

    g["y_out"] = nc.dram_tensor("y_out", [TOK, H], F32, kind="ExternalOutput").ap()
    g["spill_x"] = nc.dram_tensor("spill_x", [128, KT * TOK], F32).ap()
    g["cc_st_in"] = nc.dram_tensor("cc_st_in", [128, 512], BF16).ap()
    g["cc_st_out"] = nc.dram_tensor("cc_st_out", [128 * NCORE, 512], BF16).ap()
    g["cc_s_in"] = nc.dram_tensor("cc_s_in", [16, TOK], F32).ap()
    g["cc_s_out"] = nc.dram_tensor("cc_s_out", [16 * NCORE, TOK], F32).ap()
    g["cc_cs_in"] = nc.dram_tensor("cc_cs_in", [128, KT], F32).ap()
    g["cc_cs_out"] = nc.dram_tensor("cc_cs_out", [128 * NCORE, KT], F32).ap()

    ctx = ExitStack()
    E = ctx.enter_context
    with ctx:
        xT = E(nc.sbuf_tensor("xT", [128, KT * TOK], F32))
        xnT = E(nc.sbuf_tensor("xnT", [128, KT * TOK], BF16))
        big = E(nc.sbuf_tensor("big", [128, 16384], BF16))
        aux = E(nc.sbuf_tensor("aux", [128, 11264], BF16))
        wbuf = E(nc.sbuf_tensor("wbuf", [128, 2 * 4096], BF16))
        kT = E(nc.sbuf_tensor("kT", [128, 2 * TOK], BF16))
        vtk = E(nc.sbuf_tensor("vtk", [128, NTT * HD], BF16))
        gtk = E(nc.sbuf_tensor("gtk", [128, NTT * HD], BF16))
        scr = E(nc.sbuf_tensor("scr", [128, 4096], F32))
        rows = E(nc.sbuf_tensor("rows", [128, 1024], F32))
        cst = E(nc.sbuf_tensor("cst", [128, 224], F32))
        cstb = E(nc.sbuf_tensor("cstb", [128, 272], BF16))
        cidf = E(nc.sbuf_tensor("cidf", [128, 128], F32))
        cmask01 = E(nc.sbuf_tensor("cmask01", [128, 128], BF16))
        cqs = E(nc.sbuf_tensor("cqs", [128, 8 * 128], BF16))
        cinda = E(nc.sbuf_tensor("cinda", [128, 16], F32))
        cindb = E(nc.sbuf_tensor("cindb", [16, 128], F32))
        cindf = E(nc.sbuf_tensor("cindf", [128, 128], F32))
        lncol = E(nc.sbuf_tensor("lncol", [128, 2 * KT], F32))
        bcol = E(nc.sbuf_tensor("bcol", [128, 2 * 64 + 3 * KT], F32))
        gnga = E(nc.sbuf_tensor("gnga", [1, HD], BF16))
        gnbb = E(nc.sbuf_tensor("gnbb", [1, HD], BF16))
        rbrow = E(nc.sbuf_tensor("rbrow", [1, NE], F32))
        bis = E(nc.sbuf_tensor("bis", [16, 8], F32))

        psum = [E(nc.psum_tensor("pb%d" % i, [128, 512], F32)) for i in range(8)]

        zT = big
        Rpfx = aux[:, 0:4096]
        Rball = aux[:, 4096:6144]
        yhv = aux[:, 6144:8192]
        Rst = aux[:, 8192:10240].bitcast(F32)
        Rgcb = aux[:, 10240:10752]
        chz = aux[:, 10752:11264]
        h2 = big
        Sall = big[:, 0:2048].bitcast(F32)
        Smine = big[:, 2048:2304].bitcast(F32)
        SmT = big[0:16, 2304:4352].bitcast(F32)
        cmpb = big[:, 4352:6400].bitcast(F32)
        wBsb = big[:, 6400:8448].bitcast(F32)
        moescr = big[:, 8448:10496].bitcast(F32)
        kvh = big
        wTb = aux[:, 0:8192]
        abkv = aux[:, 8192:10240].bitcast(F32)

        p = P(nc, ctx)
        for snm in ("T", "V", "S", "DW0", "DW1", "DX0", "DX1", "DXC", "DXG", "DCU", "DCG"):
            p.sem(snm)
        NCC = 2 * HEADS + 3
        for i in range(NCC):
            p.sem("CC%d" % i)
        ccn = [0]
        ccbuf_guard = {}

        def mm(out, lhsT, rhs, start, stop, waits=(), inc=None):
            if stop:
                inc = "T"
            return p.emit("t", lambda t, o=out, l=lhsT, r=rhs, a=start, b=stop:
                          t.matmul(o, l, r, start=a, stop=b), waits=waits, inc=inc)

        def dma(queue, out, in_, sem, waits=()):
            return p.emit(queue, lambda q, o=out, i=in_: q.dma_start(out=o, in_=i),
                          waits=waits, inc=sem, inc_by=16)

        def vcopy(dst, src, waits=(), eng="v"):
            sem = "V" if eng == "v" else "S"
            return p.emit(eng, lambda e, o=dst, i=src:
                          (e.tensor_copy(o, i) if eng == "v" else e.copy(o, i)),
                          waits=waits, inc=sem)

        def tr128(dst, src, bank, ident_b, t_waits=(), v_waits=(), eng="v"):
            if ident_b:
                pv = psum[bank][:, 0:64].bitcast(BF16)
                idt = cstb[:, 0:128]
            else:
                pv = psum[bank][:, 0:128]
                idt = cidf[:]
            tv = p.emit("t", lambda t, o=pv, i=src: t.transpose(o, i, idt),
                        waits=list(t_waits) + p.bank_guard[bank], inc="T")
            sem = "V" if eng == "v" else "S"
            vv = p.emit(eng, lambda e, o=dst, i=pv:
                        (e.tensor_copy(o, i) if eng == "v" else e.copy(o, i)),
                        waits=[("T", tv)] + list(v_waits), inc=sem)
            p.bank_guard[bank] = [(sem, vv)]
            return vv

        wslot_guard = [[], []]

        def wload(slot, wap, r0, rk, c0, cw, layer=None):
            view = wbuf[:, slot * 4096: slot * 4096 + rk * cw]
            src = wap if layer is None else wap[layer]
            src = src[r0:r0 + rk * 128, c0:c0 + cw].rearrange("(k p) n -> p k n", p=128)
            step = max(1, 2048 // 128)
            d = None
            sd = "DW%d" % slot
            for ks in range(0, rk, step):
                ke = min(rk, ks + step)
                d = dma("p", view.rearrange("p (k n) -> p k n", n=cw)[:, ks:ke, :],
                        src[:, ks:ke, :], sd,
                        waits=wslot_guard[slot] if ks == 0 else ())
            return view, (sd, d)

        def wdone(slot):
            wslot_guard[slot] = [("T", p.cnt["T"])]

        dma("y", cidf[:], g["c_idf"], "DXC")
        dma("y", cstb[:, 0:128], g["c_idb"], "DXC")
        p.emit("v", lambda v: v.memset(cst[:, 0:1], 1.0), inc="V")
        p.emit("v", lambda v: v.memset(cst[:, 90:91], float(EPS)), inc="V")
        nc.const_aps.register(F32, float(EPS), cst[:, 90:91])
        p.emit("v", lambda v: v.memset(cst[0:1, 96:224], 1.0), inc="V")
        p.emit("v", lambda v: v.memset(cstb[:, 128:129], 1.0), inc="V")
        p.emit("v", lambda v: v.memset(cstb[0:1, 144:272], 1.0), inc="V")
        dma("y", cst[:, 1:9], g["c_vscale"], "DXC")
        dma("y", cst[:, 9:17], g["c_cmask"], "DXC")
        dma("y", cst[:, 17:81], g["c_gdecay"], "DXC")
        dma("y", cst[:, 81:82], g["c_selw"], "DXC")
        dma("y", cmask01[:], g["c_mask01"], "DXC")
        dma("y", cst[:, 82:90], g["c_negj"], "DXC")
        dma("y", cqs[:].rearrange("p (h i) -> p h i", h=8),
            g["c_qscale"].rearrange("h p i -> p h i"), "DXC")
        dma("y", cinda[:], g["c_inda"], "DXC")
        dma("y", cindb[:], g["c_indb"], "DXC")
        dma("y", cindf[:], g["c_indf"], "DXC")
        dma("y", lncol[:], g["c_lncol"], "DXC")
        dma("y", bcol[:], g["c_bcol"], "DXC")
        dma("y", rbrow[:], g["L0_ffn_router_b"][None, :], "DXC")
        CONST_DX = [("DXC", p.cnt["DXC"]), ("V", p.cnt["V"])]

        scrhalf_guard = [[], []]

        def stage_in(src_dram, add=False, extra=()):
            for tt in range(NTT):
                hbuf = scr[:, (tt % 2) * 2048:(tt % 2) * 2048 + 2048]
                sd = "DX%d" % (tt % 2)
                d = dma("y", hbuf, src_dram[tt * 128:(tt + 1) * 128, :], sd,
                        waits=scrhalf_guard[tt % 2] + list(extra))
                for k in range(KT):
                    b = 2 + (k % 2)
                    sl = slice(k * TOK + tt * 128, k * TOK + tt * 128 + 128)
                    if not add:
                        tr128(xT[:, sl], hbuf[:, k * 128:(k + 1) * 128], b, False,
                              t_waits=[(sd, d)] + CONST_DX)
                    else:
                        pv = psum[b][:, 0:128]
                        tv = p.emit("t", lambda t, o=pv, i=hbuf[:, k * 128:(k + 1) * 128]:
                                    t.transpose(o, i, cidf[:]),
                                    waits=[(sd, d)] + p.bank_guard[b], inc="T")
                        vv = p.emit("v", lambda v, o=xT[:, sl], i=pv:
                                    v.tensor_tensor(o, o, i, ALU.add),
                                    waits=[("T", tv)] + xg(), inc="V")
                        p.bank_guard[b] = [("V", vv)]
                scrhalf_guard[tt % 2] = [("T", p.cnt["T"])]
            xT_guard[0] = [("V", p.cnt["V"])]
            for k in range(KT):
                xTk_guard[k] = [("V", p.cnt["V"])]

        def stage_out(dst_dram, waits=()):
            for tt in range(NTT):
                hbuf = scr[:, (tt % 2) * 2048:(tt % 2) * 2048 + 2048]
                sd = "DX%d" % (tt % 2)
                last = None
                for k in range(KT):
                    b = 2 + (k % 2)
                    sl = slice(k * TOK + tt * 128, k * TOK + tt * 128 + 128)
                    last = tr128(hbuf[:, k * 128:(k + 1) * 128], xT[:, sl], b, False,
                                 t_waits=list(waits), v_waits=scrhalf_guard[tt % 2])
                d = dma("y", dst_dram[tt * 128:(tt + 1) * 128, :], hbuf, sd,
                        waits=[("V", last)])
                scrhalf_guard[tt % 2] = [(sd, d)]
            p.emit("y", lambda q: q.nop(),
                   waits=[("DX0", p.cnt["DX0"]), ("DX1", p.cnt["DX1"])])

        xT_guard = [[]]
        xTk_guard = [[] for _ in range(KT)]
        gn_guard = [[]]
        gath_guard = [[]]

        def xg():
            return list(xT_guard[0])

        def xkw():
            return [list(xTk_guard[k]) for k in range(KT)]

        bcopy_guard = [[], []]
        norm_guard = [[], []]

        def layernorm(dst_b16, src, in_waits, gamma_col=None, beta_col=None, dst_f32=None,
                      dst_waits=(), in_waits_k=None):
            nv = ns = None
            for k in range(KT):
                bb = (vtk if k % 2 == 0 else kT)[:, 0:1024]
                sq = (vtk if k % 2 == 0 else kT)[:, 1024:2048]
                wk_ = list(in_waits_k[k]) if in_waits_k is not None else list(in_waits)
                nv = p.emit("s", lambda s, o=bb, i=src[:, k * TOK:(k + 1) * TOK]:
                            s.copy(o, i),
                            waits=wk_ + bcopy_guard[k % 2], inc="S")
                ns = p.emit("s", lambda s, o=sq, i=bb: s.activation(o, i, AF.Square),
                            waits=[("S", nv)] + bcopy_guard[k % 2], inc="S")
                for half in range(2):
                    mm(psum[half][0:1, :], cstb[:, 128:129], bb[:, half * 512:(half + 1) * 512],
                       start=(k == 0), stop=(k == KT - 1),
                       waits=[("S", nv)] + (p.bank_guard[half] if k == 0 else []),
                       inc="T")
                    mm(psum[2 + half][0:1, :], cstb[:, 128:129], sq[:, half * 512:(half + 1) * 512],
                       start=(k == 0), stop=(k == KT - 1),
                       waits=[("S", ns)] + (p.bank_guard[2 + half] if k == 0 else []),
                       inc="T")
                bcopy_guard[k % 2] = [("T", p.cnt["T"])]
            tdone = p.cnt["T"]
            bmm = None
            for half in range(2):
                vv = p.emit("v", lambda v, hh=half: v.tensor_scalar(
                    rows[0:1, 0:512], psum[hh][0:1, :], 1.0 / H, None, ALU.mult),
                    waits=[("T", tdone), ("T", p.cnt["T"])], inc="V")
                vv = p.emit("v", lambda v, hh=half: v.tensor_scalar(
                    rows[0:1, 512:1024], psum[2 + hh][0:1, :], 1.0 / H, None, ALU.mult),
                    waits=[("V", vv)], inc="V")
                bmm = mm(psum[4 + half][:, :], cst[0:1, 96:224], rows[0:1, 0:512],
                         start=True, stop=True,
                         waits=[("V", vv)] + p.bank_guard[4 + half])
                vv = p.emit("v", lambda v: v.tensor_tensor(
                    rows[0:1, 0:512], rows[0:1, 0:512], rows[0:1, 0:512], ALU.mult),
                    waits=[("V", vv), ("T", bmm)], inc="V")
                vv = p.emit("v", lambda v: v.tensor_tensor(
                    rows[0:1, 0:512], rows[0:1, 512:1024], rows[0:1, 0:512], ALU.subtract),
                    waits=[("V", vv)], inc="V")
                sv = p.emit("s", lambda s: s.activation(rows[0:1, 0:512], rows[0:1, 0:512],
                                                        AF.Ln, bias=float(EPS)),
                            waits=[("V", vv)], inc="S")
                sv = p.emit("s", lambda s: s.activation(rows[0:1, 0:512], rows[0:1, 0:512],
                                                        AF.Exp, scale=-0.5),
                            waits=[("S", sv)], inc="S")
                bmm = mm(psum[6 + half][:, :], cst[0:1, 96:224], rows[0:1, 0:512],
                         start=True, stop=True,
                         waits=[("S", sv)] + p.bank_guard[6 + half])
            vfin = p.cnt["V"]
            lasts = lastv = None
            half_done = {}
            hcnt = {}
            for half in range(2):
                for k in range(KT):
                    sl = slice(k * TOK + half * 512, k * TOK + (half + 1) * 512)
                    if dst_f32 is None:
                        t1 = scr[:, (k % 2) * 1024 + half * 512:(k % 2) * 1024 + half * 512 + 512]
                        exw = norm_guard[k % 2] + scrhalf_guard[0] + scrhalf_guard[1]
                    else:
                        t1 = dst_f32[:, sl]
                        exw = list(dst_waits)
                    lastv = p.emit("v", lambda v, o=t1, i=src[:, sl], mb=psum[4 + half][:, :]:
                                   v.tensor_tensor(o, i, mb, ALU.subtract),
                                   waits=[("T", bmm)] + exw, inc="V")
                    lastv = p.emit("v", lambda v, o=t1, rb=psum[6 + half][:, :]:
                                   v.tensor_tensor(o, o, rb, ALU.mult),
                                   waits=[("V", lastv)], inc="V")
                    if gamma_col is not None:
                        lasts = p.emit("s", lambda s, o=dst_b16[:, sl], i=t1,
                                       gc=gamma_col[:, k:k + 1], bc=beta_col[:, k:k + 1]:
                                       s.activation(o, i, AF.Identity, scale=gc, bias=bc),
                                       waits=[("V", lastv)], inc="S")
                    else:
                        lasts = p.emit("s", lambda s, o=dst_b16[:, sl], i=t1: s.copy(o, i),
                                       waits=[("V", lastv)], inc="S")
                    norm_guard[k % 2] = [("S", p.cnt["S"])]
                    if dst_f32 is not None and half == 1:
                        xTk_guard[k] = [("V", lastv)]
                half_done[half] = [("S", p.cnt["S"]), ("V", p.cnt["V"])]
                hcnt[half] = (p.cnt["V"], p.cnt["S"])
            if dst_f32 is None:
                scrhalf_guard[0] = scrhalf_guard[0] + [("S", p.cnt["S"])]
                scrhalf_guard[1] = scrhalf_guard[1] + [("S", p.cnt["S"])]
            for b in range(4):
                p.bank_guard[b] = [("V", vfin)]
            for half in range(2):
                p.bank_guard[4 + half] = [("V", hcnt[half][0]), ("S", hcnt[half][1])]
                p.bank_guard[6 + half] = [("V", hcnt[half][0]), ("S", hcnt[half][1])]
            half_done["all"] = [("S", lasts), ("V", lastv)]
            return half_done

        def retention_layer(l):
            xnw = layernorm(xnT, xT, xg(), gamma_col=lncol[:, 0:KT], beta_col=lncol[:, KT:2 * KT],
                            in_waits_k=xkw())
            zT_guard = [[]]
            wq, dq = wload(0, g["L0_attn_a_Wq"], 0, KT, 0 * HD, HD, layer=l)
            pend = {"q": (wq, dq, 0)}
            fin = [None]
            for h in range(NH):
                retention_head(l, h, xnw, pend, zT_guard, fin)
            fin[0]()
            fin[0] = None
            wo_pass(l, zT_guard)
            ffn(l)

        yhv_guard = [[]]

        def retention_head(l, h, xnw, pend, zT_guard, fin):
            ccq = ccn[0]
            ccn[0] += 1
            gam128 = float(np.float32(GAMMAS[h]) ** 128)
            qT = big[:, (2 * h) * TOK:(2 * h + 2) * TOK]
            q2b = scr[:, 1024 + (h % 2) * 1024: 2048 + (h % 2) * 1024].bitcast(BF16)
            wq, dq, slq = pend["q"]
            wk, dk = wload(1 - slq, g["L0_attn_a_Wk"], 0, KT, h * HD, HD, layer=l)
            for dt in range(2):
                for half in range(2):
                    bk = dt * 2 + half
                    for k in range(KT):
                        mm(psum[bk][:, :], wq[:, k * HD + dt * 128: k * HD + dt * 128 + 128],
                           xnT[:, k * TOK + half * 512: k * TOK + (half + 1) * 512],
                           start=(k == 0), stop=(k == KT - 1),
                           waits=([dq] + list(xnw[half]) + p.bank_guard[bk]) if k == 0 else [])
                    vcopy(qT[:, dt * TOK + half * 512: dt * TOK + (half + 1) * 512],
                          psum[bk][:, :], waits=[("T", p.cnt["T"])], eng="s")
                    p.bank_guard[bk] = [("S", p.cnt["S"])]
            wdone(slq)
            wv, dv = wload(slq, g["L0_attn_a_Wv"], 0, KT, h * HD, HD, layer=l)
            for dt in range(2):
                for half in range(2):
                    bk = 4 + dt * 2 + half
                    for k in range(KT):
                        mm(psum[bk][:, :], wk[:, k * HD + dt * 128: k * HD + dt * 128 + 128],
                           xnT[:, k * TOK + half * 512: k * TOK + (half + 1) * 512],
                           start=(k == 0), stop=(k == KT - 1),
                           waits=([dk] + list(xnw[half]) + p.bank_guard[bk]) if k == 0 else [])
                    vcopy(kT[:, dt * TOK + half * 512: dt * TOK + (half + 1) * 512],
                          psum[bk][:, :], waits=[("T", p.cnt["T"])], eng="s")
                    p.bank_guard[bk] = [("S", p.cnt["S"])]
            wdone(1 - slq)
            wg_, dg = wload(1 - slq, g["L0_attn_a_Wg"], 0, KT, h * HD, HD, layer=l)
            if fin[0] is not None:
                fin[0]()
                fin[0] = None
            head_T0 = p.cnt["T"]
            rz = p.emit("v", lambda v: v.memset(Rst[:, 0:512], 0.0),
                        waits=[("V", p.cnt["V"]), ("S", p.cnt["S"])], inc="V")
            k_ready_S = p.cnt["S"]
            cons_T = [0, 0]
            for i in range(NTT):
                bk = i % 4
                for k in range(KT):
                    mm(psum[bk][:, 0:HD], xnT[:, k * TOK + i * 128: k * TOK + i * 128 + 128],
                       wv[:, k * HD:(k + 1) * HD],
                       start=(k == 0), stop=(k == KT - 1),
                       waits=([dv] + list(xnw[i // 4]) + p.bank_guard[bk]) if k == 0 else [])
                sv_ = vcopy(vtk[:, i * HD:(i + 1) * HD], psum[bk][:, 0:HD],
                            waits=[("T", p.cnt["T"])], eng="s")
                p.bank_guard[bk] = [("S", sv_)]
                bi = i % 2
                base = bi * 128
                ktok = rows[:, base:base + 128].bitcast(BF16)
                vp = rows[:, 512 + base:512 + base + 128].bitcast(BF16)
                pv = vcopy(Rpfx[:, i * 512:(i + 1) * 512], Rst[:, 0:512],
                           waits=[("V", p.cnt["V"]), ("T", head_T0)], eng="s")
                evtr = None
                for dt in range(2):
                    evtr = tr128(ktok[:, dt * 128:(dt + 1) * 128],
                                 kT[:, dt * TOK + i * 128: dt * TOK + i * 128 + 128],
                                 4 + bi, True,
                                 t_waits=[("S", k_ready_S)],
                                 v_waits=[("T", cons_T[bi])] if cons_T[bi] else [])
                vv2 = p.emit("v", lambda v, o=vp, i2=vtk[:, i * HD:(i + 1) * HD],
                             sc=cst[:, 1 + h:2 + h]:
                             v.tensor_scalar(o, i2, sc, None, ALU.mult),
                             waits=[("S", sv_), ("V", p.cnt["V"]),
                                    ("T", cons_T[bi])], inc="V")
                for dt in range(2):
                    mm(psum[6 + dt][:, 0:HD], ktok[:, dt * 128:(dt + 1) * 128], vp,
                       start=True, stop=True,
                       waits=[("V", vv2), ("V", evtr)] + p.bank_guard[6 + dt])
                tui = p.cnt["T"]
                cons_T[bi] = tui
                for dt in range(2):
                    uv = p.emit("v", lambda v, o=Rst[:, dt * 256:(dt + 1) * 256],
                                ps=psum[6 + dt][:, 0:HD], gm=gam128:
                                v.scalar_tensor_tensor(o, o, gm, ps, ALU.mult, ALU.add),
                                waits=[("T", tui), ("S", pv)], inc="V")
                    p.bank_guard[6 + dt] = [("V", uv)]
            pfx_S = p.cnt["S"]
            wdone(slq)
            if h + 1 < NH:
                wqn, dqn = wload(slq, g["L0_attn_a_Wq"], 0, KT, (h + 1) * HD, HD, layer=l)
                pend["q"] = (wqn, dqn, slq)
            bv = vcopy(Rgcb[:, 0:512], Rst[:, 0:512],
                       waits=[("V", p.cnt["V"])] + ccbuf_guard.get("st_gcb", []))
            dcc = dma("y", g["cc_st_in"], Rgcb[:, 0:512], "DCU",
                      waits=[("V", bv)] + ccbuf_guard.get("st_in", []))
            ccs = "CC%d" % ccq
            p.emit("p", lambda gp: gp.collective_compute(
                "AllGather", ALU.bypass, replica_groups=[list(range(NCORE))],
                ins=[g["cc_st_in"]], outs=[g["cc_st_out"]]),
                waits=[("DCU", dcc)] + ccbuf_guard.get("st_out", []), inc=ccs)
            ccbuf_guard["st_in"] = [(ccs, 1)]
            for tt in range(NTT):
                bk = tt % 4
                for k in range(KT):
                    mm(psum[bk][:, 0:HD], xnT[:, k * TOK + tt * 128: k * TOK + tt * 128 + 128],
                       wg_[:, k * HD:(k + 1) * HD],
                       start=(k == 0), stop=(k == KT - 1),
                       waits=([dg] + list(xnw[tt // 4]) + p.bank_guard[bk]) if k == 0 else [])
                sg = p.emit("s", lambda s, o=gtk[:, tt * HD:(tt + 1) * HD], i2=psum[bk][:, 0:HD]:
                            s.activation(o, i2, AF.Sigmoid),
                            waits=[("T", p.cnt["T"])] + yhv_guard[0], inc="S")
                gv = p.emit("v", lambda v, o=gtk[:, tt * HD:(tt + 1) * HD],
                            ps=psum[bk][:, 0:HD]:
                            v.tensor_tensor(o, o, ps, ALU.mult),
                            waits=[("S", sg)], inc="V")
                p.bank_guard[bk] = [("V", gv)]
            wdone(1 - slq)
            pend["slot_for_next"] = 1 - slq
            qsc = cqs[:, h * 128:(h + 1) * 128]
            consq_T = [0, 0]
            for i in range(NTT):
                bi = i % 2
                base = bi * 256
                q1 = rows[:, base:base + 128].bitcast(BF16)
                smT = rows[:, 576 + bi * 64:576 + bi * 64 + 64].bitcast(BF16)
                vq = None
                for dt in range(2):
                    vq = p.emit("v", lambda v, o=q1[:, dt * 128:(dt + 1) * 128],
                                i2=qT[:, dt * TOK + i * 128: dt * TOK + i * 128 + 128], qs=qsc:
                                v.tensor_tensor(o, i2, qs, ALU.mult),
                                waits=([("T", consq_T[bi])] if consq_T[bi] else []) +
                                      [("V", p.cnt["V"])], inc="V")
                bsc = 4 + bi * 2
                for dt in range(2):
                    mm(psum[bsc][:, 0:128], kT[:, dt * TOK + i * 128: dt * TOK + i * 128 + 128],
                       q1[:, dt * 128:(dt + 1) * 128],
                       start=(dt == 0), stop=(dt == 1),
                       waits=([("V", vq)] + p.bank_guard[bsc]) if dt == 0 else [])
                tsc = p.cnt["T"]
                for dt in range(2):
                    p.emit("v", lambda v, o=q2b[:, dt * TOK + i * 128: dt * TOK + i * 128 + 128],
                           i2=q1[:, dt * 128:(dt + 1) * 128],
                           gm=float(np.float32(GAMMAS[h]) ** (128 * i)):
                           v.tensor_scalar(o, i2, gm, None, ALU.mult),
                           waits=[("T", tsc), ("V", vq)], inc="V")
                sv2 = p.emit("v", lambda v, o=smT, ps=psum[bsc][:, 0:128],
                             nj=cst[:, 82 + h:83 + h], mk=cmask01[:]:
                             v.scalar_tensor_tensor(o, ps, nj, mk, ALU.mult, ALU.mult),
                             waits=[("T", tsc)], inc="V")
                p.bank_guard[bsc] = [("V", sv2)]
                bky = 5 + bi * 2
                mm(psum[bky][:, 0:HD], smT, vtk[:, i * HD:(i + 1) * HD],
                   start=True, stop=False,
                   waits=[("V", sv2), ("V", vq), ("S", pfx_S)] + p.bank_guard[bky])
                for dt in range(2):
                    mm(psum[bky][:, 0:HD], q1[:, dt * 128:(dt + 1) * 128],
                       Rpfx[:, i * 512 + dt * 256: i * 512 + (dt + 1) * 256],
                       start=False, stop=(dt == 1), waits=[])
                tyi = p.cnt["T"]
                yv = vcopy(yhv[:, i * HD:(i + 1) * HD], psum[bky][:, 0:HD],
                           waits=[("T", tyi)] + yhv_guard[0], eng="s")
                p.bank_guard[bky] = [("S", yv)]
                consq_T[bi] = tyi

            def stage2():
                Gc = Rst[:, 512:1024]
                gv = p.emit("v", lambda v: v.memset(Gc, 0.0), waits=[("V", p.cnt["V"])], inc="V")
                for halfq in range(2):
                    d2 = dma("y", Rball[:, 0:2048].rearrange("p (r n) -> p r n", r=4),
                             g["cc_st_out"].rearrange("(r p) n -> p r n", p=128)[:, halfq * 4:(halfq + 1) * 4, :],
                             "DCG", waits=[(ccs, 1), ("V", gv)] + gath_guard[0])
                    for rr in range(4):
                        r = halfq * 4 + rr
                        gv2 = p.emit("v", lambda v, o=Gc, i=Rball[:, rr * 512:(rr + 1) * 512],
                                     sc=cst[:, 17 + h * 8 + r:18 + h * 8 + r]:
                                     v.scalar_tensor_tensor(o, i, sc, o, ALU.mult, ALU.add),
                                     waits=[("DCG", d2), ("V", gv)], inc="V")
                        gv = gv2
                gath_guard[0] = [("V", gv)]
                ccbuf_guard["st_out"] = [("DCG", p.cnt["DCG"])]
                gcb = vcopy(Rgcb[:, 0:512], Gc, waits=[("V", gv), ("DCU", dcc)])
                q2done = p.cnt["V"]
                for i in range(NTT):
                    bi = i % 2
                    bky = 5 + bi * 2
                    for dt in range(2):
                        mm(psum[bky][:, 0:HD], q2b[:, dt * TOK + i * 128: dt * TOK + i * 128 + 128],
                           Rgcb[:, dt * 256:(dt + 1) * 256],
                           start=(dt == 0), stop=(dt == 1),
                           waits=([("V", gcb), ("V", q2done)] + p.bank_guard[bky]) if dt == 0 else [])
                    tyg = p.cnt["T"]
                    yg = p.emit("v", lambda v, o=yhv[:, i * HD:(i + 1) * HD],
                                ps=psum[bky][:, 0:HD]:
                                v.tensor_tensor(o, o, ps, ALU.add),
                                waits=[("T", tyg), ("S", p.cnt["S"])], inc="V")
                    p.bank_guard[bky] = [("V", yg)]
                ccbuf_guard["st_gcb"] = [("T", p.cnt["T"])]
                ydone = [("S", p.cnt["S"])]
                bn1 = None
                for cq in range(8):
                    bn1 = p.emit("v", lambda v, c2=cq: v.bn_stats(
                        rows[:, 640 + c2 * 6:646 + c2 * 6], yhv[:, c2 * 256:(c2 + 1) * 256]),
                        waits=ydone + [("V", p.cnt["V"])], inc="V")
                bn2 = None
                for cq in range(8):
                    bn2 = p.emit("v", lambda v, c2=cq: v.bn_aggr(
                        rows[:, 688 + c2 * 2:690 + c2 * 2], rows[:, 640 + c2 * 6:646 + c2 * 6]),
                        waits=[("V", bn1)], inc="V")
                sv = p.emit("s", lambda s: s.activation(rows[:, 704:712],
                            rows[:, 688:704].rearrange("p (c n) -> p c n", c=8)[:, :, 1:2],
                            AF.Ln, bias=float(EPS)), waits=[("V", bn2)], inc="S")
                sv = p.emit("s", lambda s: s.activation(rows[:, 704:712], rows[:, 704:712],
                                                        AF.Exp, scale=-0.5),
                            waits=[("S", sv)], inc="S")
                dgn = dma("p", gnga[:], g["L0_attn_a_gn_gamma"][l][None, h * HD:(h + 1) * HD],
                          "DXG", waits=gn_guard[0])
                dgn = dma("p", gnbb[:], g["L0_attn_a_gn_beta"][l][None, h * HD:(h + 1) * HD],
                          "DXG", waits=())
                gmm = mm(psum[4][:, 0:HD], cstb[0:1, 144:272], gnga[:], start=True, stop=True,
                         waits=[("DXG", dgn)] + p.bank_guard[4])
                gmm = mm(psum[5][:, 0:HD], cstb[0:1, 144:272], gnbb[:], start=True, stop=True,
                         waits=p.bank_guard[5])
                gn_guard[0] = [("T", p.cnt["T"])]
                lastz = None
                for i in range(NTT):
                    t1 = scr[:, 0:256]
                    zv = p.emit("v", lambda v, o=t1, i2=yhv[:, i * HD:(i + 1) * HD],
                                mc=rows[:, 688 + 2 * i:689 + 2 * i], rc=rows[:, 704 + i:705 + i]:
                                v.tensor_scalar(o, i2, mc, rc, ALU.subtract, op1=ALU.mult),
                                waits=[("S", sv), ("V", p.cnt["V"]), ("T", gmm)], inc="V")
                    zv = p.emit("v", lambda v, o=t1, gb=psum[4][:, 0:HD]:
                                v.tensor_tensor(o, o, gb, ALU.mult), waits=[("V", zv)], inc="V")
                    zv = p.emit("v", lambda v, o=t1, bb=psum[5][:, 0:HD]:
                                v.tensor_tensor(o, o, bb, ALU.add), waits=[("V", zv)], inc="V")
                    zv = p.emit("v", lambda v, o=chz[:, 0:256], i2=t1, gk=gtk[:, i * HD:(i + 1) * HD]:
                                v.tensor_tensor(o, i2, gk, ALU.mult), waits=[("V", zv)], inc="V")
                    for dt in range(2):
                        kz = h * 2 + dt
                        lastz = tr128(zT[:, kz * TOK + i * 128: kz * TOK + i * 128 + 128],
                                      chz[:, dt * 128:(dt + 1) * 128], 6 + dt, True,
                                      t_waits=[("V", zv)])
                p.bank_guard[4] = [("V", p.cnt["V"])]
                p.bank_guard[5] = [("V", p.cnt["V"])]
                zT_guard[0] = zT_guard[0] + [("V", p.cnt["V"])]
                yhv_guard[0] = [("V", p.cnt["V"])]

            fin[0] = stage2

        def wo_pass(l, zT_guard):
            slot = [0]
            wo, do = wload(0, g["L0_attn_a_Wo"], 0, KT, 0, HD, layer=l)
            for sl8 in range(8):
                won, don = (wload(1 - slot[0], g["L0_attn_a_Wo"], 0, KT, (sl8 + 1) * HD, HD, layer=l)
                            if sl8 + 1 < 8 else (None, None))
                for co in range(2):
                    ct = sl8 * 2 + co
                    for half in range(2):
                        bk = (ct * 2 + half) % 4
                        for k in range(KT):
                            mm(psum[bk][:, :], wo[:, k * HD + co * 128: k * HD + co * 128 + 128],
                               zT[:, k * TOK + half * 512: k * TOK + (half + 1) * 512],
                               start=(k == 0), stop=(k == KT - 1),
                               waits=([do] + zT_guard[0] + p.bank_guard[bk]) if k == 0 else [])
                        sl = slice(ct * TOK + half * 512, ct * TOK + (half + 1) * 512)
                        rv = p.emit("v", lambda v, o=xT[:, sl], ps=psum[bk][:, :]:
                                    v.tensor_tensor(o, o, ps, ALU.add),
                                    waits=[("T", p.cnt["T"])] + xg(), inc="V")
                        p.bank_guard[bk] = [("V", rv)]
                        if half == 1:
                            xTk_guard[ct] = [("V", rv)]
                wdone(slot[0])
                if won is not None:
                    wo, do = won, don
                    slot[0] = 1 - slot[0]
            xT_guard[0] = [("V", p.cnt["V"])]

        def ffn(l):
            rnw = layernorm(xnT, xT, xg(), gamma_col=lncol[:, 0:KT], beta_col=lncol[:, KT:2 * KT],
                            in_waits_k=xkw())
            nffs = 4
            cur = [0]
            wv_, dv_ = wload(0, g["L0_attn_a_W1"], 0, KT, 0, HD, layer=l)
            for ffs in range(nffs):
                for s8 in range(8):
                    if s8 < 7:
                        nxt = wload(1 - cur[0], g["L0_attn_a_W1"], 0, KT,
                                    (ffs * 16 + (s8 + 1) * 2) * 128, HD, layer=l)
                    else:
                        nxt = wload(1 - cur[0], g["L0_attn_a_W2"], ffs * 2048, KT, 0, HD, layer=l)
                    for fi in range(2):
                        ft = ffs * 16 + s8 * 2 + fi
                        for half in range(2):
                            bk = (fi * 2 + half) % 4
                            for k in range(KT):
                                mm(psum[bk][:, :], wv_[:, k * HD + fi * 128: k * HD + fi * 128 + 128],
                                   xnT[:, k * TOK + half * 512: k * TOK + (half + 1) * 512],
                                   start=(k == 0), stop=(k == KT - 1),
                                   waits=([dv_] + list(rnw[half]) + p.bank_guard[bk]) if k == 0 else [])
                            gelu_evac(psum[bk], h2[:, (ft % 16) * TOK + half * 512:
                                                   (ft % 16) * TOK + (half + 1) * 512],
                                      bcol[:, l * 64 + ft: l * 64 + ft + 1], bk)
                    wdone(cur[0])
                    wv_, dv_ = nxt
                    cur[0] = 1 - cur[0]
                h2done = [("V", p.cnt["V"])]
                for cs in range(8):
                    if cs < 7:
                        nxt = wload(1 - cur[0], g["L0_attn_a_W2"], ffs * 2048, KT,
                                    (cs + 1) * HD, HD, layer=l)
                    elif ffs + 1 < nffs:
                        nxt = wload(1 - cur[0], g["L0_attn_a_W1"], 0, KT,
                                    (ffs + 1) * 2048, HD, layer=l)
                    else:
                        nxt = None
                    for co in range(2):
                        ct = cs * 2 + co
                        for half in range(2):
                            bk = (co * 2 + half) % 4 + 4
                            for k in range(KT):
                                mm(psum[bk][:, :], wv_[:, k * HD + co * 128: k * HD + co * 128 + 128],
                                   h2[:, k * TOK + half * 512: k * TOK + (half + 1) * 512],
                                   start=(k == 0), stop=(k == KT - 1),
                                   waits=([dv_] + h2done + p.bank_guard[bk]) if k == 0 else [])
                            sl = slice(ct * TOK + half * 512, ct * TOK + (half + 1) * 512)
                            if ffs == 0:
                                rv = p.emit("v", lambda v, o=xT[:, sl], ps=psum[bk][:, :],
                                            bc=bcol[:, 128 + l * KT + ct:129 + l * KT + ct]:
                                            v.scalar_tensor_tensor(o, ps, bc, o, ALU.add, ALU.add),
                                            waits=[("T", p.cnt["T"])] + xg(), inc="V")
                            else:
                                rv = p.emit("v", lambda v, o=xT[:, sl], ps=psum[bk][:, :]:
                                            v.tensor_tensor(o, o, ps, ALU.add),
                                            waits=[("T", p.cnt["T"])] + xg(), inc="V")
                            p.bank_guard[bk] = [("V", rv)]
                            if ffs == nffs - 1 and half == 1:
                                xTk_guard[ct] = [("V", rv)]
                    wdone(cur[0])
                    if nxt is not None:
                        wv_, dv_ = nxt
                        cur[0] = 1 - cur[0]
            xT_guard[0] = [("V", p.cnt["V"])]
            sg2 = [("V", p.cnt["V"]), ("S", p.cnt["S"])]
            scrhalf_guard[0] = scrhalf_guard[0] + sg2
            scrhalf_guard[1] = scrhalf_guard[1] + sg2

        gelu_scr_guard = [[], [], [], []]

        def gelu_evac(ps, dst, b1c, bk):
            gi = bk
            xs = scr[:, gi * 1024: gi * 1024 + 512]
            t2 = scr[:, gi * 1024 + 512: gi * 1024 + 1024]
            sv = p.emit("s", lambda s, o=xs, i=ps[:, :], bc=b1c:
                        s.activation(o, i, AF.Identity, bias=bc),
                        waits=[("T", p.cnt["T"])] + gelu_scr_guard[gi] + scrhalf_guard[gi // 2], inc="S")
            sv2 = p.emit("s", lambda s, o=t2, i=xs: s.activation(o, i, AF.Square),
                         waits=[("S", sv)], inc="S")
            vv = p.emit("v", lambda v, o=t2: v.tensor_scalar(o, o, C3G, C1G, ALU.mult, op1=ALU.add),
                        waits=[("S", sv2)], inc="V")
            vv = p.emit("v", lambda v, o=t2, i=xs: v.tensor_tensor(o, o, i, ALU.mult),
                        waits=[("V", vv)], inc="V")
            sv3 = p.emit("s", lambda s, o=t2: s.activation(o, o, AF.Sigmoid),
                         waits=[("V", vv)], inc="S")
            vv = p.emit("v", lambda v, o=dst, a=xs, b=t2: v.tensor_tensor(o, a, b, ALU.mult),
                        waits=[("S", sv3)], inc="V")
            gelu_scr_guard[gi] = [("V", vv)]
            p.bank_guard[bk] = [("S", sv)]

        def moe():
            ccq = ccn[0]
            ccn[0] += 1
            moescr_guard = [[], []]
            xnw = layernorm(xnT, xT, xg(), in_waits_k=xkw())
            ln_T = p.cnt["T"]
            scratch = [big[:, 10496 + i * 1024: 11520 + i * 1024] for i in range(5)] \
                + [kT[:, 0:1024], kT[:, 1024:2048],
                   vtk[:, 0:1024], vtk[:, 1024:2048],
                   gtk[:, 0:1024], gtk[:, 1024:2048]]
            NPRE = len(scratch)
            we, de = wload(1, g["L0_ffn_expert_w"], 0, KT, 0, HD)
            res = {"wv": None}
            evS = {}

            def expert_mms():
                weL, deL = we, de
                cur = 1
                for cs in range(8):
                    for co in range(2):
                        ct = cs * 2 + co
                        for half in range(2):
                            bk = (co * 2 + half) % 4
                            for k in range(KT):
                                mm(psum[bk][:, :],
                                   weL[:, k * HD + co * 128: k * HD + co * 128 + 128],
                                   xnT[:, k * TOK + half * 512: k * TOK + (half + 1) * 512],
                                   start=(k == 0), stop=(k == KT - 1),
                                   waits=([deL] + list(xnw[half]) + p.bank_guard[bk])
                                         if k == 0 else [])
                                yield
                            if ct < NPRE:
                                xw = [("T", ln_T)] if ct >= 5 else []
                                ev = p.emit("s", lambda s,
                                            o=scratch[ct][:, half * 512:(half + 1) * 512],
                                            i=psum[bk][:, :],
                                            bc=bcol[:, 128 + 2 * KT + ct:129 + 2 * KT + ct]:
                                            s.activation(o, i, AF.Identity, bias=bc),
                                            waits=[("T", p.cnt["T"])] + xw, inc="S")
                                evS[ct] = ev
                                p.bank_guard[bk] = [("S", ev)]
                            else:
                                assert res["wv"] is not None
                                sl = slice(ct * TOK + half * 512, ct * TOK + (half + 1) * 512)
                                ms = moescr[:, half * 512:(half + 1) * 512]
                                ev = p.emit("s", lambda s, o=ms, i=psum[bk][:, :],
                                            bc=bcol[:, 128 + 2 * KT + ct:129 + 2 * KT + ct]:
                                            s.activation(o, i, AF.Identity, bias=bc),
                                            waits=[("T", p.cnt["T"])] + moescr_guard[half],
                                            inc="S")
                                ev2 = p.emit("v", lambda v, o=ms,
                                             wb=wBsb[:, half * 512:(half + 1) * 512]:
                                             v.tensor_tensor(o, o, wb, ALU.mult),
                                             waits=[("S", ev), ("V", res["wv"])], inc="V")
                                ev2 = p.emit("v", lambda v, o=xT[:, sl], i=ms:
                                             v.tensor_tensor(o, o, i, ALU.add),
                                             waits=[("V", ev2)] + xg(), inc="V")
                                moescr_guard[half] = [("V", ev2)]
                                p.bank_guard[bk] = [("S", ev), ("V", ev2)]
                                if half == 1:
                                    xTk_guard[ct] = [("V", ev2)]
                    nxt = (wload(1 - cur, g["L0_ffn_expert_w"], 0, KT, (cs + 1) * HD, HD)
                           if cs < 7 else None)
                    wdone(cur)
                    if nxt is not None:
                        weL, deL = nxt
                        cur = 1 - cur

            gen = expert_mms()

            def pump(n):
                for _ in range(n):
                    if next(gen, "END") == "END":
                        break

            rw = wbuf[:, 0:256]
            drw = None
            src = g["L0_ffn_router_w"].rearrange("(k p) n -> p k n", p=128)
            drw = dma("p", rw.rearrange("p (k n) -> p k n", n=16), src, "DW0",
                      waits=wslot_guard[0])
            drw = ("DW0", drw)
            smw = None
            for tt in range(NTT):
                bk = 4 + tt % 2
                for k in range(KT):
                    mm(psum[bk][:, 0:16], xnT[:, k * TOK + tt * 128: k * TOK + tt * 128 + 128],
                       rw[:, k * 16:(k + 1) * 16],
                       start=(k == 0), stop=False,
                       waits=([drw] + list(xnw[tt // 4]) + p.bank_guard[bk]) if k == 0 else [])
                mm(psum[bk][:, 0:16], cst[0:1, 96:224], rbrow[:],
                   start=False, stop=True, waits=CONST_DX, inc="T")
                tl = p.cnt["T"]
                blk = Smine[:, tt * 16:(tt + 1) * 16]
                nm = rows[:, 0:1]
                vv = p.emit("v", lambda v, o=nm, i=psum[bk][:, 0:16]:
                            v.reduce_max(o, i, axis=AX.X, negate=True),
                            waits=[("T", tl), ("V", p.cnt["V"])], inc="V")
                sv = p.emit("s", lambda s, o=blk, i=psum[bk][:, 0:16], b=nm,
                            ac=rows[:, 1:2]:
                            s.activation(o, i, AF.Exp, bias=b, accum_out=ac),
                            waits=[("V", vv), ("S", p.cnt["S"])], inc="S")
                vv = p.emit("v", lambda v, o=rows[:, 2:3], i=rows[:, 1:2]:
                            v.reciprocal(o, i), waits=[("S", sv)], inc="V")
                smw = p.emit("v", lambda v, o=blk, sc=rows[:, 2:3]:
                             v.tensor_scalar(o, o, sc, None, ALU.mult),
                             waits=[("V", vv)], inc="V")
                p.bank_guard[bk] = [("S", sv)]
                bkt = 2 + bk
                pvt = psum[bkt][0:16, 0:128]
                tvt = p.emit("t", lambda t, o=pvt, i=blk: t.transpose(o, i, cidf[:]),
                             waits=[("V", smw)] + p.bank_guard[bkt], inc="T")
                svt = p.emit("v", lambda v, o=SmT[:, tt * 128:(tt + 1) * 128], i=pvt:
                             v.tensor_copy(o, i), waits=[("T", tvt)], inc="V")
                p.bank_guard[bkt] = [("V", svt)]
                pump(4)
            wdone(0)
            stw = p.cnt["V"]
            dsm = dma("y", g["cc_s_in"], SmT, "DCU", waits=[("V", stw)] + ccbuf_guard.get("s", []))
            ccs = "CC%d" % ccq
            p.emit("p", lambda gp: gp.collective_compute(
                "AllGather", ALU.bypass, replica_groups=[list(range(NCORE))],
                ins=[g["cc_s_in"]], outs=[g["cc_s_out"]]),
                waits=[("DCU", dsm)], inc=ccs)
            d2 = dma("y", Sall, g["cc_s_out"], "DCG", waits=[(ccs, 1)])
            ccbuf_guard["s"] = [(ccs, 1), ("DCG", d2)]
            taucol = rows[:, 3:4]
            csum = rows[:, 4:5]
            locol = rows[:, 5:6]
            selcol = rows[:, 6:7]
            midcol = rows[:, 7:8]
            last = {"v": p.cnt["V"], "s": p.cnt["S"]}

            def _stepA(it):
                a = p.emit("v", lambda v, c=0.5 ** (it + 1): v.tensor_scalar(
                    midcol, locol, c, None, ALU.add),
                    waits=[("V", last["v"])], inc="V")
                a = p.emit("v", lambda v: v.tensor_scalar(cmpb[:], Sall, midcol, 0.0,
                                                          ALU.is_ge, op1=ALU.add,
                                                          accum_out=csum),
                           waits=[("V", a)] + ([("DCG", d2)] if it == 0 else []), inc="V")
                last["v"] = a

            def _stepB(it):
                tb = mm(psum[4][:, 0:1], cindf[:], csum, start=True, stop=True,
                        waits=[("V", last["v"])] + p.bank_guard[4]
                              + (CONST_DX if it == 0 else []),
                        inc="T")
                a = p.emit("v", lambda v: v.tensor_scalar(selcol, psum[4][:, 0:1],
                                                          float(KTOP), None, ALU.is_gt),
                           waits=[("T", tb), ("V", last["v"])], inc="V")
                p.bank_guard[4] = [("V", a)]
                a = p.emit("v", lambda v, c=0.5 ** (it + 1): v.tensor_scalar(
                    selcol, selcol, c, None, ALU.mult),
                    waits=[("V", a)], inc="V")
                a = p.emit("v", lambda v: v.tensor_tensor(locol, locol, selcol, ALU.add),
                           waits=[("V", a)], inc="V")
                last["v"] = a

            def _stepF():
                a = p.emit("v", lambda v, c=0.5 ** BISECT_ITERS: v.tensor_scalar(
                    taucol, locol, c, None, ALU.add),
                    waits=[("V", last["v"])], inc="V")
                a = p.emit("v", lambda v: v.tensor_scalar(cmpb[:], Sall, taucol, None,
                                                          ALU.is_ge),
                           waits=[("V", a)], inc="V")
                a = p.emit("v", lambda v: v.tensor_tensor(cmpb[:], cmpb[:], Sall, ALU.mult),
                           waits=[("V", a)], inc="V")
                last["v"] = a

            def _stepW(half):
                wmm = mm(psum[4][0:1, :], cst[:, 81:82], cmpb[:, half * 512:(half + 1) * 512],
                         start=True, stop=True,
                         waits=[("V", last["v"])] + p.bank_guard[4], inc="T")
                a = p.emit("v", lambda v, hh=half: v.tensor_copy(
                    rows[0:1, hh * 512:(hh + 1) * 512], psum[4][0:1, :]),
                    waits=[("T", wmm)], inc="V")
                p.bank_guard[4] = [("V", a)]
                wm2 = mm(psum[5 + half][:, :], cst[0:1, 96:224],
                         rows[0:1, half * 512:(half + 1) * 512],
                         start=True, stop=True, waits=[("V", a)] + p.bank_guard[5 + half],
                         inc="T")
                a = p.emit("v", lambda v, hh=half: v.tensor_copy(
                    wBsb[:, hh * 512:(hh + 1) * 512], psum[5 + hh][:, :]),
                    waits=[("T", wm2)], inc="V")
                p.bank_guard[5 + half] = [("V", a)]
                last["v"] = a
                if half == 1:
                    res["wv"] = a

            def _memsets():
                a = p.emit("v", lambda v: v.memset(locol, 0.0),
                           waits=[("V", last["v"])], inc="V")
                last["v"] = a
                _stepA(0)

            steps = [(4, _memsets), (110, lambda: _stepB(0))]
            for it in range(1, BISECT_ITERS):
                steps.append((3, lambda it=it: _stepA(it)))
                steps.append((3, lambda it=it: _stepB(it)))
            steps.append((3, _stepF))
            steps.append((3, lambda: _stepW(0)))
            steps.append((3, lambda: _stepW(1)))

            for gap, fn in steps:
                pump(gap)
                fn()
            pump(1 << 30)
            for ti in range(NPRE):
                for half in range(2):
                    tmp = cmpb[:, half * 512:(half + 1) * 512]
                    sl = slice(ti * TOK + half * 512, ti * TOK + (half + 1) * 512)
                    c1 = p.emit("v", lambda v, o=tmp,
                                a=scratch[ti][:, half * 512:(half + 1) * 512],
                                wb=wBsb[:, half * 512:(half + 1) * 512]:
                                v.tensor_tensor(o, a, wb, ALU.mult),
                                waits=[("S", evS[ti]), ("V", res["wv"])], inc="V")
                    c2 = p.emit("v", lambda v, o=xT[:, sl], i=tmp:
                                v.tensor_tensor(o, o, i, ALU.add),
                                waits=[("V", c1)] + xg(), inc="V")
                    if half == 1:
                        xTk_guard[ti] = [("V", c2)]
            xT_guard[0] = [("V", p.cnt["V"])]

        def attn_b():
            ccq = ccn[0]
            ccn[0] += 1
            wsl_guard = [[], []]
            wTb_guard = [0, 0]
            totals_done = []
            bk_ = xkw()
            xnw = layernorm(xnT, xT, xg(),
                            in_waits_k=[bk_[k] + bk_[8] for k in range(KT)])
            wslA = [[[], []], [[], []]]

            def stg(wi, par):
                return scr[:, wi * 2048 + par * 1024: wi * 2048 + par * 1024 + 1024].bitcast(BF16)

            for et in range(KT):
                par = et % 2
                for wi, wnm in ((0, "L0_attn_b_Wk"), (1, "L0_attn_b_Wv")):
                    if True:
                        bufb = stg(wi, par)
                        sd = "DW%d" % wi
                        d = dma("p", bufb, g[wnm][et * 128:(et + 1) * 128, :], sd,
                                waits=wslA[wi][par] + (scrhalf_guard[0] + scrhalf_guard[1] if et < 2 else []))
                        for k in range(KT):
                            tr128(wTb[:, wi * 2048 + k * 128: wi * 2048 + (k + 1) * 128],
                                  bufb[:, k * 128:(k + 1) * 128], 2 + (k % 2), True,
                                  t_waits=[(sd, d)],
                                  v_waits=[("T", wTb_guard[wi])] if wTb_guard[wi] else [])
                        wslA[wi][par] = [("T", p.cnt["T"])]
                    wtr = p.cnt["V"]
                    wtr_s = p.cnt["S"]
                    for half in range(2):
                        bk = 4 + wi * 2 + half
                        for k in range(KT):
                            mm(psum[bk][:, :],
                               wTb[:, wi * 2048 + k * 128: wi * 2048 + (k + 1) * 128],
                               xnT[:, k * TOK + half * 512: k * TOK + (half + 1) * 512],
                               start=(k == 0), stop=(k == KT - 1),
                               waits=([("V", wtr)] + list(xnw[half]) + p.bank_guard[bk]) if k == 0 else [])
                        if wi == 0:
                            ev = vcopy(kT[:, half * 512:(half + 1) * 512], psum[bk][:, :],
                                       waits=[("T", p.cnt["T"])], eng="s")
                            p.bank_guard[bk] = [("S", ev)]
                        else:
                            ev = vcopy(vtk[:, half * 512:(half + 1) * 512], psum[bk][:, :],
                                       waits=[("T", p.cnt["T"]), ("V", p.cnt["V"])], eng="s")
                            p.bank_guard[bk] = [("S", ev)]
                            p.emit("v", lambda v, o=abkv[:, half * 512:(half + 1) * 512],
                                   vv_=vtk[:, half * 512:(half + 1) * 512],
                                   kk=kT[:, half * 512:(half + 1) * 512]:
                                   v.tensor_tensor(o, vv_, kk, ALU.mult),
                                   waits=[("S", ev), ("V", p.cnt["V"])], inc="V")
                wTb_guard[0] = p.cnt["T"]
                wTb_guard[1] = p.cnt["T"]
                kvw = p.cnt["V"]
                tv = p.emit("v", lambda v, o=rows[:, et:et + 1], i=abkv[:, 0:1024]:
                            v.tensor_reduce(o, i, AX.X, ALU.add),
                            waits=[("V", kvw)], inc="V")
                sc = p.emit("v", lambda v, o=kvh[:, et * TOK:(et + 1) * TOK], i=abkv[:, 0:1024]:
                            v.tensor_tensor_scan(o, i, i, 0.0, ALU.add, ALU.bypass),
                            waits=[("V", kvw)], inc="V")
                totals_done = [("V", p.cnt["V"])]
            dcc = dma("y", g["cc_cs_in"], rows[:, 0:KT], "DCU", waits=totals_done)
            ccs = "CC%d" % ccq
            dq0 = [None, None]
            for et0 in range(2):
                dq0[et0] = dma("p", stg(0, et0), g["L0_attn_b_Wq"][et0 * 128:(et0 + 1) * 128, :],
                               "DW0", waits=wslA[0][et0])
            p.emit("p", lambda gp: gp.collective_compute(
                "AllGather", ALU.bypass, replica_groups=[list(range(NCORE))],
                ins=[g["cc_cs_in"]], outs=[g["cc_cs_out"]]),
                waits=[("DCU", dcc)], inc=ccs)
            d2 = dma("y", rows[:, 32:160].rearrange("p (r n) -> p r n", r=8),
                     g["cc_cs_out"].rearrange("(r p) n -> p r n", p=128), "DCG",
                     waits=[(ccs, 1)])
            cv = p.emit("v", lambda v: v.memset(rows[:, 16:32], 0.0),
                        waits=[("DCG", d2)], inc="V")
            for r in range(NCORE):
                cv = p.emit("v", lambda v, rr=r: v.scalar_tensor_tensor(
                    rows[:, 16:32], rows[:, 32 + rr * 16:48 + rr * 16],
                    cst[:, 9 + rr:10 + rr],
                    rows[:, 16:32], ALU.mult, ALU.add),
                    waits=[("V", cv)], inc="V")
            for et in range(KT):
                par = et % 2
                bufq = stg(0, par)
                d = (dq0[et] if et < 2 else
                     dma("p", bufq, g["L0_attn_b_Wq"][et * 128:(et + 1) * 128, :], "DW0",
                         waits=wslA[0][par]))
                for k in range(KT):
                    tr128(wTb[:, k * 128:(k + 1) * 128], bufq[:, k * 128:(k + 1) * 128],
                          2 + (k % 2), True, t_waits=[("DW0", d)])
                wslA[0][par] = [("T", p.cnt["T"])]
                wtr = p.cnt["V"]
                for half in range(2):
                    bk = 4 + half
                    for k in range(KT):
                        mm(psum[bk][:, :], wTb[:, k * 128:(k + 1) * 128],
                           xnT[:, k * TOK + half * 512: k * TOK + (half + 1) * 512],
                           start=(k == 0), stop=(k == KT - 1),
                           waits=([("V", wtr)] + p.bank_guard[bk]) if k == 0 else [])
                    ev = vcopy(kT[:, half * 512:(half + 1) * 512], psum[bk][:, :],
                               waits=[("T", p.cnt["T"])], eng="s")
                    p.bank_guard[bk] = [("S", ev)]
                qw = p.cnt["S"]
                tv = p.emit("v", lambda v, o=abkv[:, 0:1024], i=kvh[:, et * TOK:(et + 1) * TOK],
                            cc=rows[:, 16 + et:17 + et]:
                            v.tensor_scalar(o, i, cc, DENOM_B, ALU.add, op1=ALU.mult),
                            waits=[("V", cv), ("V", p.cnt["V"]), ("S", qw)], inc="V")
                tv = p.emit("v", lambda v, o=kvh[:, et * TOK:(et + 1) * TOK],
                            a=abkv[:, 0:1024], b=kT[:, 0:1024]:
                            v.tensor_tensor(o, a, b, ALU.mult),
                            waits=[("V", tv)], inc="V")
            vhat_done = [("V", p.cnt["V"])]
            for ft in range(KT):
                par = ft % 2
                bufo = stg(0, par)
                d = dma("p", bufo, g["L0_attn_b_Wo"][ft * 128:(ft + 1) * 128, :], "DW0",
                        waits=wslA[0][par])
                for k in range(KT):
                    tr128(wTb[:, k * 128:(k + 1) * 128], bufo[:, k * 128:(k + 1) * 128],
                          2 + (k % 2), True, t_waits=[("DW0", d)])
                wslA[0][par] = [("T", p.cnt["T"])]
                wtr = p.cnt["V"]
                for half in range(2):
                    bk = 4 + half
                    for k in range(KT):
                        mm(psum[bk][:, :], wTb[:, k * 128:(k + 1) * 128],
                           kvh[:, k * TOK + half * 512: k * TOK + (half + 1) * 512],
                           start=(k == 0), stop=(k == KT - 1),
                           waits=([("V", wtr)] + vhat_done + p.bank_guard[bk]) if k == 0 else [])
                    sl = slice(ft * TOK + half * 512, ft * TOK + (half + 1) * 512)
                    rv = p.emit("v", lambda v, o=xT[:, sl], ps=psum[bk][:, :]:
                                v.tensor_tensor(o, o, ps, ALU.add),
                                waits=[("T", p.cnt["T"])] + xg(), inc="V")
                    p.bank_guard[bk] = [("V", rv)]
                    if half == 1:
                        xTk_guard[ft] = [("V", rv)]
            xT_guard[0] = [("V", p.cnt["V"])]

        stage_in(g["x_in"])

        if upto == 0:
            stage_out(g["y_out"], waits=xg())
        elif tap == "ln_affine":
            w = layernorm(xnT, xT, xg(), gamma_col=lncol[:, 0:KT], beta_col=lncol[:, KT:2 * KT])["all"]
            for k in range(KT):
                vcopy(xT[:, k * TOK:(k + 1) * TOK], xnT[:, k * TOK:(k + 1) * TOK], waits=w)
            stage_out(g["y_out"], waits=[("V", p.cnt["V"])])
        elif tap == "ln_plain":
            w = layernorm(xnT, xT, xg())["all"]
            for k in range(KT):
                vcopy(xT[:, k * TOK:(k + 1) * TOK], xnT[:, k * TOK:(k + 1) * TOK], waits=w)
            stage_out(g["y_out"], waits=[("V", p.cnt["V"])])
        else:
            for c in range(4):
                dma("y", g["spill_x"][:, c * 4096:(c + 1) * 4096],
                    xT[:, c * 4096:(c + 1) * 4096], "DX0",
                    waits=xg() if c == 0 else ())
            spill_done = [("DX0", p.cnt["DX0"])]
            lnw = layernorm(xnT, xT, xg(), dst_f32=xT, dst_waits=spill_done, in_waits_k=xkw())
            xT_guard[0] = lnw["all"]
            if tap == "msr0":
                xnw2 = layernorm(xnT, xT, xg(), gamma_col=lncol[:, 0:KT],
                                 beta_col=lncol[:, KT:2 * KT])
                wq, dq = wload(0, g["L0_attn_a_Wq"], 0, KT, 0, HD, layer=0)
                pend = {"q": (wq, dq, 0)}
                ztg = [[]]
                if NH < HEADS:
                    p.emit("v", lambda v: v.memset(big[:], 0.0), inc="V")
                fin2 = [None]
                for h in range(NH):
                    retention_head(0, h, xnw2, pend, ztg, fin2)
                fin2[0]()
                zv0 = p.emit("v", lambda v: v.memset(xT[:], 0.0),
                             waits=[("V", p.cnt["V"]), ("S", p.cnt["S"]),
                                    ("T", p.cnt["T"])], inc="V")
                xT_guard[0] = [("V", zv0)]
                wo_pass(0, ztg)
                stage_out(g["y_out"], waits=xg())
            else:
                for l in range(2):
                    retention_layer(l)
                for c in range(8):
                    hbuf = scr[:, (c % 2) * 2048:(c % 2) * 2048 + 2048]
                    dld = dma("y", hbuf, g["spill_x"][:, c * 2048:(c + 1) * 2048],
                              "DX1", waits=scrhalf_guard[c % 2] + (spill_done if c < 2 else []))
                    av = p.emit("v", lambda v, o=xT[:, c * 2048:(c + 1) * 2048], i=hbuf:
                                v.tensor_tensor(o, o, i, ALU.add),
                                waits=[("DX1", dld)] + xg(), inc="V")
                    scrhalf_guard[c % 2] = [("V", av)]
                    xTk_guard[2 * c] = [("V", av)]
                    xTk_guard[2 * c + 1] = [("V", av)]
                xT_guard[0] = [("V", p.cnt["V"])]
                if upto == 1:
                    stage_out(g["y_out"], waits=xg())
                else:
                    moe()
                    if upto == 2:
                        stage_out(g["y_out"], waits=xg())
                    else:
                        attn_b()
                        if upto == 3:
                            stage_out(g["y_out"], waits=xg())
                        else:
                            moe()
                            stage_out(g["y_out"], waits=xg())

        p.finalize(E(nc.Block()))
    return nc


_CACHE = {}


def kernel(**inputs):
    upto = int(inputs.pop("_upto", 99))
    tap = inputs.pop("_tap", None)
    key = (upto, tap)
    if key not in _CACHE:
        _CACHE[key] = build_program(upto, tap)
    nc = _CACHE[key]
    x = np.ascontiguousarray(np.asarray(inputs["L0_attn_a_x"], dtype=np.float32).reshape(SEQ, H))
    cols = _host_cols(inputs)
    in_maps = []
    for c in range(NCORE):
        m = {"x_in": x[c * TOK:(c + 1) * TOK]}
        for k, v in inputs.items():
            if k == "L0_attn_a_x":
                continue
            m[k] = np.ascontiguousarray(np.asarray(v, dtype=np.float32))
        m.update(_host_consts(c))
        m.update(cols)
        in_maps.append(m)
    res = run_bass_kernel_spmd(nc, in_maps, core_ids=list(range(NCORE)))
    out = np.concatenate([r["y_out"] for r in res.results], axis=0)
    return out.reshape(1, SEQ, H)


# version 24
# speedup vs baseline: 1.1113x; 1.1065x over previous
import math
from contextlib import ExitStack

import numpy as np
import ml_dtypes
import concourse.bass as bass
import concourse.mybir as mybir
from concourse.bass_utils import run_bass_kernel_spmd

F32 = mybir.dt.float32
BF16 = mybir.dt.bfloat16
AF = mybir.ActivationFunctionType
ALU = mybir.AluOpType
AX = mybir.AxisListType

H = 2048
SEQ = 8192
NCORE = 8
TOK = SEQ // NCORE
KT = H // 128
NTT = TOK // 128
HEADS = 8
HD = 256
FF = 8192
NE = 16
KTOP = 512
EPS = 1e-5
C1G = 1.5957691216057308
C3G = 0.07135481627338621
DENOM_B = math.sqrt(2048.0) / 8192.0
BISECT_ITERS = 20

GAMMAS = (1.0 - np.exp(np.linspace(np.log(1.0 / 32), np.log(1.0 / 512), HEADS))).astype(np.float32)
NH = HEADS


class P:

    ENGS = ("t", "v", "s", "p", "y")

    def __init__(self, nc, ctx):
        self.nc = nc
        self.ctx = ctx
        self.ops = {e: [] for e in self.ENGS}
        self.sems = {}
        self.cnt = {}
        self.lastw = {e: {} for e in self.ENGS}
        self.bank_guard = [[] for _ in range(8)]

    def sem(self, name):
        self.sems[name] = self.ctx.enter_context(self.nc.semaphore(name))
        self.cnt[name] = 0
        return name

    def emit(self, e, fn, waits=(), inc=None, inc_by=1):
        w2 = []
        for sn, v in waits:
            if not v:
                continue
            if self.lastw[e].get(sn, 0) < v:
                self.lastw[e][sn] = v
                w2.append((sn, v))
        val = None
        if inc is not None:
            self.cnt[inc] += inc_by
            val = self.cnt[inc]
        sems = self.sems

        def gfun(eng, w2=tuple(w2), fn=fn, inc=inc, inc_by=inc_by):
            for sn, v in w2:
                eng.wait_ge(sems[sn], v)
            r = fn(eng)
            if inc is not None:
                r.then_inc(sems[inc], inc_by)

        self.ops[e].append(gfun)
        return val

    def finalize(self, block):
        for e, sec in (("y", block.sync), ("p", block.gpsimd), ("t", block.tensor),
                       ("v", block.vector), ("s", block.scalar)):
            ops = self.ops[e]

            def runner(eng, ops=ops):
                for f in ops:
                    f(eng)

            sec(runner)


def _host_consts(core):
    c = {}
    ident = np.eye(128, dtype=np.float32)
    c["c_idf"] = ident
    c["c_idb"] = ident.astype(ml_dtypes.bfloat16)
    a = np.arange(128)
    diff = a[None, :] - a[:, None]
    msk = (diff >= 0).astype(np.float32)
    c["c_mask01"] = msk.astype(ml_dtypes.bfloat16)
    ng = np.stack([g ** (-a.astype(np.float32)) for g in GAMMAS], axis=1)
    c["c_negj"] = ng.astype(np.float32)
    qs = np.stack([np.broadcast_to(g ** a.astype(np.float32), (128, 128)) for g in GAMMAS])
    c["c_qscale"] = np.ascontiguousarray(qs).astype(ml_dtypes.bfloat16)
    vs = np.stack([g ** (128.0 - a.astype(np.float32)) for g in GAMMAS], axis=1)
    c["c_vscale"] = vs.astype(np.float32)
    gd = np.zeros((128, HEADS * 8), dtype=np.float32)
    for h in range(HEADS):
        for r in range(8):
            if r < core:
                gd[:, h * 8 + r] = np.float32(GAMMAS[h]) ** np.float32(1024 * (core - r - 1))
    c["c_gdecay"] = gd
    cm = np.zeros((128, 8), dtype=np.float32)
    cm[:, :core] = 1.0
    c["c_cmask"] = cm
    ind_a = np.zeros((128, 16), dtype=np.float32)
    for p2 in range(128):
        ind_a[p2, p2 % 16] = 1.0
    c["c_inda"] = ind_a
    c["c_indb"] = np.ascontiguousarray(ind_a.T)
    a2 = np.arange(128)
    c["c_indf"] = (a2[:, None] % 16 == a2[None, :] % 16).astype(np.float32)
    sel = np.zeros((128, 1), dtype=np.float32)
    sel[16 * core:16 * core + 16, 0] = 1.0
    c["c_selw"] = sel
    return c


def _col_layout(v):
    v = np.asarray(v, dtype=np.float32).reshape(-1, 128)
    return np.ascontiguousarray(v.T)


def _host_cols(inputs):
    c = {}
    c["c_lncol"] = np.concatenate([_col_layout(inputs["L0_attn_a_ln_gamma"]),
                                   _col_layout(inputs["L0_attn_a_ln_beta"])], axis=1)
    b1 = np.asarray(inputs["L0_attn_a_b1"], dtype=np.float32)
    b2 = np.asarray(inputs["L0_attn_a_b2"], dtype=np.float32)
    be = np.asarray(inputs["L0_ffn_expert_b"], dtype=np.float32)
    c["c_bcol"] = np.concatenate([_col_layout(b1[0]), _col_layout(b1[1]),
                                  _col_layout(b2[0]), _col_layout(b2[1]),
                                  _col_layout(be)], axis=1)
    return c


def _const_specs():
    cc = _host_consts(0)
    specs = {k: (list(v.shape), BF16 if v.dtype == ml_dtypes.bfloat16 else F32)
             for k, v in cc.items()}
    specs["c_lncol"] = ([128, 2 * KT], F32)
    specs["c_bcol"] = ([128, 2 * 64 + 2 * KT + KT], F32)
    return specs


def build_program(upto=99, tap=None):
    nc = bass.Bass("TRN2", target_bir_lowering=False, debug=False)
    g = {}

    def dram_in(name, shape, dt=F32):
        g[name] = nc.dram_tensor(name, list(shape), dt, kind="ExternalInput").ap()

    dram_in("x_in", (TOK, H))
    for nm, shp in (
        ("L0_attn_a_Wq", (2, H, H)), ("L0_attn_a_Wk", (2, H, H)),
        ("L0_attn_a_Wv", (2, H, H)), ("L0_attn_a_Wg", (2, H, H)),
        ("L0_attn_a_Wo", (2, H, H)),
        ("L0_attn_a_gn_gamma", (2, H)), ("L0_attn_a_gn_beta", (2, H)),
        ("L0_attn_a_W1", (2, H, FF)), ("L0_attn_a_b1", (2, FF)),
        ("L0_attn_a_W2", (2, FF, H)), ("L0_attn_a_b2", (2, H)),
        ("L0_attn_a_ln_gamma", (H,)), ("L0_attn_a_ln_beta", (H,)),
        ("L0_attn_b_Wq", (H, H)), ("L0_attn_b_Wk", (H, H)),
        ("L0_attn_b_Wv", (H, H)), ("L0_attn_b_Wo", (H, H)),
        ("L0_ffn_router_w", (H, NE)), ("L0_ffn_router_b", (NE,)),
        ("L0_ffn_expert_w", (H, H)), ("L0_ffn_expert_b", (H,)),
    ):
        dram_in(nm, shp)
    for nm, (shp, dt) in _const_specs().items():
        dram_in(nm, shp, dt)

    g["y_out"] = nc.dram_tensor("y_out", [TOK, H], F32, kind="ExternalOutput").ap()
    g["spill_x"] = nc.dram_tensor("spill_x", [128, KT * TOK], F32).ap()
    g["cc_st_in"] = nc.dram_tensor("cc_st_in", [128, 512], BF16).ap()
    g["cc_st_out"] = nc.dram_tensor("cc_st_out", [128 * NCORE, 512], BF16).ap()
    g["cc_s_in"] = nc.dram_tensor("cc_s_in", [16, TOK], F32).ap()
    g["cc_s_out"] = nc.dram_tensor("cc_s_out", [16 * NCORE, TOK], F32).ap()
    g["cc_cs_in"] = nc.dram_tensor("cc_cs_in", [128, KT], F32).ap()
    g["cc_cs_out"] = nc.dram_tensor("cc_cs_out", [128 * NCORE, KT], F32).ap()

    ctx = ExitStack()
    E = ctx.enter_context
    with ctx:
        xT = E(nc.sbuf_tensor("xT", [128, KT * TOK], F32))
        xnT = E(nc.sbuf_tensor("xnT", [128, KT * TOK], BF16))
        big = E(nc.sbuf_tensor("big", [128, 16384], BF16))
        aux = E(nc.sbuf_tensor("aux", [128, 11264], BF16))
        wbuf = E(nc.sbuf_tensor("wbuf", [128, 2 * 4096], BF16))
        kT = E(nc.sbuf_tensor("kT", [128, 2 * TOK], BF16))
        vtk = E(nc.sbuf_tensor("vtk", [128, NTT * HD], BF16))
        gtk = E(nc.sbuf_tensor("gtk", [128, NTT * HD], BF16))
        scr = E(nc.sbuf_tensor("scr", [128, 4096], F32))
        rows = E(nc.sbuf_tensor("rows", [128, 1024], F32))
        cst = E(nc.sbuf_tensor("cst", [128, 224], F32))
        cstb = E(nc.sbuf_tensor("cstb", [128, 272], BF16))
        cidf = E(nc.sbuf_tensor("cidf", [128, 128], F32))
        cmask01 = E(nc.sbuf_tensor("cmask01", [128, 128], BF16))
        cqs = E(nc.sbuf_tensor("cqs", [128, 8 * 128], BF16))
        cinda = E(nc.sbuf_tensor("cinda", [128, 16], F32))
        cindb = E(nc.sbuf_tensor("cindb", [16, 128], F32))
        cindf = E(nc.sbuf_tensor("cindf", [128, 128], F32))
        lncol = E(nc.sbuf_tensor("lncol", [128, 2 * KT], F32))
        bcol = E(nc.sbuf_tensor("bcol", [128, 2 * 64 + 3 * KT], F32))
        gnga = E(nc.sbuf_tensor("gnga", [1, HD], BF16))
        gnbb = E(nc.sbuf_tensor("gnbb", [1, HD], BF16))
        rbrow = E(nc.sbuf_tensor("rbrow", [1, NE], F32))
        bis = E(nc.sbuf_tensor("bis", [16, 8], F32))

        psum = [E(nc.psum_tensor("pb%d" % i, [128, 512], F32)) for i in range(8)]

        zT = big
        Rpfx = aux[:, 0:4096]
        Rball = aux[:, 4096:6144]
        yhv = aux[:, 6144:8192]
        Rst = aux[:, 8192:10240].bitcast(F32)
        Rgcb = aux[:, 10240:10752]
        chz = aux[:, 10752:11264]
        h2 = big
        Sall = big[:, 0:2048].bitcast(F32)
        Smine = big[:, 2048:2304].bitcast(F32)
        SmT = big[0:16, 2304:4352].bitcast(F32)
        cmpb = big[:, 4352:6400].bitcast(F32)
        wBsb = big[:, 6400:8448].bitcast(F32)
        moescr = big[:, 8448:10496].bitcast(F32)
        kvh = big
        wTb = aux[:, 0:8192]
        abkv = aux[:, 8192:10240].bitcast(F32)

        p = P(nc, ctx)
        for snm in ("T", "V", "S", "DW0", "DW1", "DX0", "DX1", "DXC", "DXG", "DCU", "DCG"):
            p.sem(snm)
        NCC = 2 * HEADS + 3
        for i in range(NCC):
            p.sem("CC%d" % i)
        ccn = [0]
        ccbuf_guard = {}

        def mm(out, lhsT, rhs, start, stop, waits=(), inc=None):
            if stop:
                inc = "T"
            return p.emit("t", lambda t, o=out, l=lhsT, r=rhs, a=start, b=stop:
                          t.matmul(o, l, r, start=a, stop=b), waits=waits, inc=inc)

        def dma(queue, out, in_, sem, waits=()):
            return p.emit(queue, lambda q, o=out, i=in_: q.dma_start(out=o, in_=i),
                          waits=waits, inc=sem, inc_by=16)

        def vcopy(dst, src, waits=(), eng="v"):
            sem = "V" if eng == "v" else "S"
            return p.emit(eng, lambda e, o=dst, i=src:
                          (e.tensor_copy(o, i) if eng == "v" else e.copy(o, i)),
                          waits=waits, inc=sem)

        def tr128(dst, src, bank, ident_b, t_waits=(), v_waits=(), eng="v"):
            if ident_b:
                pv = psum[bank][:, 0:64].bitcast(BF16)
                idt = cstb[:, 0:128]
            else:
                pv = psum[bank][:, 0:128]
                idt = cidf[:]
            tv = p.emit("t", lambda t, o=pv, i=src: t.transpose(o, i, idt),
                        waits=list(t_waits) + p.bank_guard[bank], inc="T")
            sem = "V" if eng == "v" else "S"
            vv = p.emit(eng, lambda e, o=dst, i=pv:
                        (e.tensor_copy(o, i) if eng == "v" else e.copy(o, i)),
                        waits=[("T", tv)] + list(v_waits), inc=sem)
            p.bank_guard[bank] = [(sem, vv)]
            return vv

        wslot_guard = [[], []]

        def wload(slot, wap, r0, rk, c0, cw, layer=None):
            view = wbuf[:, slot * 4096: slot * 4096 + rk * cw]
            src = wap if layer is None else wap[layer]
            src = src[r0:r0 + rk * 128, c0:c0 + cw].rearrange("(k p) n -> p k n", p=128)
            step = max(1, 2048 // 128)
            d = None
            sd = "DW%d" % slot
            for ks in range(0, rk, step):
                ke = min(rk, ks + step)
                d = dma("p", view.rearrange("p (k n) -> p k n", n=cw)[:, ks:ke, :],
                        src[:, ks:ke, :], sd,
                        waits=wslot_guard[slot] if ks == 0 else ())
            return view, (sd, d)

        def wdone(slot):
            wslot_guard[slot] = [("T", p.cnt["T"])]

        dma("y", cidf[:], g["c_idf"], "DXC")
        dma("y", cstb[:, 0:128], g["c_idb"], "DXC")
        p.emit("v", lambda v: v.memset(cst[:, 0:1], 1.0), inc="V")
        p.emit("v", lambda v: v.memset(cst[:, 90:91], float(EPS)), inc="V")
        nc.const_aps.register(F32, float(EPS), cst[:, 90:91])
        p.emit("v", lambda v: v.memset(cst[0:1, 96:224], 1.0), inc="V")
        p.emit("v", lambda v: v.memset(cstb[:, 128:129], 1.0), inc="V")
        p.emit("v", lambda v: v.memset(cstb[0:1, 144:272], 1.0), inc="V")
        dma("y", cst[:, 1:9], g["c_vscale"], "DXC")
        dma("y", cst[:, 9:17], g["c_cmask"], "DXC")
        dma("y", cst[:, 17:81], g["c_gdecay"], "DXC")
        dma("y", cst[:, 81:82], g["c_selw"], "DXC")
        dma("y", cmask01[:], g["c_mask01"], "DXC")
        dma("y", cst[:, 82:90], g["c_negj"], "DXC")
        dma("y", cqs[:].rearrange("p (h i) -> p h i", h=8),
            g["c_qscale"].rearrange("h p i -> p h i"), "DXC")
        dma("y", cinda[:], g["c_inda"], "DXC")
        dma("y", cindb[:], g["c_indb"], "DXC")
        dma("y", cindf[:], g["c_indf"], "DXC")
        dma("y", lncol[:], g["c_lncol"], "DXC")
        dma("y", bcol[:], g["c_bcol"], "DXC")
        dma("y", rbrow[:], g["L0_ffn_router_b"][None, :], "DXC")
        CONST_DX = [("DXC", p.cnt["DXC"]), ("V", p.cnt["V"])]

        scrhalf_guard = [[], []]

        def stage_in(src_dram, add=False, extra=()):
            for tt in range(NTT):
                hbuf = scr[:, (tt % 2) * 2048:(tt % 2) * 2048 + 2048]
                sd = "DX%d" % (tt % 2)
                d = dma("y", hbuf, src_dram[tt * 128:(tt + 1) * 128, :], sd,
                        waits=scrhalf_guard[tt % 2] + list(extra))
                for k in range(KT):
                    b = 2 + (k % 2)
                    sl = slice(k * TOK + tt * 128, k * TOK + tt * 128 + 128)
                    if not add:
                        tr128(xT[:, sl], hbuf[:, k * 128:(k + 1) * 128], b, False,
                              t_waits=[(sd, d)] + CONST_DX)
                    else:
                        pv = psum[b][:, 0:128]
                        tv = p.emit("t", lambda t, o=pv, i=hbuf[:, k * 128:(k + 1) * 128]:
                                    t.transpose(o, i, cidf[:]),
                                    waits=[(sd, d)] + p.bank_guard[b], inc="T")
                        vv = p.emit("v", lambda v, o=xT[:, sl], i=pv:
                                    v.tensor_tensor(o, o, i, ALU.add),
                                    waits=[("T", tv)] + xg(), inc="V")
                        p.bank_guard[b] = [("V", vv)]
                scrhalf_guard[tt % 2] = [("T", p.cnt["T"])]
            xT_guard[0] = [("V", p.cnt["V"])]

        def stage_out(dst_dram, waits=()):
            for tt in range(NTT):
                hbuf = scr[:, (tt % 2) * 2048:(tt % 2) * 2048 + 2048]
                sd = "DX%d" % (tt % 2)
                last = None
                for k in range(KT):
                    b = 2 + (k % 2)
                    sl = slice(k * TOK + tt * 128, k * TOK + tt * 128 + 128)
                    last = tr128(hbuf[:, k * 128:(k + 1) * 128], xT[:, sl], b, False,
                                 t_waits=list(waits), v_waits=scrhalf_guard[tt % 2])
                d = dma("y", dst_dram[tt * 128:(tt + 1) * 128, :], hbuf, sd,
                        waits=[("V", last)])
                scrhalf_guard[tt % 2] = [(sd, d)]
            p.emit("y", lambda q: q.nop(),
                   waits=[("DX0", p.cnt["DX0"]), ("DX1", p.cnt["DX1"])])

        xT_guard = [[]]
        gn_guard = [[]]
        gath_guard = [[]]

        def xg():
            return list(xT_guard[0])

        bcopy_guard = [[], []]
        norm_guard = [[], []]

        def layernorm(dst_b16, src, in_waits, gamma_col=None, beta_col=None, dst_f32=None,
                      dst_waits=()):
            nv = ns = None
            for k in range(KT):
                bb = (vtk if k % 2 == 0 else kT)[:, 0:1024]
                sq = (vtk if k % 2 == 0 else kT)[:, 1024:2048]
                nv = p.emit("v", lambda v, o=bb, i=src[:, k * TOK:(k + 1) * TOK]:
                            v.tensor_copy(o, i),
                            waits=list(in_waits) + bcopy_guard[k % 2], inc="V")
                ns = p.emit("s", lambda s, o=sq, i=bb: s.activation(o, i, AF.Square),
                            waits=[("V", nv)] + bcopy_guard[k % 2], inc="S")
                for half in range(2):
                    mm(psum[half][0:1, :], cstb[:, 128:129], bb[:, half * 512:(half + 1) * 512],
                       start=(k == 0), stop=(k == KT - 1),
                       waits=[("V", nv)] + (p.bank_guard[half] if k == 0 else []),
                       inc="T")
                    mm(psum[2 + half][0:1, :], cstb[:, 128:129], sq[:, half * 512:(half + 1) * 512],
                       start=(k == 0), stop=(k == KT - 1),
                       waits=[("S", ns)] + (p.bank_guard[2 + half] if k == 0 else []),
                       inc="T")
                bcopy_guard[k % 2] = [("T", p.cnt["T"])]
            tdone = p.cnt["T"]
            bmm = None
            for half in range(2):
                vv = p.emit("v", lambda v, hh=half: v.tensor_scalar(
                    rows[0:1, 0:512], psum[hh][0:1, :], 1.0 / H, None, ALU.mult),
                    waits=[("T", tdone), ("T", p.cnt["T"])], inc="V")
                vv = p.emit("v", lambda v, hh=half: v.tensor_scalar(
                    rows[0:1, 512:1024], psum[2 + hh][0:1, :], 1.0 / H, None, ALU.mult),
                    waits=[("V", vv)], inc="V")
                bmm = mm(psum[4 + half][:, :], cst[0:1, 96:224], rows[0:1, 0:512],
                         start=True, stop=True,
                         waits=[("V", vv)] + p.bank_guard[4 + half])
                vv = p.emit("v", lambda v: v.tensor_tensor(
                    rows[0:1, 0:512], rows[0:1, 0:512], rows[0:1, 0:512], ALU.mult),
                    waits=[("V", vv), ("T", bmm)], inc="V")
                vv = p.emit("v", lambda v: v.tensor_tensor(
                    rows[0:1, 0:512], rows[0:1, 512:1024], rows[0:1, 0:512], ALU.subtract),
                    waits=[("V", vv)], inc="V")
                sv = p.emit("s", lambda s: s.activation(rows[0:1, 0:512], rows[0:1, 0:512],
                                                        AF.Ln, bias=float(EPS)),
                            waits=[("V", vv)], inc="S")
                sv = p.emit("s", lambda s: s.activation(rows[0:1, 0:512], rows[0:1, 0:512],
                                                        AF.Exp, scale=-0.5),
                            waits=[("S", sv)], inc="S")
                bmm = mm(psum[6 + half][:, :], cst[0:1, 96:224], rows[0:1, 0:512],
                         start=True, stop=True,
                         waits=[("S", sv)] + p.bank_guard[6 + half])
            vfin = p.cnt["V"]
            lasts = lastv = None
            half_done = {}
            hcnt = {}
            for half in range(2):
                for k in range(KT):
                    sl = slice(k * TOK + half * 512, k * TOK + (half + 1) * 512)
                    if dst_f32 is None:
                        t1 = scr[:, (k % 2) * 1024 + half * 512:(k % 2) * 1024 + half * 512 + 512]
                        exw = norm_guard[k % 2] + scrhalf_guard[0] + scrhalf_guard[1]
                    else:
                        t1 = dst_f32[:, sl]
                        exw = list(dst_waits)
                    lastv = p.emit("v", lambda v, o=t1, i=src[:, sl], mb=psum[4 + half][:, :]:
                                   v.tensor_tensor(o, i, mb, ALU.subtract),
                                   waits=[("T", bmm)] + exw, inc="V")
                    lastv = p.emit("v", lambda v, o=t1, rb=psum[6 + half][:, :]:
                                   v.tensor_tensor(o, o, rb, ALU.mult),
                                   waits=[("V", lastv)], inc="V")
                    if gamma_col is not None:
                        lasts = p.emit("s", lambda s, o=dst_b16[:, sl], i=t1,
                                       gc=gamma_col[:, k:k + 1], bc=beta_col[:, k:k + 1]:
                                       s.activation(o, i, AF.Identity, scale=gc, bias=bc),
                                       waits=[("V", lastv)], inc="S")
                    else:
                        lasts = p.emit("s", lambda s, o=dst_b16[:, sl], i=t1: s.copy(o, i),
                                       waits=[("V", lastv)], inc="S")
                    norm_guard[k % 2] = [("S", p.cnt["S"])]
                half_done[half] = [("S", p.cnt["S"]), ("V", p.cnt["V"])]
                hcnt[half] = (p.cnt["V"], p.cnt["S"])
            if dst_f32 is None:
                scrhalf_guard[0] = scrhalf_guard[0] + [("S", p.cnt["S"])]
                scrhalf_guard[1] = scrhalf_guard[1] + [("S", p.cnt["S"])]
            for b in range(4):
                p.bank_guard[b] = [("V", vfin)]
            for half in range(2):
                p.bank_guard[4 + half] = [("V", hcnt[half][0]), ("S", hcnt[half][1])]
                p.bank_guard[6 + half] = [("V", hcnt[half][0]), ("S", hcnt[half][1])]
            half_done["all"] = [("S", lasts), ("V", lastv)]
            return half_done

        def retention_layer(l):
            xnw = layernorm(xnT, xT, xg(), gamma_col=lncol[:, 0:KT], beta_col=lncol[:, KT:2 * KT])
            zT_guard = [[]]
            wq, dq = wload(0, g["L0_attn_a_Wq"], 0, KT, 0 * HD, HD, layer=l)
            pend = {"q": (wq, dq, 0)}
            fin = [None]
            for h in range(NH):
                retention_head(l, h, xnw, pend, zT_guard, fin)
            fin[0]()
            fin[0] = None
            wo_pass(l, zT_guard)
            ffn(l)

        yhv_guard = [[]]

        def retention_head(l, h, xnw, pend, zT_guard, fin):
            ccq = ccn[0]
            ccn[0] += 1
            gam128 = float(np.float32(GAMMAS[h]) ** 128)
            qT = big[:, (2 * h) * TOK:(2 * h + 2) * TOK]
            q2b = scr[:, 1024 + (h % 2) * 1024: 2048 + (h % 2) * 1024].bitcast(BF16)
            wq, dq, slq = pend["q"]
            wk, dk = wload(1 - slq, g["L0_attn_a_Wk"], 0, KT, h * HD, HD, layer=l)
            for dt in range(2):
                for half in range(2):
                    bk = dt * 2 + half
                    for k in range(KT):
                        mm(psum[bk][:, :], wq[:, k * HD + dt * 128: k * HD + dt * 128 + 128],
                           xnT[:, k * TOK + half * 512: k * TOK + (half + 1) * 512],
                           start=(k == 0), stop=(k == KT - 1),
                           waits=([dq] + list(xnw[half]) + p.bank_guard[bk]) if k == 0 else [])
                    vcopy(qT[:, dt * TOK + half * 512: dt * TOK + (half + 1) * 512],
                          psum[bk][:, :], waits=[("T", p.cnt["T"])], eng="s")
                    p.bank_guard[bk] = [("S", p.cnt["S"])]
            wdone(slq)
            wv, dv = wload(slq, g["L0_attn_a_Wv"], 0, KT, h * HD, HD, layer=l)
            for dt in range(2):
                for half in range(2):
                    bk = 4 + dt * 2 + half
                    for k in range(KT):
                        mm(psum[bk][:, :], wk[:, k * HD + dt * 128: k * HD + dt * 128 + 128],
                           xnT[:, k * TOK + half * 512: k * TOK + (half + 1) * 512],
                           start=(k == 0), stop=(k == KT - 1),
                           waits=([dk] + list(xnw[half]) + p.bank_guard[bk]) if k == 0 else [])
                    vcopy(kT[:, dt * TOK + half * 512: dt * TOK + (half + 1) * 512],
                          psum[bk][:, :], waits=[("T", p.cnt["T"])], eng="s")
                    p.bank_guard[bk] = [("S", p.cnt["S"])]
            wdone(1 - slq)
            wg_, dg = wload(1 - slq, g["L0_attn_a_Wg"], 0, KT, h * HD, HD, layer=l)
            if fin[0] is not None:
                fin[0]()
                fin[0] = None
            head_T0 = p.cnt["T"]
            rz = p.emit("v", lambda v: v.memset(Rst[:, 0:512], 0.0),
                        waits=[("V", p.cnt["V"]), ("S", p.cnt["S"])], inc="V")
            k_ready_S = p.cnt["S"]
            cons_T = [0, 0]
            for i in range(NTT):
                bk = i % 4
                for k in range(KT):
                    mm(psum[bk][:, 0:HD], xnT[:, k * TOK + i * 128: k * TOK + i * 128 + 128],
                       wv[:, k * HD:(k + 1) * HD],
                       start=(k == 0), stop=(k == KT - 1),
                       waits=([dv] + list(xnw[i // 4]) + p.bank_guard[bk]) if k == 0 else [])
                sv_ = vcopy(vtk[:, i * HD:(i + 1) * HD], psum[bk][:, 0:HD],
                            waits=[("T", p.cnt["T"])], eng="s")
                p.bank_guard[bk] = [("S", sv_)]
                bi = i % 2
                base = bi * 128
                ktok = rows[:, base:base + 128].bitcast(BF16)
                vp = rows[:, 512 + base:512 + base + 128].bitcast(BF16)
                pv = vcopy(Rpfx[:, i * 512:(i + 1) * 512], Rst[:, 0:512],
                           waits=[("V", p.cnt["V"]), ("T", head_T0)], eng="s")
                evtr = None
                for dt in range(2):
                    evtr = tr128(ktok[:, dt * 128:(dt + 1) * 128],
                                 kT[:, dt * TOK + i * 128: dt * TOK + i * 128 + 128],
                                 4 + bi, True,
                                 t_waits=[("S", k_ready_S)],
                                 v_waits=[("T", cons_T[bi])] if cons_T[bi] else [])
                vv2 = p.emit("v", lambda v, o=vp, i2=vtk[:, i * HD:(i + 1) * HD],
                             sc=cst[:, 1 + h:2 + h]:
                             v.tensor_scalar(o, i2, sc, None, ALU.mult),
                             waits=[("S", sv_), ("V", p.cnt["V"]),
                                    ("T", cons_T[bi])], inc="V")
                for dt in range(2):
                    mm(psum[6 + dt][:, 0:HD], ktok[:, dt * 128:(dt + 1) * 128], vp,
                       start=True, stop=True,
                       waits=[("V", vv2), ("V", evtr)] + p.bank_guard[6 + dt])
                tui = p.cnt["T"]
                cons_T[bi] = tui
                for dt in range(2):
                    uv = p.emit("v", lambda v, o=Rst[:, dt * 256:(dt + 1) * 256],
                                ps=psum[6 + dt][:, 0:HD], gm=gam128:
                                v.scalar_tensor_tensor(o, o, gm, ps, ALU.mult, ALU.add),
                                waits=[("T", tui), ("S", pv)], inc="V")
                    p.bank_guard[6 + dt] = [("V", uv)]
            pfx_S = p.cnt["S"]
            wdone(slq)
            if h + 1 < NH:
                wqn, dqn = wload(slq, g["L0_attn_a_Wq"], 0, KT, (h + 1) * HD, HD, layer=l)
                pend["q"] = (wqn, dqn, slq)
            bv = vcopy(Rgcb[:, 0:512], Rst[:, 0:512],
                       waits=[("V", p.cnt["V"])] + ccbuf_guard.get("st_gcb", []))
            dcc = dma("y", g["cc_st_in"], Rgcb[:, 0:512], "DCU",
                      waits=[("V", bv)] + ccbuf_guard.get("st_in", []))
            ccs = "CC%d" % ccq
            p.emit("p", lambda gp: gp.collective_compute(
                "AllGather", ALU.bypass, replica_groups=[list(range(NCORE))],
                ins=[g["cc_st_in"]], outs=[g["cc_st_out"]]),
                waits=[("DCU", dcc)] + ccbuf_guard.get("st_out", []), inc=ccs)
            ccbuf_guard["st_in"] = [(ccs, 1)]
            for tt in range(NTT):
                bk = tt % 4
                for k in range(KT):
                    mm(psum[bk][:, 0:HD], xnT[:, k * TOK + tt * 128: k * TOK + tt * 128 + 128],
                       wg_[:, k * HD:(k + 1) * HD],
                       start=(k == 0), stop=(k == KT - 1),
                       waits=([dg] + list(xnw[tt // 4]) + p.bank_guard[bk]) if k == 0 else [])
                sg = p.emit("s", lambda s, o=gtk[:, tt * HD:(tt + 1) * HD], i2=psum[bk][:, 0:HD]:
                            s.activation(o, i2, AF.Sigmoid),
                            waits=[("T", p.cnt["T"])] + yhv_guard[0], inc="S")
                gv = p.emit("v", lambda v, o=gtk[:, tt * HD:(tt + 1) * HD],
                            ps=psum[bk][:, 0:HD]:
                            v.tensor_tensor(o, o, ps, ALU.mult),
                            waits=[("S", sg)], inc="V")
                p.bank_guard[bk] = [("V", gv)]
            wdone(1 - slq)
            pend["slot_for_next"] = 1 - slq
            qsc = cqs[:, h * 128:(h + 1) * 128]
            consq_T = [0, 0]
            for i in range(NTT):
                bi = i % 2
                base = bi * 256
                q1 = rows[:, base:base + 128].bitcast(BF16)
                smT = rows[:, 576 + bi * 64:576 + bi * 64 + 64].bitcast(BF16)
                vq = None
                for dt in range(2):
                    vq = p.emit("v", lambda v, o=q1[:, dt * 128:(dt + 1) * 128],
                                i2=qT[:, dt * TOK + i * 128: dt * TOK + i * 128 + 128], qs=qsc:
                                v.tensor_tensor(o, i2, qs, ALU.mult),
                                waits=([("T", consq_T[bi])] if consq_T[bi] else []) +
                                      [("V", p.cnt["V"])], inc="V")
                bsc = 4 + bi * 2
                for dt in range(2):
                    mm(psum[bsc][:, 0:128], kT[:, dt * TOK + i * 128: dt * TOK + i * 128 + 128],
                       q1[:, dt * 128:(dt + 1) * 128],
                       start=(dt == 0), stop=(dt == 1),
                       waits=([("V", vq)] + p.bank_guard[bsc]) if dt == 0 else [])
                tsc = p.cnt["T"]
                for dt in range(2):
                    p.emit("v", lambda v, o=q2b[:, dt * TOK + i * 128: dt * TOK + i * 128 + 128],
                           i2=q1[:, dt * 128:(dt + 1) * 128],
                           gm=float(np.float32(GAMMAS[h]) ** (128 * i)):
                           v.tensor_scalar(o, i2, gm, None, ALU.mult),
                           waits=[("T", tsc), ("V", vq)], inc="V")
                sv2 = p.emit("v", lambda v, o=smT, ps=psum[bsc][:, 0:128],
                             nj=cst[:, 82 + h:83 + h], mk=cmask01[:]:
                             v.scalar_tensor_tensor(o, ps, nj, mk, ALU.mult, ALU.mult),
                             waits=[("T", tsc)], inc="V")
                p.bank_guard[bsc] = [("V", sv2)]
                bky = 5 + bi * 2
                mm(psum[bky][:, 0:HD], smT, vtk[:, i * HD:(i + 1) * HD],
                   start=True, stop=False,
                   waits=[("V", sv2), ("V", vq), ("S", pfx_S)] + p.bank_guard[bky])
                for dt in range(2):
                    mm(psum[bky][:, 0:HD], q1[:, dt * 128:(dt + 1) * 128],
                       Rpfx[:, i * 512 + dt * 256: i * 512 + (dt + 1) * 256],
                       start=False, stop=(dt == 1), waits=[])
                tyi = p.cnt["T"]
                yv = vcopy(yhv[:, i * HD:(i + 1) * HD], psum[bky][:, 0:HD],
                           waits=[("T", tyi)] + yhv_guard[0], eng="s")
                p.bank_guard[bky] = [("S", yv)]
                consq_T[bi] = tyi

            def stage2():
                Gc = Rst[:, 512:1024]
                gv = p.emit("v", lambda v: v.memset(Gc, 0.0), waits=[("V", p.cnt["V"])], inc="V")
                for halfq in range(2):
                    d2 = dma("y", Rball[:, 0:2048].rearrange("p (r n) -> p r n", r=4),
                             g["cc_st_out"].rearrange("(r p) n -> p r n", p=128)[:, halfq * 4:(halfq + 1) * 4, :],
                             "DCG", waits=[(ccs, 1), ("V", gv)] + gath_guard[0])
                    for rr in range(4):
                        r = halfq * 4 + rr
                        gv2 = p.emit("v", lambda v, o=Gc, i=Rball[:, rr * 512:(rr + 1) * 512],
                                     sc=cst[:, 17 + h * 8 + r:18 + h * 8 + r]:
                                     v.scalar_tensor_tensor(o, i, sc, o, ALU.mult, ALU.add),
                                     waits=[("DCG", d2), ("V", gv)], inc="V")
                        gv = gv2
                gath_guard[0] = [("V", gv)]
                ccbuf_guard["st_out"] = [("DCG", p.cnt["DCG"])]
                gcb = vcopy(Rgcb[:, 0:512], Gc, waits=[("V", gv), ("DCU", dcc)])
                q2done = p.cnt["V"]
                for i in range(NTT):
                    bi = i % 2
                    bky = 5 + bi * 2
                    for dt in range(2):
                        mm(psum[bky][:, 0:HD], q2b[:, dt * TOK + i * 128: dt * TOK + i * 128 + 128],
                           Rgcb[:, dt * 256:(dt + 1) * 256],
                           start=(dt == 0), stop=(dt == 1),
                           waits=([("V", gcb), ("V", q2done)] + p.bank_guard[bky]) if dt == 0 else [])
                    tyg = p.cnt["T"]
                    yg = p.emit("v", lambda v, o=yhv[:, i * HD:(i + 1) * HD],
                                ps=psum[bky][:, 0:HD]:
                                v.tensor_tensor(o, o, ps, ALU.add),
                                waits=[("T", tyg), ("S", p.cnt["S"])], inc="V")
                    p.bank_guard[bky] = [("V", yg)]
                ccbuf_guard["st_gcb"] = [("T", p.cnt["T"])]
                ydone = [("S", p.cnt["S"])]
                bn1 = None
                for cq in range(8):
                    bn1 = p.emit("v", lambda v, c2=cq: v.bn_stats(
                        rows[:, 640 + c2 * 6:646 + c2 * 6], yhv[:, c2 * 256:(c2 + 1) * 256]),
                        waits=ydone + [("V", p.cnt["V"])], inc="V")
                bn2 = None
                for cq in range(8):
                    bn2 = p.emit("v", lambda v, c2=cq: v.bn_aggr(
                        rows[:, 688 + c2 * 2:690 + c2 * 2], rows[:, 640 + c2 * 6:646 + c2 * 6]),
                        waits=[("V", bn1)], inc="V")
                sv = p.emit("s", lambda s: s.activation(rows[:, 704:712],
                            rows[:, 688:704].rearrange("p (c n) -> p c n", c=8)[:, :, 1:2],
                            AF.Ln, bias=float(EPS)), waits=[("V", bn2)], inc="S")
                sv = p.emit("s", lambda s: s.activation(rows[:, 704:712], rows[:, 704:712],
                                                        AF.Exp, scale=-0.5),
                            waits=[("S", sv)], inc="S")
                dgn = dma("p", gnga[:], g["L0_attn_a_gn_gamma"][l][None, h * HD:(h + 1) * HD],
                          "DXG", waits=gn_guard[0])
                dgn = dma("p", gnbb[:], g["L0_attn_a_gn_beta"][l][None, h * HD:(h + 1) * HD],
                          "DXG", waits=())
                gmm = mm(psum[4][:, 0:HD], cstb[0:1, 144:272], gnga[:], start=True, stop=True,
                         waits=[("DXG", dgn)] + p.bank_guard[4])
                gmm = mm(psum[5][:, 0:HD], cstb[0:1, 144:272], gnbb[:], start=True, stop=True,
                         waits=p.bank_guard[5])
                gn_guard[0] = [("T", p.cnt["T"])]
                lastz = None
                for i in range(NTT):
                    t1 = scr[:, 0:256]
                    zv = p.emit("v", lambda v, o=t1, i2=yhv[:, i * HD:(i + 1) * HD],
                                mc=rows[:, 688 + 2 * i:689 + 2 * i], rc=rows[:, 704 + i:705 + i]:
                                v.tensor_scalar(o, i2, mc, rc, ALU.subtract, op1=ALU.mult),
                                waits=[("S", sv), ("V", p.cnt["V"]), ("T", gmm)], inc="V")
                    zv = p.emit("v", lambda v, o=t1, gb=psum[4][:, 0:HD]:
                                v.tensor_tensor(o, o, gb, ALU.mult), waits=[("V", zv)], inc="V")
                    zv = p.emit("v", lambda v, o=t1, bb=psum[5][:, 0:HD]:
                                v.tensor_tensor(o, o, bb, ALU.add), waits=[("V", zv)], inc="V")
                    zv = p.emit("v", lambda v, o=chz[:, 0:256], i2=t1, gk=gtk[:, i * HD:(i + 1) * HD]:
                                v.tensor_tensor(o, i2, gk, ALU.mult), waits=[("V", zv)], inc="V")
                    for dt in range(2):
                        kz = h * 2 + dt
                        lastz = tr128(zT[:, kz * TOK + i * 128: kz * TOK + i * 128 + 128],
                                      chz[:, dt * 128:(dt + 1) * 128], 6 + dt, True,
                                      t_waits=[("V", zv)])
                p.bank_guard[4] = [("V", p.cnt["V"])]
                p.bank_guard[5] = [("V", p.cnt["V"])]
                zT_guard[0] = zT_guard[0] + [("V", p.cnt["V"])]
                yhv_guard[0] = [("V", p.cnt["V"])]

            fin[0] = stage2

        def wo_pass(l, zT_guard):
            slot = [0]
            wo, do = wload(0, g["L0_attn_a_Wo"], 0, KT, 0, HD, layer=l)
            for sl8 in range(8):
                won, don = (wload(1 - slot[0], g["L0_attn_a_Wo"], 0, KT, (sl8 + 1) * HD, HD, layer=l)
                            if sl8 + 1 < 8 else (None, None))
                for co in range(2):
                    ct = sl8 * 2 + co
                    for half in range(2):
                        bk = (ct * 2 + half) % 4
                        for k in range(KT):
                            mm(psum[bk][:, :], wo[:, k * HD + co * 128: k * HD + co * 128 + 128],
                               zT[:, k * TOK + half * 512: k * TOK + (half + 1) * 512],
                               start=(k == 0), stop=(k == KT - 1),
                               waits=([do] + zT_guard[0] + p.bank_guard[bk]) if k == 0 else [])
                        sl = slice(ct * TOK + half * 512, ct * TOK + (half + 1) * 512)
                        rv = p.emit("v", lambda v, o=xT[:, sl], ps=psum[bk][:, :]:
                                    v.tensor_tensor(o, o, ps, ALU.add),
                                    waits=[("T", p.cnt["T"])] + xg(), inc="V")
                        p.bank_guard[bk] = [("V", rv)]
                wdone(slot[0])
                if won is not None:
                    wo, do = won, don
                    slot[0] = 1 - slot[0]
            xT_guard[0] = [("V", p.cnt["V"])]

        def ffn(l):
            rnw = layernorm(xnT, xT, xg(), gamma_col=lncol[:, 0:KT], beta_col=lncol[:, KT:2 * KT])
            nffs = 4
            cur = [0]
            wv_, dv_ = wload(0, g["L0_attn_a_W1"], 0, KT, 0, HD, layer=l)
            for ffs in range(nffs):
                for s8 in range(8):
                    if s8 < 7:
                        nxt = wload(1 - cur[0], g["L0_attn_a_W1"], 0, KT,
                                    (ffs * 16 + (s8 + 1) * 2) * 128, HD, layer=l)
                    else:
                        nxt = wload(1 - cur[0], g["L0_attn_a_W2"], ffs * 2048, KT, 0, HD, layer=l)
                    for fi in range(2):
                        ft = ffs * 16 + s8 * 2 + fi
                        for half in range(2):
                            bk = (fi * 2 + half) % 4
                            for k in range(KT):
                                mm(psum[bk][:, :], wv_[:, k * HD + fi * 128: k * HD + fi * 128 + 128],
                                   xnT[:, k * TOK + half * 512: k * TOK + (half + 1) * 512],
                                   start=(k == 0), stop=(k == KT - 1),
                                   waits=([dv_] + list(rnw[half]) + p.bank_guard[bk]) if k == 0 else [])
                            gelu_evac(psum[bk], h2[:, (ft % 16) * TOK + half * 512:
                                                   (ft % 16) * TOK + (half + 1) * 512],
                                      bcol[:, l * 64 + ft: l * 64 + ft + 1], bk)
                    wdone(cur[0])
                    wv_, dv_ = nxt
                    cur[0] = 1 - cur[0]
                h2done = [("V", p.cnt["V"])]
                for cs in range(8):
                    if cs < 7:
                        nxt = wload(1 - cur[0], g["L0_attn_a_W2"], ffs * 2048, KT,
                                    (cs + 1) * HD, HD, layer=l)
                    elif ffs + 1 < nffs:
                        nxt = wload(1 - cur[0], g["L0_attn_a_W1"], 0, KT,
                                    (ffs + 1) * 2048, HD, layer=l)
                    else:
                        nxt = None
                    for co in range(2):
                        ct = cs * 2 + co
                        for half in range(2):
                            bk = (co * 2 + half) % 4 + 4
                            for k in range(KT):
                                mm(psum[bk][:, :], wv_[:, k * HD + co * 128: k * HD + co * 128 + 128],
                                   h2[:, k * TOK + half * 512: k * TOK + (half + 1) * 512],
                                   start=(k == 0), stop=(k == KT - 1),
                                   waits=([dv_] + h2done + p.bank_guard[bk]) if k == 0 else [])
                            sl = slice(ct * TOK + half * 512, ct * TOK + (half + 1) * 512)
                            if ffs == 0:
                                rv = p.emit("v", lambda v, o=xT[:, sl], ps=psum[bk][:, :],
                                            bc=bcol[:, 128 + l * KT + ct:129 + l * KT + ct]:
                                            v.scalar_tensor_tensor(o, ps, bc, o, ALU.add, ALU.add),
                                            waits=[("T", p.cnt["T"])] + xg(), inc="V")
                            else:
                                rv = p.emit("v", lambda v, o=xT[:, sl], ps=psum[bk][:, :]:
                                            v.tensor_tensor(o, o, ps, ALU.add),
                                            waits=[("T", p.cnt["T"])] + xg(), inc="V")
                            p.bank_guard[bk] = [("V", rv)]
                    wdone(cur[0])
                    if nxt is not None:
                        wv_, dv_ = nxt
                        cur[0] = 1 - cur[0]
            xT_guard[0] = [("V", p.cnt["V"])]
            sg2 = [("V", p.cnt["V"]), ("S", p.cnt["S"])]
            scrhalf_guard[0] = scrhalf_guard[0] + sg2
            scrhalf_guard[1] = scrhalf_guard[1] + sg2

        gelu_scr_guard = [[], [], [], []]

        def gelu_evac(ps, dst, b1c, bk):
            gi = bk
            xs = scr[:, gi * 1024: gi * 1024 + 512]
            t2 = scr[:, gi * 1024 + 512: gi * 1024 + 1024]
            sv = p.emit("s", lambda s, o=xs, i=ps[:, :], bc=b1c:
                        s.activation(o, i, AF.Identity, bias=bc),
                        waits=[("T", p.cnt["T"])] + gelu_scr_guard[gi] + scrhalf_guard[gi // 2], inc="S")
            sv2 = p.emit("s", lambda s, o=t2, i=xs: s.activation(o, i, AF.Square),
                         waits=[("S", sv)], inc="S")
            vv = p.emit("v", lambda v, o=t2: v.tensor_scalar(o, o, C3G, C1G, ALU.mult, op1=ALU.add),
                        waits=[("S", sv2)], inc="V")
            vv = p.emit("v", lambda v, o=t2, i=xs: v.tensor_tensor(o, o, i, ALU.mult),
                        waits=[("V", vv)], inc="V")
            sv3 = p.emit("s", lambda s, o=t2: s.activation(o, o, AF.Sigmoid),
                         waits=[("V", vv)], inc="S")
            vv = p.emit("v", lambda v, o=dst, a=xs, b=t2: v.tensor_tensor(o, a, b, ALU.mult),
                        waits=[("S", sv3)], inc="V")
            gelu_scr_guard[gi] = [("V", vv)]
            p.bank_guard[bk] = [("S", sv)]

        def moe():
            ccq = ccn[0]
            ccn[0] += 1
            moescr_guard = [[], []]
            xnw = layernorm(xnT, xT, xg())
            ln_T = p.cnt["T"]
            scratch = [big[:, 10496 + i * 1024: 11520 + i * 1024] for i in range(5)] \
                + [kT[:, 0:1024], kT[:, 1024:2048],
                   vtk[:, 0:1024], vtk[:, 1024:2048],
                   gtk[:, 0:1024], gtk[:, 1024:2048]]
            NPRE = len(scratch)
            we, de = wload(1, g["L0_ffn_expert_w"], 0, KT, 0, HD)
            res = {"wv": None}
            evS = {}

            def expert_mms():
                weL, deL = we, de
                cur = 1
                for cs in range(8):
                    for co in range(2):
                        ct = cs * 2 + co
                        for half in range(2):
                            bk = (co * 2 + half) % 4
                            for k in range(KT):
                                mm(psum[bk][:, :],
                                   weL[:, k * HD + co * 128: k * HD + co * 128 + 128],
                                   xnT[:, k * TOK + half * 512: k * TOK + (half + 1) * 512],
                                   start=(k == 0), stop=(k == KT - 1),
                                   waits=([deL] + list(xnw[half]) + p.bank_guard[bk])
                                         if k == 0 else [])
                                yield
                            if ct < NPRE:
                                xw = [("T", ln_T)] if ct >= 5 else []
                                ev = p.emit("s", lambda s,
                                            o=scratch[ct][:, half * 512:(half + 1) * 512],
                                            i=psum[bk][:, :],
                                            bc=bcol[:, 128 + 2 * KT + ct:129 + 2 * KT + ct]:
                                            s.activation(o, i, AF.Identity, bias=bc),
                                            waits=[("T", p.cnt["T"])] + xw, inc="S")
                                evS[ct] = ev
                                p.bank_guard[bk] = [("S", ev)]
                            else:
                                assert res["wv"] is not None
                                sl = slice(ct * TOK + half * 512, ct * TOK + (half + 1) * 512)
                                ms = moescr[:, half * 512:(half + 1) * 512]
                                ev = p.emit("s", lambda s, o=ms, i=psum[bk][:, :],
                                            bc=bcol[:, 128 + 2 * KT + ct:129 + 2 * KT + ct]:
                                            s.activation(o, i, AF.Identity, bias=bc),
                                            waits=[("T", p.cnt["T"])] + moescr_guard[half],
                                            inc="S")
                                ev2 = p.emit("v", lambda v, o=ms,
                                             wb=wBsb[:, half * 512:(half + 1) * 512]:
                                             v.tensor_tensor(o, o, wb, ALU.mult),
                                             waits=[("S", ev), ("V", res["wv"])], inc="V")
                                ev2 = p.emit("v", lambda v, o=xT[:, sl], i=ms:
                                             v.tensor_tensor(o, o, i, ALU.add),
                                             waits=[("V", ev2)] + xg(), inc="V")
                                moescr_guard[half] = [("V", ev2)]
                                p.bank_guard[bk] = [("S", ev), ("V", ev2)]
                    nxt = (wload(1 - cur, g["L0_ffn_expert_w"], 0, KT, (cs + 1) * HD, HD)
                           if cs < 7 else None)
                    wdone(cur)
                    if nxt is not None:
                        weL, deL = nxt
                        cur = 1 - cur

            gen = expert_mms()

            def pump(n):
                for _ in range(n):
                    if next(gen, "END") == "END":
                        break

            rw = wbuf[:, 0:256]
            drw = None
            src = g["L0_ffn_router_w"].rearrange("(k p) n -> p k n", p=128)
            drw = dma("p", rw.rearrange("p (k n) -> p k n", n=16), src, "DW0",
                      waits=wslot_guard[0])
            drw = ("DW0", drw)
            smw = None
            for tt in range(NTT):
                bk = 4 + tt % 2
                for k in range(KT):
                    mm(psum[bk][:, 0:16], xnT[:, k * TOK + tt * 128: k * TOK + tt * 128 + 128],
                       rw[:, k * 16:(k + 1) * 16],
                       start=(k == 0), stop=False,
                       waits=([drw] + list(xnw[tt // 4]) + p.bank_guard[bk]) if k == 0 else [])
                mm(psum[bk][:, 0:16], cst[0:1, 96:224], rbrow[:],
                   start=False, stop=True, waits=CONST_DX, inc="T")
                tl = p.cnt["T"]
                blk = Smine[:, tt * 16:(tt + 1) * 16]
                nm = rows[:, 0:1]
                vv = p.emit("v", lambda v, o=nm, i=psum[bk][:, 0:16]:
                            v.reduce_max(o, i, axis=AX.X, negate=True),
                            waits=[("T", tl), ("V", p.cnt["V"])], inc="V")
                sv = p.emit("s", lambda s, o=blk, i=psum[bk][:, 0:16], b=nm,
                            ac=rows[:, 1:2]:
                            s.activation(o, i, AF.Exp, bias=b, accum_out=ac),
                            waits=[("V", vv), ("S", p.cnt["S"])], inc="S")
                vv = p.emit("v", lambda v, o=rows[:, 2:3], i=rows[:, 1:2]:
                            v.reciprocal(o, i), waits=[("S", sv)], inc="V")
                smw = p.emit("v", lambda v, o=blk, sc=rows[:, 2:3]:
                             v.tensor_scalar(o, o, sc, None, ALU.mult),
                             waits=[("V", vv)], inc="V")
                p.bank_guard[bk] = [("S", sv)]
                bkt = 2 + bk
                pvt = psum[bkt][0:16, 0:128]
                tvt = p.emit("t", lambda t, o=pvt, i=blk: t.transpose(o, i, cidf[:]),
                             waits=[("V", smw)] + p.bank_guard[bkt], inc="T")
                svt = p.emit("v", lambda v, o=SmT[:, tt * 128:(tt + 1) * 128], i=pvt:
                             v.tensor_copy(o, i), waits=[("T", tvt)], inc="V")
                p.bank_guard[bkt] = [("V", svt)]
                pump(4)
            wdone(0)
            stw = p.cnt["V"]
            dsm = dma("y", g["cc_s_in"], SmT, "DCU", waits=[("V", stw)] + ccbuf_guard.get("s", []))
            ccs = "CC%d" % ccq
            p.emit("p", lambda gp: gp.collective_compute(
                "AllGather", ALU.bypass, replica_groups=[list(range(NCORE))],
                ins=[g["cc_s_in"]], outs=[g["cc_s_out"]]),
                waits=[("DCU", dsm)], inc=ccs)
            d2 = dma("y", Sall, g["cc_s_out"], "DCG", waits=[(ccs, 1)])
            ccbuf_guard["s"] = [(ccs, 1), ("DCG", d2)]
            taucol = rows[:, 3:4]
            csum = rows[:, 4:5]
            locol = rows[:, 5:6]
            selcol = rows[:, 6:7]
            midcol = rows[:, 7:8]
            last = {"v": p.cnt["V"], "s": p.cnt["S"]}

            def _stepA(it):
                a = p.emit("v", lambda v, c=0.5 ** (it + 1): v.tensor_scalar(
                    midcol, locol, c, None, ALU.add),
                    waits=[("V", last["v"])], inc="V")
                a = p.emit("v", lambda v: v.tensor_scalar(cmpb[:], Sall, midcol, 0.0,
                                                          ALU.is_ge, op1=ALU.add,
                                                          accum_out=csum),
                           waits=[("V", a)] + ([("DCG", d2)] if it == 0 else []), inc="V")
                last["v"] = a

            def _stepB(it):
                tb = mm(psum[4][:, 0:1], cindf[:], csum, start=True, stop=True,
                        waits=[("V", last["v"])] + p.bank_guard[4]
                              + (CONST_DX if it == 0 else []),
                        inc="T")
                a = p.emit("v", lambda v: v.tensor_scalar(selcol, psum[4][:, 0:1],
                                                          float(KTOP), None, ALU.is_gt),
                           waits=[("T", tb), ("V", last["v"])], inc="V")
                p.bank_guard[4] = [("V", a)]
                a = p.emit("v", lambda v, c=0.5 ** (it + 1): v.tensor_scalar(
                    selcol, selcol, c, None, ALU.mult),
                    waits=[("V", a)], inc="V")
                a = p.emit("v", lambda v: v.tensor_tensor(locol, locol, selcol, ALU.add),
                           waits=[("V", a)], inc="V")
                last["v"] = a

            def _stepF():
                a = p.emit("v", lambda v, c=0.5 ** BISECT_ITERS: v.tensor_scalar(
                    taucol, locol, c, None, ALU.add),
                    waits=[("V", last["v"])], inc="V")
                a = p.emit("v", lambda v: v.tensor_scalar(cmpb[:], Sall, taucol, None,
                                                          ALU.is_ge),
                           waits=[("V", a)], inc="V")
                a = p.emit("v", lambda v: v.tensor_tensor(cmpb[:], cmpb[:], Sall, ALU.mult),
                           waits=[("V", a)], inc="V")
                last["v"] = a

            def _stepW(half):
                wmm = mm(psum[4][0:1, :], cst[:, 81:82], cmpb[:, half * 512:(half + 1) * 512],
                         start=True, stop=True,
                         waits=[("V", last["v"])] + p.bank_guard[4], inc="T")
                a = p.emit("v", lambda v, hh=half: v.tensor_copy(
                    rows[0:1, hh * 512:(hh + 1) * 512], psum[4][0:1, :]),
                    waits=[("T", wmm)], inc="V")
                p.bank_guard[4] = [("V", a)]
                wm2 = mm(psum[5 + half][:, :], cst[0:1, 96:224],
                         rows[0:1, half * 512:(half + 1) * 512],
                         start=True, stop=True, waits=[("V", a)] + p.bank_guard[5 + half],
                         inc="T")
                a = p.emit("v", lambda v, hh=half: v.tensor_copy(
                    wBsb[:, hh * 512:(hh + 1) * 512], psum[5 + hh][:, :]),
                    waits=[("T", wm2)], inc="V")
                p.bank_guard[5 + half] = [("V", a)]
                last["v"] = a
                if half == 1:
                    res["wv"] = a

            def _memsets():
                a = p.emit("v", lambda v: v.memset(locol, 0.0),
                           waits=[("V", last["v"])], inc="V")
                last["v"] = a
                _stepA(0)

            steps = [(4, _memsets), (110, lambda: _stepB(0))]
            for it in range(1, BISECT_ITERS):
                steps.append((3, lambda it=it: _stepA(it)))
                steps.append((3, lambda it=it: _stepB(it)))
            steps.append((3, _stepF))
            steps.append((3, lambda: _stepW(0)))
            steps.append((3, lambda: _stepW(1)))

            for gap, fn in steps:
                pump(gap)
                fn()
            pump(1 << 30)
            for ti in range(NPRE):
                for half in range(2):
                    tmp = cmpb[:, half * 512:(half + 1) * 512]
                    sl = slice(ti * TOK + half * 512, ti * TOK + (half + 1) * 512)
                    c1 = p.emit("v", lambda v, o=tmp,
                                a=scratch[ti][:, half * 512:(half + 1) * 512],
                                wb=wBsb[:, half * 512:(half + 1) * 512]:
                                v.tensor_tensor(o, a, wb, ALU.mult),
                                waits=[("S", evS[ti]), ("V", res["wv"])], inc="V")
                    p.emit("v", lambda v, o=xT[:, sl], i=tmp:
                           v.tensor_tensor(o, o, i, ALU.add),
                           waits=[("V", c1)] + xg(), inc="V")
            xT_guard[0] = [("V", p.cnt["V"])]

        def attn_b():
            ccq = ccn[0]
            ccn[0] += 1
            wsl_guard = [[], []]
            wTb_guard = [0, 0]
            totals_done = []
            xnw = layernorm(xnT, xT, xg())
            wslA = [[[], []], [[], []]]

            def stg(wi, par):
                return scr[:, wi * 2048 + par * 1024: wi * 2048 + par * 1024 + 1024].bitcast(BF16)

            for et in range(KT):
                par = et % 2
                for wi, wnm in ((0, "L0_attn_b_Wk"), (1, "L0_attn_b_Wv")):
                    if True:
                        bufb = stg(wi, par)
                        sd = "DW%d" % wi
                        d = dma("p", bufb, g[wnm][et * 128:(et + 1) * 128, :], sd,
                                waits=wslA[wi][par] + (scrhalf_guard[0] + scrhalf_guard[1] if et < 2 else []))
                        for k in range(KT):
                            tr128(wTb[:, wi * 2048 + k * 128: wi * 2048 + (k + 1) * 128],
                                  bufb[:, k * 128:(k + 1) * 128], 2 + (k % 2), True,
                                  t_waits=[(sd, d)],
                                  v_waits=[("T", wTb_guard[wi])] if wTb_guard[wi] else [])
                        wslA[wi][par] = [("T", p.cnt["T"])]
                    wtr = p.cnt["V"]
                    wtr_s = p.cnt["S"]
                    for half in range(2):
                        bk = 4 + wi * 2 + half
                        for k in range(KT):
                            mm(psum[bk][:, :],
                               wTb[:, wi * 2048 + k * 128: wi * 2048 + (k + 1) * 128],
                               xnT[:, k * TOK + half * 512: k * TOK + (half + 1) * 512],
                               start=(k == 0), stop=(k == KT - 1),
                               waits=([("V", wtr)] + list(xnw[half]) + p.bank_guard[bk]) if k == 0 else [])
                        if wi == 0:
                            ev = vcopy(kT[:, half * 512:(half + 1) * 512], psum[bk][:, :],
                                       waits=[("T", p.cnt["T"])], eng="s")
                            p.bank_guard[bk] = [("S", ev)]
                        else:
                            ev = vcopy(vtk[:, half * 512:(half + 1) * 512], psum[bk][:, :],
                                       waits=[("T", p.cnt["T"]), ("V", p.cnt["V"])], eng="s")
                            p.bank_guard[bk] = [("S", ev)]
                            p.emit("v", lambda v, o=abkv[:, half * 512:(half + 1) * 512],
                                   vv_=vtk[:, half * 512:(half + 1) * 512],
                                   kk=kT[:, half * 512:(half + 1) * 512]:
                                   v.tensor_tensor(o, vv_, kk, ALU.mult),
                                   waits=[("S", ev), ("V", p.cnt["V"])], inc="V")
                wTb_guard[0] = p.cnt["T"]
                wTb_guard[1] = p.cnt["T"]
                kvw = p.cnt["V"]
                tv = p.emit("v", lambda v, o=rows[:, et:et + 1], i=abkv[:, 0:1024]:
                            v.tensor_reduce(o, i, AX.X, ALU.add),
                            waits=[("V", kvw)], inc="V")
                sc = p.emit("v", lambda v, o=kvh[:, et * TOK:(et + 1) * TOK], i=abkv[:, 0:1024]:
                            v.tensor_tensor_scan(o, i, i, 0.0, ALU.add, ALU.bypass),
                            waits=[("V", kvw)], inc="V")
                totals_done = [("V", p.cnt["V"])]
            dcc = dma("y", g["cc_cs_in"], rows[:, 0:KT], "DCU", waits=totals_done)
            ccs = "CC%d" % ccq
            dq0 = [None, None]
            for et0 in range(2):
                dq0[et0] = dma("p", stg(0, et0), g["L0_attn_b_Wq"][et0 * 128:(et0 + 1) * 128, :],
                               "DW0", waits=wslA[0][et0])
            p.emit("p", lambda gp: gp.collective_compute(
                "AllGather", ALU.bypass, replica_groups=[list(range(NCORE))],
                ins=[g["cc_cs_in"]], outs=[g["cc_cs_out"]]),
                waits=[("DCU", dcc)], inc=ccs)
            d2 = dma("y", rows[:, 32:160].rearrange("p (r n) -> p r n", r=8),
                     g["cc_cs_out"].rearrange("(r p) n -> p r n", p=128), "DCG",
                     waits=[(ccs, 1)])
            cv = None
            for et in range(KT):
                par = et % 2
                bufq = stg(0, par)
                d = (dq0[et] if et < 2 else
                     dma("p", bufq, g["L0_attn_b_Wq"][et * 128:(et + 1) * 128, :], "DW0",
                         waits=wslA[0][par]))
                for k in range(KT):
                    tr128(wTb[:, k * 128:(k + 1) * 128], bufq[:, k * 128:(k + 1) * 128],
                          2 + (k % 2), True, t_waits=[("DW0", d)])
                wslA[0][par] = [("T", p.cnt["T"])]
                wtr = p.cnt["V"]
                if et == 0:
                    cv = p.emit("v", lambda v: v.memset(rows[:, 16:32], 0.0),
                                waits=[("DCG", d2)], inc="V")
                    for r in range(NCORE):
                        cv = p.emit("v", lambda v, rr=r: v.scalar_tensor_tensor(
                            rows[:, 16:32], rows[:, 32 + rr * 16:48 + rr * 16],
                            cst[:, 9 + rr:10 + rr],
                            rows[:, 16:32], ALU.mult, ALU.add),
                            waits=[("V", cv)], inc="V")
                for half in range(2):
                    bk = 4 + half
                    for k in range(KT):
                        mm(psum[bk][:, :], wTb[:, k * 128:(k + 1) * 128],
                           xnT[:, k * TOK + half * 512: k * TOK + (half + 1) * 512],
                           start=(k == 0), stop=(k == KT - 1),
                           waits=([("V", wtr)] + p.bank_guard[bk]) if k == 0 else [])
                    ev = vcopy(kT[:, half * 512:(half + 1) * 512], psum[bk][:, :],
                               waits=[("T", p.cnt["T"])], eng="s")
                    p.bank_guard[bk] = [("S", ev)]
                qw = p.cnt["S"]
                tv = p.emit("v", lambda v, o=abkv[:, 0:1024], i=kvh[:, et * TOK:(et + 1) * TOK],
                            cc=rows[:, 16 + et:17 + et]:
                            v.tensor_scalar(o, i, cc, DENOM_B, ALU.add, op1=ALU.mult),
                            waits=[("V", cv), ("V", p.cnt["V"]), ("S", qw)], inc="V")
                tv = p.emit("v", lambda v, o=kvh[:, et * TOK:(et + 1) * TOK],
                            a=abkv[:, 0:1024], b=kT[:, 0:1024]:
                            v.tensor_tensor(o, a, b, ALU.mult),
                            waits=[("V", tv)], inc="V")
            vhat_done = [("V", p.cnt["V"])]
            for ft in range(KT):
                par = ft % 2
                bufo = stg(0, par)
                d = dma("p", bufo, g["L0_attn_b_Wo"][ft * 128:(ft + 1) * 128, :], "DW0",
                        waits=wslA[0][par])
                for k in range(KT):
                    tr128(wTb[:, k * 128:(k + 1) * 128], bufo[:, k * 128:(k + 1) * 128],
                          2 + (k % 2), True, t_waits=[("DW0", d)])
                wslA[0][par] = [("T", p.cnt["T"])]
                wtr = p.cnt["V"]
                for half in range(2):
                    bk = 4 + half
                    for k in range(KT):
                        mm(psum[bk][:, :], wTb[:, k * 128:(k + 1) * 128],
                           kvh[:, k * TOK + half * 512: k * TOK + (half + 1) * 512],
                           start=(k == 0), stop=(k == KT - 1),
                           waits=([("V", wtr)] + vhat_done + p.bank_guard[bk]) if k == 0 else [])
                    sl = slice(ft * TOK + half * 512, ft * TOK + (half + 1) * 512)
                    rv = p.emit("v", lambda v, o=xT[:, sl], ps=psum[bk][:, :]:
                                v.tensor_tensor(o, o, ps, ALU.add),
                                waits=[("T", p.cnt["T"])] + xg(), inc="V")
                    p.bank_guard[bk] = [("V", rv)]
            xT_guard[0] = [("V", p.cnt["V"])]

        stage_in(g["x_in"])

        if upto == 0:
            stage_out(g["y_out"], waits=xg())
        elif tap == "ln_affine":
            w = layernorm(xnT, xT, xg(), gamma_col=lncol[:, 0:KT], beta_col=lncol[:, KT:2 * KT])["all"]
            for k in range(KT):
                vcopy(xT[:, k * TOK:(k + 1) * TOK], xnT[:, k * TOK:(k + 1) * TOK], waits=w)
            stage_out(g["y_out"], waits=[("V", p.cnt["V"])])
        elif tap == "ln_plain":
            w = layernorm(xnT, xT, xg())["all"]
            for k in range(KT):
                vcopy(xT[:, k * TOK:(k + 1) * TOK], xnT[:, k * TOK:(k + 1) * TOK], waits=w)
            stage_out(g["y_out"], waits=[("V", p.cnt["V"])])
        else:
            for c in range(4):
                dma("y", g["spill_x"][:, c * 4096:(c + 1) * 4096],
                    xT[:, c * 4096:(c + 1) * 4096], "DX0",
                    waits=xg() if c == 0 else ())
            spill_done = [("DX0", p.cnt["DX0"])]
            lnw = layernorm(xnT, xT, xg(), dst_f32=xT, dst_waits=spill_done)
            xT_guard[0] = lnw["all"]
            if tap == "msr0":
                xnw2 = layernorm(xnT, xT, xg(), gamma_col=lncol[:, 0:KT],
                                 beta_col=lncol[:, KT:2 * KT])
                wq, dq = wload(0, g["L0_attn_a_Wq"], 0, KT, 0, HD, layer=0)
                pend = {"q": (wq, dq, 0)}
                ztg = [[]]
                if NH < HEADS:
                    p.emit("v", lambda v: v.memset(big[:], 0.0), inc="V")
                fin2 = [None]
                for h in range(NH):
                    retention_head(0, h, xnw2, pend, ztg, fin2)
                fin2[0]()
                zv0 = p.emit("v", lambda v: v.memset(xT[:], 0.0),
                             waits=[("V", p.cnt["V"]), ("S", p.cnt["S"]),
                                    ("T", p.cnt["T"])], inc="V")
                xT_guard[0] = [("V", zv0)]
                wo_pass(0, ztg)
                stage_out(g["y_out"], waits=xg())
            else:
                for l in range(2):
                    retention_layer(l)
                for c in range(8):
                    hbuf = scr[:, (c % 2) * 2048:(c % 2) * 2048 + 2048]
                    dld = dma("y", hbuf, g["spill_x"][:, c * 2048:(c + 1) * 2048],
                              "DX1", waits=scrhalf_guard[c % 2] + (spill_done if c < 2 else []))
                    av = p.emit("v", lambda v, o=xT[:, c * 2048:(c + 1) * 2048], i=hbuf:
                                v.tensor_tensor(o, o, i, ALU.add),
                                waits=[("DX1", dld)] + xg(), inc="V")
                    scrhalf_guard[c % 2] = [("V", av)]
                xT_guard[0] = [("V", p.cnt["V"])]
                if upto == 1:
                    stage_out(g["y_out"], waits=xg())
                else:
                    moe()
                    if upto == 2:
                        stage_out(g["y_out"], waits=xg())
                    else:
                        attn_b()
                        if upto == 3:
                            stage_out(g["y_out"], waits=xg())
                        else:
                            moe()
                            stage_out(g["y_out"], waits=xg())

        p.finalize(E(nc.Block()))
    return nc


_CACHE = {}


def kernel(**inputs):
    upto = int(inputs.pop("_upto", 99))
    tap = inputs.pop("_tap", None)
    key = (upto, tap)
    if key not in _CACHE:
        _CACHE[key] = build_program(upto, tap)
    nc = _CACHE[key]
    x = np.ascontiguousarray(np.asarray(inputs["L0_attn_a_x"], dtype=np.float32).reshape(SEQ, H))
    cols = _host_cols(inputs)
    in_maps = []
    for c in range(NCORE):
        m = {"x_in": x[c * TOK:(c + 1) * TOK]}
        for k, v in inputs.items():
            if k == "L0_attn_a_x":
                continue
            m[k] = np.ascontiguousarray(np.asarray(v, dtype=np.float32))
        m.update(_host_consts(c))
        m.update(cols)
        in_maps.append(m)
    res = run_bass_kernel_spmd(nc, in_maps, core_ids=list(range(NCORE)))
    out = np.concatenate([r["y_out"] for r in res.results], axis=0)
    return out.reshape(1, SEQ, H)


# version 25
# speedup vs baseline: 1.1237x; 1.0111x over previous
import math
from contextlib import ExitStack

import numpy as np
import ml_dtypes
import concourse.bass as bass
import concourse.mybir as mybir
from concourse.bass_utils import run_bass_kernel_spmd

F32 = mybir.dt.float32
BF16 = mybir.dt.bfloat16
AF = mybir.ActivationFunctionType
ALU = mybir.AluOpType
AX = mybir.AxisListType

H = 2048
SEQ = 8192
NCORE = 8
TOK = SEQ // NCORE
KT = H // 128
NTT = TOK // 128
HEADS = 8
HD = 256
FF = 8192
NE = 16
KTOP = 512
EPS = 1e-5
C1G = 1.5957691216057308
C3G = 0.07135481627338621
DENOM_B = math.sqrt(2048.0) / 8192.0
BISECT_ITERS = 20

GAMMAS = (1.0 - np.exp(np.linspace(np.log(1.0 / 32), np.log(1.0 / 512), HEADS))).astype(np.float32)
NH = HEADS


class P:

    ENGS = ("t", "v", "s", "p", "y")

    def __init__(self, nc, ctx):
        self.nc = nc
        self.ctx = ctx
        self.ops = {e: [] for e in self.ENGS}
        self.sems = {}
        self.cnt = {}
        self.lastw = {e: {} for e in self.ENGS}
        self.bank_guard = [[] for _ in range(8)]

    def sem(self, name):
        self.sems[name] = self.ctx.enter_context(self.nc.semaphore(name))
        self.cnt[name] = 0
        return name

    def emit(self, e, fn, waits=(), inc=None, inc_by=1):
        w2 = []
        for sn, v in waits:
            if not v:
                continue
            if self.lastw[e].get(sn, 0) < v:
                self.lastw[e][sn] = v
                w2.append((sn, v))
        val = None
        if inc is not None:
            self.cnt[inc] += inc_by
            val = self.cnt[inc]
        sems = self.sems

        def gfun(eng, w2=tuple(w2), fn=fn, inc=inc, inc_by=inc_by):
            for sn, v in w2:
                eng.wait_ge(sems[sn], v)
            r = fn(eng)
            if inc is not None:
                r.then_inc(sems[inc], inc_by)

        self.ops[e].append(gfun)
        return val

    def finalize(self, block):
        for e, sec in (("y", block.sync), ("p", block.gpsimd), ("t", block.tensor),
                       ("v", block.vector), ("s", block.scalar)):
            ops = self.ops[e]

            def runner(eng, ops=ops):
                for f in ops:
                    f(eng)

            sec(runner)


def _host_consts(core):
    c = {}
    ident = np.eye(128, dtype=np.float32)
    c["c_idf"] = ident
    c["c_idb"] = ident.astype(ml_dtypes.bfloat16)
    a = np.arange(128)
    diff = a[None, :] - a[:, None]
    msk = (diff >= 0).astype(np.float32)
    c["c_mask01"] = msk.astype(ml_dtypes.bfloat16)
    ng = np.stack([g ** (-a.astype(np.float32)) for g in GAMMAS], axis=1)
    c["c_negj"] = ng.astype(np.float32)
    qs = np.stack([np.broadcast_to(g ** a.astype(np.float32), (128, 128)) for g in GAMMAS])
    c["c_qscale"] = np.ascontiguousarray(qs).astype(ml_dtypes.bfloat16)
    vs = np.stack([g ** (128.0 - a.astype(np.float32)) for g in GAMMAS], axis=1)
    c["c_vscale"] = vs.astype(np.float32)
    gd = np.zeros((128, HEADS * 8), dtype=np.float32)
    for h in range(HEADS):
        for r in range(8):
            if r < core:
                gd[:, h * 8 + r] = np.float32(GAMMAS[h]) ** np.float32(1024 * (core - r - 1))
    c["c_gdecay"] = gd
    cm = np.zeros((128, 8), dtype=np.float32)
    cm[:, :core] = 1.0
    c["c_cmask"] = cm
    ind_a = np.zeros((128, 16), dtype=np.float32)
    for p2 in range(128):
        ind_a[p2, p2 % 16] = 1.0
    c["c_inda"] = ind_a
    c["c_indb"] = np.ascontiguousarray(ind_a.T)
    a2 = np.arange(128)
    c["c_indf"] = (a2[:, None] % 16 == a2[None, :] % 16).astype(np.float32)
    sel = np.zeros((128, 1), dtype=np.float32)
    sel[16 * core:16 * core + 16, 0] = 1.0
    c["c_selw"] = sel
    return c


def _col_layout(v):
    v = np.asarray(v, dtype=np.float32).reshape(-1, 128)
    return np.ascontiguousarray(v.T)


def _host_cols(inputs):
    c = {}
    c["c_lncol"] = np.concatenate([_col_layout(inputs["L0_attn_a_ln_gamma"]),
                                   _col_layout(inputs["L0_attn_a_ln_beta"])], axis=1)
    b1 = np.asarray(inputs["L0_attn_a_b1"], dtype=np.float32)
    b2 = np.asarray(inputs["L0_attn_a_b2"], dtype=np.float32)
    be = np.asarray(inputs["L0_ffn_expert_b"], dtype=np.float32)
    c["c_bcol"] = np.concatenate([_col_layout(b1[0]), _col_layout(b1[1]),
                                  _col_layout(b2[0]), _col_layout(b2[1]),
                                  _col_layout(be)], axis=1)
    return c


def _const_specs():
    cc = _host_consts(0)
    specs = {k: (list(v.shape), BF16 if v.dtype == ml_dtypes.bfloat16 else F32)
             for k, v in cc.items()}
    specs["c_lncol"] = ([128, 2 * KT], F32)
    specs["c_bcol"] = ([128, 2 * 64 + 2 * KT + KT], F32)
    return specs


def build_program(upto=99, tap=None):
    nc = bass.Bass("TRN2", target_bir_lowering=False, debug=False)
    g = {}

    def dram_in(name, shape, dt=F32):
        g[name] = nc.dram_tensor(name, list(shape), dt, kind="ExternalInput").ap()

    dram_in("x_in", (TOK, H))
    for nm, shp in (
        ("L0_attn_a_Wq", (2, H, H)), ("L0_attn_a_Wk", (2, H, H)),
        ("L0_attn_a_Wv", (2, H, H)), ("L0_attn_a_Wg", (2, H, H)),
        ("L0_attn_a_Wo", (2, H, H)),
        ("L0_attn_a_gn_gamma", (2, H)), ("L0_attn_a_gn_beta", (2, H)),
        ("L0_attn_a_W1", (2, H, FF)), ("L0_attn_a_b1", (2, FF)),
        ("L0_attn_a_W2", (2, FF, H)), ("L0_attn_a_b2", (2, H)),
        ("L0_attn_a_ln_gamma", (H,)), ("L0_attn_a_ln_beta", (H,)),
        ("L0_attn_b_Wq", (H, H)), ("L0_attn_b_Wk", (H, H)),
        ("L0_attn_b_Wv", (H, H)), ("L0_attn_b_Wo", (H, H)),
        ("L0_ffn_router_w", (H, NE)), ("L0_ffn_router_b", (NE,)),
        ("L0_ffn_expert_w", (H, H)), ("L0_ffn_expert_b", (H,)),
    ):
        dram_in(nm, shp)
    for nm, (shp, dt) in _const_specs().items():
        dram_in(nm, shp, dt)

    g["y_out"] = nc.dram_tensor("y_out", [TOK, H], F32, kind="ExternalOutput").ap()
    g["spill_x"] = nc.dram_tensor("spill_x", [128, KT * TOK], F32).ap()
    g["cc_st_in"] = nc.dram_tensor("cc_st_in", [128, 512], BF16).ap()
    g["cc_st_out"] = nc.dram_tensor("cc_st_out", [128 * NCORE, 512], BF16).ap()
    g["cc_s_in"] = nc.dram_tensor("cc_s_in", [16, TOK], F32).ap()
    g["cc_s_out"] = nc.dram_tensor("cc_s_out", [16 * NCORE, TOK], F32).ap()
    g["cc_cs_in"] = nc.dram_tensor("cc_cs_in", [128, KT], F32).ap()
    g["cc_cs_out"] = nc.dram_tensor("cc_cs_out", [128 * NCORE, KT], F32).ap()

    ctx = ExitStack()
    E = ctx.enter_context
    with ctx:
        xT = E(nc.sbuf_tensor("xT", [128, KT * TOK], F32))
        xnT = E(nc.sbuf_tensor("xnT", [128, KT * TOK], BF16))
        big = E(nc.sbuf_tensor("big", [128, 16384], BF16))
        aux = E(nc.sbuf_tensor("aux", [128, 11264], BF16))
        wbuf = E(nc.sbuf_tensor("wbuf", [128, 2 * 4096], BF16))
        kT = E(nc.sbuf_tensor("kT", [128, 2 * TOK], BF16))
        vtk = E(nc.sbuf_tensor("vtk", [128, NTT * HD], BF16))
        gtk = E(nc.sbuf_tensor("gtk", [128, NTT * HD], BF16))
        scr = E(nc.sbuf_tensor("scr", [128, 4096], F32))
        rows = E(nc.sbuf_tensor("rows", [128, 1024], F32))
        cst = E(nc.sbuf_tensor("cst", [128, 224], F32))
        cstb = E(nc.sbuf_tensor("cstb", [128, 272], BF16))
        cidf = E(nc.sbuf_tensor("cidf", [128, 128], F32))
        cmask01 = E(nc.sbuf_tensor("cmask01", [128, 128], BF16))
        cqs = E(nc.sbuf_tensor("cqs", [128, 8 * 128], BF16))
        cinda = E(nc.sbuf_tensor("cinda", [128, 16], F32))
        cindb = E(nc.sbuf_tensor("cindb", [16, 128], F32))
        cindf = E(nc.sbuf_tensor("cindf", [128, 128], F32))
        lncol = E(nc.sbuf_tensor("lncol", [128, 2 * KT], F32))
        bcol = E(nc.sbuf_tensor("bcol", [128, 2 * 64 + 3 * KT], F32))
        gnga = E(nc.sbuf_tensor("gnga", [1, HD], BF16))
        gnbb = E(nc.sbuf_tensor("gnbb", [1, HD], BF16))
        rbrow = E(nc.sbuf_tensor("rbrow", [1, NE], F32))
        bis = E(nc.sbuf_tensor("bis", [16, 8], F32))

        psum = [E(nc.psum_tensor("pb%d" % i, [128, 512], F32)) for i in range(8)]

        zT = big
        Rpfx = aux[:, 0:4096]
        Rball = aux[:, 4096:6144]
        yhv = aux[:, 6144:8192]
        Rst = aux[:, 8192:10240].bitcast(F32)
        Rgcb = aux[:, 10240:10752]
        chz = aux[:, 10752:11264]
        h2 = big
        Sall = big[:, 0:2048].bitcast(F32)
        Smine = big[:, 2048:2304].bitcast(F32)
        SmT = big[0:16, 2304:4352].bitcast(F32)
        cmpb = big[:, 4352:6400].bitcast(F32)
        wBsb = big[:, 6400:8448].bitcast(F32)
        moescr = big[:, 8448:10496].bitcast(F32)
        kvh = big
        wTb = aux[:, 0:8192]
        abkv = aux[:, 8192:10240].bitcast(F32)

        p = P(nc, ctx)
        for snm in ("T", "V", "S", "DW0", "DW1", "DX0", "DX1", "DXC", "DXG", "DCU", "DCG"):
            p.sem(snm)
        NCC = 2 * HEADS + 3
        for i in range(NCC):
            p.sem("CC%d" % i)
        ccn = [0]
        ccbuf_guard = {}

        def mm(out, lhsT, rhs, start, stop, waits=(), inc=None):
            if stop:
                inc = "T"
            return p.emit("t", lambda t, o=out, l=lhsT, r=rhs, a=start, b=stop:
                          t.matmul(o, l, r, start=a, stop=b), waits=waits, inc=inc)

        def dma(queue, out, in_, sem, waits=()):
            return p.emit(queue, lambda q, o=out, i=in_: q.dma_start(out=o, in_=i),
                          waits=waits, inc=sem, inc_by=16)

        def vcopy(dst, src, waits=(), eng="v"):
            sem = "V" if eng == "v" else "S"
            return p.emit(eng, lambda e, o=dst, i=src:
                          (e.tensor_copy(o, i) if eng == "v" else e.copy(o, i)),
                          waits=waits, inc=sem)

        def tr128(dst, src, bank, ident_b, t_waits=(), v_waits=(), eng="v"):
            if ident_b:
                pv = psum[bank][:, 0:64].bitcast(BF16)
                idt = cstb[:, 0:128]
            else:
                pv = psum[bank][:, 0:128]
                idt = cidf[:]
            tv = p.emit("t", lambda t, o=pv, i=src: t.transpose(o, i, idt),
                        waits=list(t_waits) + p.bank_guard[bank], inc="T")
            sem = "V" if eng == "v" else "S"
            vv = p.emit(eng, lambda e, o=dst, i=pv:
                        (e.tensor_copy(o, i) if eng == "v" else e.copy(o, i)),
                        waits=[("T", tv)] + list(v_waits), inc=sem)
            p.bank_guard[bank] = [(sem, vv)]
            return vv

        wslot_guard = [[], []]

        def wload(slot, wap, r0, rk, c0, cw, layer=None):
            view = wbuf[:, slot * 4096: slot * 4096 + rk * cw]
            src = wap if layer is None else wap[layer]
            src = src[r0:r0 + rk * 128, c0:c0 + cw].rearrange("(k p) n -> p k n", p=128)
            step = max(1, 2048 // 128)
            d = None
            sd = "DW%d" % slot
            for ks in range(0, rk, step):
                ke = min(rk, ks + step)
                d = dma("p", view.rearrange("p (k n) -> p k n", n=cw)[:, ks:ke, :],
                        src[:, ks:ke, :], sd,
                        waits=wslot_guard[slot] if ks == 0 else ())
            return view, (sd, d)

        def wdone(slot):
            wslot_guard[slot] = [("T", p.cnt["T"])]

        dma("y", cidf[:], g["c_idf"], "DXC")
        dma("y", cstb[:, 0:128], g["c_idb"], "DXC")
        p.emit("v", lambda v: v.memset(cst[:, 0:1], 1.0), inc="V")
        p.emit("v", lambda v: v.memset(cst[:, 90:91], float(EPS)), inc="V")
        nc.const_aps.register(F32, float(EPS), cst[:, 90:91])
        p.emit("v", lambda v: v.memset(cst[0:1, 96:224], 1.0), inc="V")
        p.emit("v", lambda v: v.memset(cstb[:, 128:129], 1.0), inc="V")
        p.emit("v", lambda v: v.memset(cstb[0:1, 144:272], 1.0), inc="V")
        dma("y", cst[:, 1:9], g["c_vscale"], "DXC")
        dma("y", cst[:, 9:17], g["c_cmask"], "DXC")
        dma("y", cst[:, 17:81], g["c_gdecay"], "DXC")
        dma("y", cst[:, 81:82], g["c_selw"], "DXC")
        dma("y", cmask01[:], g["c_mask01"], "DXC")
        dma("y", cst[:, 82:90], g["c_negj"], "DXC")
        dma("y", cqs[:].rearrange("p (h i) -> p h i", h=8),
            g["c_qscale"].rearrange("h p i -> p h i"), "DXC")
        dma("y", cinda[:], g["c_inda"], "DXC")
        dma("y", cindb[:], g["c_indb"], "DXC")
        dma("y", cindf[:], g["c_indf"], "DXC")
        dma("y", lncol[:], g["c_lncol"], "DXC")
        dma("y", bcol[:], g["c_bcol"], "DXC")
        dma("y", rbrow[:], g["L0_ffn_router_b"][None, :], "DXC")
        CONST_DX = [("DXC", p.cnt["DXC"]), ("V", p.cnt["V"])]

        scrhalf_guard = [[], []]

        def stage_in(src_dram, add=False, extra=()):
            for tt in range(NTT):
                hbuf = scr[:, (tt % 2) * 2048:(tt % 2) * 2048 + 2048]
                sd = "DX%d" % (tt % 2)
                d = dma("y", hbuf, src_dram[tt * 128:(tt + 1) * 128, :], sd,
                        waits=scrhalf_guard[tt % 2] + list(extra))
                for k in range(KT):
                    b = 2 + (k % 2)
                    sl = slice(k * TOK + tt * 128, k * TOK + tt * 128 + 128)
                    if not add:
                        tr128(xT[:, sl], hbuf[:, k * 128:(k + 1) * 128], b, False,
                              t_waits=[(sd, d)] + CONST_DX)
                    else:
                        pv = psum[b][:, 0:128]
                        tv = p.emit("t", lambda t, o=pv, i=hbuf[:, k * 128:(k + 1) * 128]:
                                    t.transpose(o, i, cidf[:]),
                                    waits=[(sd, d)] + p.bank_guard[b], inc="T")
                        vv = p.emit("v", lambda v, o=xT[:, sl], i=pv:
                                    v.tensor_tensor(o, o, i, ALU.add),
                                    waits=[("T", tv)] + xg(), inc="V")
                        p.bank_guard[b] = [("V", vv)]
                scrhalf_guard[tt % 2] = [("T", p.cnt["T"])]
            xT_guard[0] = [("V", p.cnt["V"])]

        def stage_out(dst_dram, waits=()):
            for tt in range(NTT):
                hbuf = scr[:, (tt % 2) * 2048:(tt % 2) * 2048 + 2048]
                sd = "DX%d" % (tt % 2)
                last = None
                for k in range(KT):
                    b = 2 + (k % 2)
                    sl = slice(k * TOK + tt * 128, k * TOK + tt * 128 + 128)
                    last = tr128(hbuf[:, k * 128:(k + 1) * 128], xT[:, sl], b, False,
                                 t_waits=list(waits), v_waits=scrhalf_guard[tt % 2])
                d = dma("y", dst_dram[tt * 128:(tt + 1) * 128, :], hbuf, sd,
                        waits=[("V", last)])
                scrhalf_guard[tt % 2] = [(sd, d)]
            p.emit("y", lambda q: q.nop(),
                   waits=[("DX0", p.cnt["DX0"]), ("DX1", p.cnt["DX1"])])

        xT_guard = [[]]
        gn_guard = [[]]
        gath_guard = [[]]

        def xg():
            return list(xT_guard[0])

        bcopy_guard = [[], []]
        norm_guard = [[], []]

        def layernorm(dst_b16, src, in_waits, gamma_col=None, beta_col=None, dst_f32=None,
                      dst_waits=()):
            nv = ns = None
            for k in range(KT):
                bb = (vtk if k % 2 == 0 else kT)[:, 0:1024]
                sq = (vtk if k % 2 == 0 else kT)[:, 1024:2048]
                nv = p.emit("v", lambda v, o=bb, i=src[:, k * TOK:(k + 1) * TOK]:
                            v.tensor_copy(o, i),
                            waits=list(in_waits) + bcopy_guard[k % 2], inc="V")
                ns = p.emit("s", lambda s, o=sq, i=bb: s.activation(o, i, AF.Square),
                            waits=[("V", nv)] + bcopy_guard[k % 2], inc="S")
                for half in range(2):
                    mm(psum[half][0:1, :], cstb[:, 128:129], bb[:, half * 512:(half + 1) * 512],
                       start=(k == 0), stop=(k == KT - 1),
                       waits=[("V", nv)] + (p.bank_guard[half] if k == 0 else []),
                       inc="T")
                    mm(psum[2 + half][0:1, :], cstb[:, 128:129], sq[:, half * 512:(half + 1) * 512],
                       start=(k == 0), stop=(k == KT - 1),
                       waits=[("S", ns)] + (p.bank_guard[2 + half] if k == 0 else []),
                       inc="T")
                bcopy_guard[k % 2] = [("T", p.cnt["T"])]
            tdone = p.cnt["T"]
            bmm = None
            for half in range(2):
                vv = p.emit("v", lambda v, hh=half: v.tensor_scalar(
                    rows[0:1, 0:512], psum[hh][0:1, :], 1.0 / H, None, ALU.mult),
                    waits=[("T", tdone), ("T", p.cnt["T"])], inc="V")
                vv = p.emit("v", lambda v, hh=half: v.tensor_scalar(
                    rows[0:1, 512:1024], psum[2 + hh][0:1, :], 1.0 / H, None, ALU.mult),
                    waits=[("V", vv)], inc="V")
                bmm = mm(psum[4 + half][:, :], cst[0:1, 96:224], rows[0:1, 0:512],
                         start=True, stop=True,
                         waits=[("V", vv)] + p.bank_guard[4 + half])
                vv = p.emit("v", lambda v: v.tensor_tensor(
                    rows[0:1, 0:512], rows[0:1, 0:512], rows[0:1, 0:512], ALU.mult),
                    waits=[("V", vv), ("T", bmm)], inc="V")
                vv = p.emit("v", lambda v: v.tensor_tensor(
                    rows[0:1, 0:512], rows[0:1, 512:1024], rows[0:1, 0:512], ALU.subtract),
                    waits=[("V", vv)], inc="V")
                sv = p.emit("s", lambda s: s.activation(rows[0:1, 0:512], rows[0:1, 0:512],
                                                        AF.Ln, bias=float(EPS)),
                            waits=[("V", vv)], inc="S")
                sv = p.emit("s", lambda s: s.activation(rows[0:1, 0:512], rows[0:1, 0:512],
                                                        AF.Exp, scale=-0.5),
                            waits=[("S", sv)], inc="S")
                bmm = mm(psum[6 + half][:, :], cst[0:1, 96:224], rows[0:1, 0:512],
                         start=True, stop=True,
                         waits=[("S", sv)] + p.bank_guard[6 + half])
            vfin = p.cnt["V"]
            lasts = lastv = None
            half_done = {}
            hcnt = {}
            for half in range(2):
                for k in range(KT):
                    sl = slice(k * TOK + half * 512, k * TOK + (half + 1) * 512)
                    if dst_f32 is None:
                        t1 = scr[:, (k % 2) * 1024 + half * 512:(k % 2) * 1024 + half * 512 + 512]
                        exw = norm_guard[k % 2] + scrhalf_guard[0] + scrhalf_guard[1]
                    else:
                        t1 = dst_f32[:, sl]
                        exw = list(dst_waits)
                    lastv = p.emit("v", lambda v, o=t1, i=src[:, sl], mb=psum[4 + half][:, :]:
                                   v.tensor_tensor(o, i, mb, ALU.subtract),
                                   waits=[("T", bmm)] + exw, inc="V")
                    lastv = p.emit("v", lambda v, o=t1, rb=psum[6 + half][:, :]:
                                   v.tensor_tensor(o, o, rb, ALU.mult),
                                   waits=[("V", lastv)], inc="V")
                    if gamma_col is not None:
                        lasts = p.emit("s", lambda s, o=dst_b16[:, sl], i=t1,
                                       gc=gamma_col[:, k:k + 1], bc=beta_col[:, k:k + 1]:
                                       s.activation(o, i, AF.Identity, scale=gc, bias=bc),
                                       waits=[("V", lastv)], inc="S")
                    else:
                        lasts = p.emit("s", lambda s, o=dst_b16[:, sl], i=t1: s.copy(o, i),
                                       waits=[("V", lastv)], inc="S")
                    norm_guard[k % 2] = [("S", p.cnt["S"])]
                half_done[half] = [("S", p.cnt["S"]), ("V", p.cnt["V"])]
                hcnt[half] = (p.cnt["V"], p.cnt["S"])
            if dst_f32 is None:
                scrhalf_guard[0] = scrhalf_guard[0] + [("S", p.cnt["S"])]
                scrhalf_guard[1] = scrhalf_guard[1] + [("S", p.cnt["S"])]
            for b in range(4):
                p.bank_guard[b] = [("V", vfin)]
            for half in range(2):
                p.bank_guard[4 + half] = [("V", hcnt[half][0]), ("S", hcnt[half][1])]
                p.bank_guard[6 + half] = [("V", hcnt[half][0]), ("S", hcnt[half][1])]
            half_done["all"] = [("S", lasts), ("V", lastv)]
            return half_done

        def retention_layer(l):
            xnw = layernorm(xnT, xT, xg(), gamma_col=lncol[:, 0:KT], beta_col=lncol[:, KT:2 * KT])
            zT_guard = [[]]
            wq, dq = wload(0, g["L0_attn_a_Wq"], 0, KT, 0 * HD, HD, layer=l)
            pend = {"q": (wq, dq, 0)}
            fin = [None]
            for h in range(NH):
                retention_head(l, h, xnw, pend, zT_guard, fin)
            fin[0]()
            fin[0] = None
            wo_pass(l, zT_guard)
            ffn(l)

        yhv_guard = [[]]

        def retention_head(l, h, xnw, pend, zT_guard, fin):
            ccq = ccn[0]
            ccn[0] += 1
            gam128 = float(np.float32(GAMMAS[h]) ** 128)
            qT = big[:, (2 * h) * TOK:(2 * h + 2) * TOK]
            q2b = scr[:, 1024 + (h % 2) * 1024: 2048 + (h % 2) * 1024].bitcast(BF16)
            wq, dq, slq = pend["q"]
            wk, dk = wload(1 - slq, g["L0_attn_a_Wk"], 0, KT, h * HD, HD, layer=l)
            for dt in range(2):
                for half in range(2):
                    bk = dt * 2 + half
                    for k in range(KT):
                        mm(psum[bk][:, :], wq[:, k * HD + dt * 128: k * HD + dt * 128 + 128],
                           xnT[:, k * TOK + half * 512: k * TOK + (half + 1) * 512],
                           start=(k == 0), stop=(k == KT - 1),
                           waits=([dq] + list(xnw[half]) + p.bank_guard[bk]) if k == 0 else [])
                    vcopy(qT[:, dt * TOK + half * 512: dt * TOK + (half + 1) * 512],
                          psum[bk][:, :], waits=[("T", p.cnt["T"])], eng="s")
                    p.bank_guard[bk] = [("S", p.cnt["S"])]
            wdone(slq)
            wv, dv = wload(slq, g["L0_attn_a_Wv"], 0, KT, h * HD, HD, layer=l)
            for dt in range(2):
                for half in range(2):
                    bk = 4 + dt * 2 + half
                    for k in range(KT):
                        mm(psum[bk][:, :], wk[:, k * HD + dt * 128: k * HD + dt * 128 + 128],
                           xnT[:, k * TOK + half * 512: k * TOK + (half + 1) * 512],
                           start=(k == 0), stop=(k == KT - 1),
                           waits=([dk] + list(xnw[half]) + p.bank_guard[bk]) if k == 0 else [])
                    vcopy(kT[:, dt * TOK + half * 512: dt * TOK + (half + 1) * 512],
                          psum[bk][:, :], waits=[("T", p.cnt["T"])], eng="s")
                    p.bank_guard[bk] = [("S", p.cnt["S"])]
            wdone(1 - slq)
            wg_, dg = wload(1 - slq, g["L0_attn_a_Wg"], 0, KT, h * HD, HD, layer=l)
            if fin[0] is not None:
                fin[0]()
                fin[0] = None
            head_T0 = p.cnt["T"]
            rz = p.emit("v", lambda v: v.memset(Rst[:, 0:512], 0.0),
                        waits=[("V", p.cnt["V"]), ("S", p.cnt["S"])], inc="V")
            k_ready_S = p.cnt["S"]
            cons_T = [0, 0]
            for i in range(NTT):
                bk = i % 4
                for k in range(KT):
                    mm(psum[bk][:, 0:HD], xnT[:, k * TOK + i * 128: k * TOK + i * 128 + 128],
                       wv[:, k * HD:(k + 1) * HD],
                       start=(k == 0), stop=(k == KT - 1),
                       waits=([dv] + list(xnw[i // 4]) + p.bank_guard[bk]) if k == 0 else [])
                sv_ = vcopy(vtk[:, i * HD:(i + 1) * HD], psum[bk][:, 0:HD],
                            waits=[("T", p.cnt["T"])], eng="s")
                p.bank_guard[bk] = [("S", sv_)]
                bi = i % 2
                base = bi * 128
                ktok = rows[:, base:base + 128].bitcast(BF16)
                vp = rows[:, 512 + base:512 + base + 128].bitcast(BF16)
                pv = vcopy(Rpfx[:, i * 512:(i + 1) * 512], Rst[:, 0:512],
                           waits=[("V", p.cnt["V"]), ("T", head_T0)], eng="s")
                evtr = None
                for dt in range(2):
                    evtr = tr128(ktok[:, dt * 128:(dt + 1) * 128],
                                 kT[:, dt * TOK + i * 128: dt * TOK + i * 128 + 128],
                                 4 + bi, True,
                                 t_waits=[("S", k_ready_S)],
                                 v_waits=[("T", cons_T[bi])] if cons_T[bi] else [])
                vv2 = p.emit("v", lambda v, o=vp, i2=vtk[:, i * HD:(i + 1) * HD],
                             sc=cst[:, 1 + h:2 + h]:
                             v.tensor_scalar(o, i2, sc, None, ALU.mult),
                             waits=[("S", sv_), ("V", p.cnt["V"]),
                                    ("T", cons_T[bi])], inc="V")
                for dt in range(2):
                    mm(psum[6 + dt][:, 0:HD], ktok[:, dt * 128:(dt + 1) * 128], vp,
                       start=True, stop=True,
                       waits=[("V", vv2), ("V", evtr)] + p.bank_guard[6 + dt])
                tui = p.cnt["T"]
                cons_T[bi] = tui
                for dt in range(2):
                    uv = p.emit("v", lambda v, o=Rst[:, dt * 256:(dt + 1) * 256],
                                ps=psum[6 + dt][:, 0:HD], gm=gam128:
                                v.scalar_tensor_tensor(o, o, gm, ps, ALU.mult, ALU.add),
                                waits=[("T", tui), ("S", pv)], inc="V")
                    p.bank_guard[6 + dt] = [("V", uv)]
            pfx_S = p.cnt["S"]
            wdone(slq)
            if h + 1 < NH:
                wqn, dqn = wload(slq, g["L0_attn_a_Wq"], 0, KT, (h + 1) * HD, HD, layer=l)
                pend["q"] = (wqn, dqn, slq)
            bv = vcopy(Rgcb[:, 0:512], Rst[:, 0:512],
                       waits=[("V", p.cnt["V"])] + ccbuf_guard.get("st_gcb", []))
            dcc = dma("y", g["cc_st_in"], Rgcb[:, 0:512], "DCU",
                      waits=[("V", bv)] + ccbuf_guard.get("st_in", []))
            ccs = "CC%d" % ccq
            p.emit("p", lambda gp: gp.collective_compute(
                "AllGather", ALU.bypass, replica_groups=[list(range(NCORE))],
                ins=[g["cc_st_in"]], outs=[g["cc_st_out"]]),
                waits=[("DCU", dcc)] + ccbuf_guard.get("st_out", []), inc=ccs)
            ccbuf_guard["st_in"] = [(ccs, 1)]
            for tt in range(NTT):
                bk = tt % 4
                for k in range(KT):
                    mm(psum[bk][:, 0:HD], xnT[:, k * TOK + tt * 128: k * TOK + tt * 128 + 128],
                       wg_[:, k * HD:(k + 1) * HD],
                       start=(k == 0), stop=(k == KT - 1),
                       waits=([dg] + list(xnw[tt // 4]) + p.bank_guard[bk]) if k == 0 else [])
                sg = p.emit("s", lambda s, o=gtk[:, tt * HD:(tt + 1) * HD], i2=psum[bk][:, 0:HD]:
                            s.activation(o, i2, AF.Sigmoid),
                            waits=[("T", p.cnt["T"])] + yhv_guard[0], inc="S")
                gv = p.emit("v", lambda v, o=gtk[:, tt * HD:(tt + 1) * HD],
                            ps=psum[bk][:, 0:HD]:
                            v.tensor_tensor(o, o, ps, ALU.mult),
                            waits=[("S", sg)], inc="V")
                p.bank_guard[bk] = [("V", gv)]
            wdone(1 - slq)
            pend["slot_for_next"] = 1 - slq
            qsc = cqs[:, h * 128:(h + 1) * 128]
            consq_T = [0, 0]
            for i in range(NTT):
                bi = i % 2
                base = bi * 256
                q1 = rows[:, base:base + 128].bitcast(BF16)
                smT = rows[:, 576 + bi * 64:576 + bi * 64 + 64].bitcast(BF16)
                vq = None
                for dt in range(2):
                    vq = p.emit("v", lambda v, o=q1[:, dt * 128:(dt + 1) * 128],
                                i2=qT[:, dt * TOK + i * 128: dt * TOK + i * 128 + 128], qs=qsc:
                                v.tensor_tensor(o, i2, qs, ALU.mult),
                                waits=([("T", consq_T[bi])] if consq_T[bi] else []) +
                                      [("V", p.cnt["V"])], inc="V")
                bsc = 4 + bi * 2
                for dt in range(2):
                    mm(psum[bsc][:, 0:128], kT[:, dt * TOK + i * 128: dt * TOK + i * 128 + 128],
                       q1[:, dt * 128:(dt + 1) * 128],
                       start=(dt == 0), stop=(dt == 1),
                       waits=([("V", vq)] + p.bank_guard[bsc]) if dt == 0 else [])
                tsc = p.cnt["T"]
                for dt in range(2):
                    p.emit("v", lambda v, o=q2b[:, dt * TOK + i * 128: dt * TOK + i * 128 + 128],
                           i2=q1[:, dt * 128:(dt + 1) * 128],
                           gm=float(np.float32(GAMMAS[h]) ** (128 * i)):
                           v.tensor_scalar(o, i2, gm, None, ALU.mult),
                           waits=[("T", tsc), ("V", vq)], inc="V")
                sv2 = p.emit("v", lambda v, o=smT, ps=psum[bsc][:, 0:128],
                             nj=cst[:, 82 + h:83 + h], mk=cmask01[:]:
                             v.scalar_tensor_tensor(o, ps, nj, mk, ALU.mult, ALU.mult),
                             waits=[("T", tsc)], inc="V")
                p.bank_guard[bsc] = [("V", sv2)]
                bky = 5 + bi * 2
                mm(psum[bky][:, 0:HD], smT, vtk[:, i * HD:(i + 1) * HD],
                   start=True, stop=False,
                   waits=[("V", sv2), ("V", vq), ("S", pfx_S)] + p.bank_guard[bky])
                for dt in range(2):
                    mm(psum[bky][:, 0:HD], q1[:, dt * 128:(dt + 1) * 128],
                       Rpfx[:, i * 512 + dt * 256: i * 512 + (dt + 1) * 256],
                       start=False, stop=(dt == 1), waits=[])
                tyi = p.cnt["T"]
                yv = vcopy(yhv[:, i * HD:(i + 1) * HD], psum[bky][:, 0:HD],
                           waits=[("T", tyi)] + yhv_guard[0], eng="s")
                p.bank_guard[bky] = [("S", yv)]
                consq_T[bi] = tyi

            def stage2():
                Gc = Rst[:, 512:1024]
                gv = p.emit("v", lambda v: v.memset(Gc, 0.0), waits=[("V", p.cnt["V"])], inc="V")
                for halfq in range(2):
                    d2 = dma("y", Rball[:, 0:2048].rearrange("p (r n) -> p r n", r=4),
                             g["cc_st_out"].rearrange("(r p) n -> p r n", p=128)[:, halfq * 4:(halfq + 1) * 4, :],
                             "DCG", waits=[(ccs, 1), ("V", gv)] + gath_guard[0])
                    for rr in range(4):
                        r = halfq * 4 + rr
                        gv2 = p.emit("v", lambda v, o=Gc, i=Rball[:, rr * 512:(rr + 1) * 512],
                                     sc=cst[:, 17 + h * 8 + r:18 + h * 8 + r]:
                                     v.scalar_tensor_tensor(o, i, sc, o, ALU.mult, ALU.add),
                                     waits=[("DCG", d2), ("V", gv)], inc="V")
                        gv = gv2
                gath_guard[0] = [("V", gv)]
                ccbuf_guard["st_out"] = [("DCG", p.cnt["DCG"])]
                gcb = vcopy(Rgcb[:, 0:512], Gc, waits=[("V", gv), ("DCU", dcc)])
                q2done = p.cnt["V"]
                for i in range(NTT):
                    bi = i % 2
                    bky = 5 + bi * 2
                    for dt in range(2):
                        mm(psum[bky][:, 0:HD], q2b[:, dt * TOK + i * 128: dt * TOK + i * 128 + 128],
                           Rgcb[:, dt * 256:(dt + 1) * 256],
                           start=(dt == 0), stop=(dt == 1),
                           waits=([("V", gcb), ("V", q2done)] + p.bank_guard[bky]) if dt == 0 else [])
                    tyg = p.cnt["T"]
                    yg = p.emit("v", lambda v, o=yhv[:, i * HD:(i + 1) * HD],
                                ps=psum[bky][:, 0:HD]:
                                v.tensor_tensor(o, o, ps, ALU.add),
                                waits=[("T", tyg), ("S", p.cnt["S"])], inc="V")
                    p.bank_guard[bky] = [("V", yg)]
                ccbuf_guard["st_gcb"] = [("T", p.cnt["T"])]
                ydone = [("S", p.cnt["S"])]
                bn1 = None
                for cq in range(8):
                    bn1 = p.emit("v", lambda v, c2=cq: v.bn_stats(
                        rows[:, 640 + c2 * 6:646 + c2 * 6], yhv[:, c2 * 256:(c2 + 1) * 256]),
                        waits=ydone + [("V", p.cnt["V"])], inc="V")
                bn2 = None
                for cq in range(8):
                    bn2 = p.emit("v", lambda v, c2=cq: v.bn_aggr(
                        rows[:, 688 + c2 * 2:690 + c2 * 2], rows[:, 640 + c2 * 6:646 + c2 * 6]),
                        waits=[("V", bn1)], inc="V")
                sv = p.emit("s", lambda s: s.activation(rows[:, 704:712],
                            rows[:, 688:704].rearrange("p (c n) -> p c n", c=8)[:, :, 1:2],
                            AF.Ln, bias=float(EPS)), waits=[("V", bn2)], inc="S")
                sv = p.emit("s", lambda s: s.activation(rows[:, 704:712], rows[:, 704:712],
                                                        AF.Exp, scale=-0.5),
                            waits=[("S", sv)], inc="S")
                dgn = dma("p", gnga[:], g["L0_attn_a_gn_gamma"][l][None, h * HD:(h + 1) * HD],
                          "DXG", waits=gn_guard[0])
                dgn = dma("p", gnbb[:], g["L0_attn_a_gn_beta"][l][None, h * HD:(h + 1) * HD],
                          "DXG", waits=())
                gmm = mm(psum[4][:, 0:HD], cstb[0:1, 144:272], gnga[:], start=True, stop=True,
                         waits=[("DXG", dgn)] + p.bank_guard[4])
                gmm = mm(psum[5][:, 0:HD], cstb[0:1, 144:272], gnbb[:], start=True, stop=True,
                         waits=p.bank_guard[5])
                gn_guard[0] = [("T", p.cnt["T"])]
                lastz = None
                for i in range(NTT):
                    t1 = scr[:, 0:256]
                    zv = p.emit("v", lambda v, o=t1, i2=yhv[:, i * HD:(i + 1) * HD],
                                mc=rows[:, 688 + 2 * i:689 + 2 * i], rc=rows[:, 704 + i:705 + i]:
                                v.tensor_scalar(o, i2, mc, rc, ALU.subtract, op1=ALU.mult),
                                waits=[("S", sv), ("V", p.cnt["V"]), ("T", gmm)], inc="V")
                    zv = p.emit("v", lambda v, o=t1, gb=psum[4][:, 0:HD]:
                                v.tensor_tensor(o, o, gb, ALU.mult), waits=[("V", zv)], inc="V")
                    zv = p.emit("v", lambda v, o=t1, bb=psum[5][:, 0:HD]:
                                v.tensor_tensor(o, o, bb, ALU.add), waits=[("V", zv)], inc="V")
                    zv = p.emit("v", lambda v, o=chz[:, 0:256], i2=t1, gk=gtk[:, i * HD:(i + 1) * HD]:
                                v.tensor_tensor(o, i2, gk, ALU.mult), waits=[("V", zv)], inc="V")
                    for dt in range(2):
                        kz = h * 2 + dt
                        lastz = tr128(zT[:, kz * TOK + i * 128: kz * TOK + i * 128 + 128],
                                      chz[:, dt * 128:(dt + 1) * 128], 6 + dt, True,
                                      t_waits=[("V", zv)])
                p.bank_guard[4] = [("V", p.cnt["V"])]
                p.bank_guard[5] = [("V", p.cnt["V"])]
                zT_guard[0] = zT_guard[0] + [("V", p.cnt["V"])]
                yhv_guard[0] = [("V", p.cnt["V"])]

            fin[0] = stage2

        def wo_pass(l, zT_guard):
            slot = [0]
            wo, do = wload(0, g["L0_attn_a_Wo"], 0, KT, 0, HD, layer=l)
            for sl8 in range(8):
                won, don = (wload(1 - slot[0], g["L0_attn_a_Wo"], 0, KT, (sl8 + 1) * HD, HD, layer=l)
                            if sl8 + 1 < 8 else (None, None))
                for co in range(2):
                    ct = sl8 * 2 + co
                    for half in range(2):
                        bk = (ct * 2 + half) % 4
                        for k in range(KT):
                            mm(psum[bk][:, :], wo[:, k * HD + co * 128: k * HD + co * 128 + 128],
                               zT[:, k * TOK + half * 512: k * TOK + (half + 1) * 512],
                               start=(k == 0), stop=(k == KT - 1),
                               waits=([do] + zT_guard[0] + p.bank_guard[bk]) if k == 0 else [])
                        sl = slice(ct * TOK + half * 512, ct * TOK + (half + 1) * 512)
                        rv = p.emit("v", lambda v, o=xT[:, sl], ps=psum[bk][:, :]:
                                    v.tensor_tensor(o, o, ps, ALU.add),
                                    waits=[("T", p.cnt["T"])] + xg(), inc="V")
                        p.bank_guard[bk] = [("V", rv)]
                wdone(slot[0])
                if won is not None:
                    wo, do = won, don
                    slot[0] = 1 - slot[0]
            xT_guard[0] = [("V", p.cnt["V"])]

        def ffn(l):
            rnw = layernorm(xnT, xT, xg(), gamma_col=lncol[:, 0:KT], beta_col=lncol[:, KT:2 * KT])
            nffs = 4
            cur = [0]
            wv_, dv_ = wload(0, g["L0_attn_a_W1"], 0, KT, 0, HD, layer=l)
            for ffs in range(nffs):
                for s8 in range(8):
                    if s8 < 7:
                        nxt = wload(1 - cur[0], g["L0_attn_a_W1"], 0, KT,
                                    (ffs * 16 + (s8 + 1) * 2) * 128, HD, layer=l)
                    else:
                        nxt = wload(1 - cur[0], g["L0_attn_a_W2"], ffs * 2048, KT, 0, HD, layer=l)
                    for fi in range(2):
                        ft = ffs * 16 + s8 * 2 + fi
                        for half in range(2):
                            bk = (fi * 2 + half) % 4
                            for k in range(KT):
                                mm(psum[bk][:, :], wv_[:, k * HD + fi * 128: k * HD + fi * 128 + 128],
                                   xnT[:, k * TOK + half * 512: k * TOK + (half + 1) * 512],
                                   start=(k == 0), stop=(k == KT - 1),
                                   waits=([dv_] + list(rnw[half]) + p.bank_guard[bk]) if k == 0 else [])
                            gelu_evac(psum[bk], h2[:, (ft % 16) * TOK + half * 512:
                                                   (ft % 16) * TOK + (half + 1) * 512],
                                      bcol[:, l * 64 + ft: l * 64 + ft + 1], bk)
                    wdone(cur[0])
                    wv_, dv_ = nxt
                    cur[0] = 1 - cur[0]
                h2done = [("V", p.cnt["V"])]
                for cs in range(8):
                    if cs < 7:
                        nxt = wload(1 - cur[0], g["L0_attn_a_W2"], ffs * 2048, KT,
                                    (cs + 1) * HD, HD, layer=l)
                    elif ffs + 1 < nffs:
                        nxt = wload(1 - cur[0], g["L0_attn_a_W1"], 0, KT,
                                    (ffs + 1) * 2048, HD, layer=l)
                    else:
                        nxt = None
                    for co in range(2):
                        ct = cs * 2 + co
                        for half in range(2):
                            bk = (co * 2 + half) % 4 + 4
                            for k in range(KT):
                                mm(psum[bk][:, :], wv_[:, k * HD + co * 128: k * HD + co * 128 + 128],
                                   h2[:, k * TOK + half * 512: k * TOK + (half + 1) * 512],
                                   start=(k == 0), stop=(k == KT - 1),
                                   waits=([dv_] + h2done + p.bank_guard[bk]) if k == 0 else [])
                            sl = slice(ct * TOK + half * 512, ct * TOK + (half + 1) * 512)
                            if ffs == 0:
                                rv = p.emit("v", lambda v, o=xT[:, sl], ps=psum[bk][:, :],
                                            bc=bcol[:, 128 + l * KT + ct:129 + l * KT + ct]:
                                            v.scalar_tensor_tensor(o, ps, bc, o, ALU.add, ALU.add),
                                            waits=[("T", p.cnt["T"])] + xg(), inc="V")
                            else:
                                rv = p.emit("v", lambda v, o=xT[:, sl], ps=psum[bk][:, :]:
                                            v.tensor_tensor(o, o, ps, ALU.add),
                                            waits=[("T", p.cnt["T"])] + xg(), inc="V")
                            p.bank_guard[bk] = [("V", rv)]
                    wdone(cur[0])
                    if nxt is not None:
                        wv_, dv_ = nxt
                        cur[0] = 1 - cur[0]
            xT_guard[0] = [("V", p.cnt["V"])]
            sg2 = [("V", p.cnt["V"]), ("S", p.cnt["S"])]
            scrhalf_guard[0] = scrhalf_guard[0] + sg2
            scrhalf_guard[1] = scrhalf_guard[1] + sg2

        gelu_scr_guard = [[], [], [], []]

        def gelu_evac(ps, dst, b1c, bk):
            gi = bk
            xs = scr[:, gi * 1024: gi * 1024 + 512]
            t2 = scr[:, gi * 1024 + 512: gi * 1024 + 1024]
            sv = p.emit("s", lambda s, o=xs, i=ps[:, :], bc=b1c:
                        s.activation(o, i, AF.Identity, bias=bc),
                        waits=[("T", p.cnt["T"])] + gelu_scr_guard[gi] + scrhalf_guard[gi // 2], inc="S")
            sv2 = p.emit("s", lambda s, o=t2, i=xs: s.activation(o, i, AF.Square),
                         waits=[("S", sv)], inc="S")
            vv = p.emit("v", lambda v, o=t2: v.tensor_scalar(o, o, C3G, C1G, ALU.mult, op1=ALU.add),
                        waits=[("S", sv2)], inc="V")
            vv = p.emit("v", lambda v, o=t2, i=xs: v.tensor_tensor(o, o, i, ALU.mult),
                        waits=[("V", vv)], inc="V")
            sv3 = p.emit("s", lambda s, o=t2: s.activation(o, o, AF.Sigmoid),
                         waits=[("V", vv)], inc="S")
            vv = p.emit("v", lambda v, o=dst, a=xs, b=t2: v.tensor_tensor(o, a, b, ALU.mult),
                        waits=[("S", sv3)], inc="V")
            gelu_scr_guard[gi] = [("V", vv)]
            p.bank_guard[bk] = [("S", sv)]

        def moe():
            ccq = ccn[0]
            ccn[0] += 1
            moescr_guard = [[], []]
            xnw = layernorm(xnT, xT, xg())
            ln_T = p.cnt["T"]
            scratch = [big[:, 10496 + i * 1024: 11520 + i * 1024] for i in range(5)] \
                + [kT[:, 0:1024], kT[:, 1024:2048],
                   vtk[:, 0:1024], vtk[:, 1024:2048],
                   gtk[:, 0:1024], gtk[:, 1024:2048]]
            NPRE = len(scratch)
            we, de = wload(1, g["L0_ffn_expert_w"], 0, KT, 0, HD)
            res = {"wv": None}
            evS = {}

            def expert_mms():
                weL, deL = we, de
                cur = 1
                for cs in range(8):
                    for co in range(2):
                        ct = cs * 2 + co
                        for half in range(2):
                            bk = (co * 2 + half) % 4
                            for k in range(KT):
                                mm(psum[bk][:, :],
                                   weL[:, k * HD + co * 128: k * HD + co * 128 + 128],
                                   xnT[:, k * TOK + half * 512: k * TOK + (half + 1) * 512],
                                   start=(k == 0), stop=(k == KT - 1),
                                   waits=([deL] + list(xnw[half]) + p.bank_guard[bk])
                                         if k == 0 else [])
                                yield
                            if ct < NPRE:
                                xw = [("T", ln_T)] if ct >= 5 else []
                                ev = p.emit("s", lambda s,
                                            o=scratch[ct][:, half * 512:(half + 1) * 512],
                                            i=psum[bk][:, :],
                                            bc=bcol[:, 128 + 2 * KT + ct:129 + 2 * KT + ct]:
                                            s.activation(o, i, AF.Identity, bias=bc),
                                            waits=[("T", p.cnt["T"])] + xw, inc="S")
                                evS[ct] = ev
                                p.bank_guard[bk] = [("S", ev)]
                            else:
                                assert res["wv"] is not None
                                sl = slice(ct * TOK + half * 512, ct * TOK + (half + 1) * 512)
                                ms = moescr[:, half * 512:(half + 1) * 512]
                                ev = p.emit("s", lambda s, o=ms, i=psum[bk][:, :],
                                            bc=bcol[:, 128 + 2 * KT + ct:129 + 2 * KT + ct]:
                                            s.activation(o, i, AF.Identity, bias=bc),
                                            waits=[("T", p.cnt["T"])] + moescr_guard[half],
                                            inc="S")
                                ev2 = p.emit("v", lambda v, o=ms,
                                             wb=wBsb[:, half * 512:(half + 1) * 512]:
                                             v.tensor_tensor(o, o, wb, ALU.mult),
                                             waits=[("S", ev), ("V", res["wv"])], inc="V")
                                ev2 = p.emit("v", lambda v, o=xT[:, sl], i=ms:
                                             v.tensor_tensor(o, o, i, ALU.add),
                                             waits=[("V", ev2)] + xg(), inc="V")
                                moescr_guard[half] = [("V", ev2)]
                                p.bank_guard[bk] = [("S", ev), ("V", ev2)]
                    nxt = (wload(1 - cur, g["L0_ffn_expert_w"], 0, KT, (cs + 1) * HD, HD)
                           if cs < 7 else None)
                    wdone(cur)
                    if nxt is not None:
                        weL, deL = nxt
                        cur = 1 - cur

            gen = expert_mms()

            def pump(n):
                for _ in range(n):
                    if next(gen, "END") == "END":
                        break

            rw = wbuf[:, 0:256]
            drw = None
            src = g["L0_ffn_router_w"].rearrange("(k p) n -> p k n", p=128)
            drw = dma("p", rw.rearrange("p (k n) -> p k n", n=16), src, "DW0",
                      waits=wslot_guard[0])
            drw = ("DW0", drw)
            smw = None
            for tt in range(NTT):
                bk = 4 + tt % 2
                for k in range(KT):
                    mm(psum[bk][:, 0:16], xnT[:, k * TOK + tt * 128: k * TOK + tt * 128 + 128],
                       rw[:, k * 16:(k + 1) * 16],
                       start=(k == 0), stop=False,
                       waits=([drw] + list(xnw[tt // 4]) + p.bank_guard[bk]) if k == 0 else [])
                mm(psum[bk][:, 0:16], cst[0:1, 96:224], rbrow[:],
                   start=False, stop=True, waits=CONST_DX, inc="T")
                tl = p.cnt["T"]
                blk = Smine[:, tt * 16:(tt + 1) * 16]
                nm = rows[:, 0:1]
                vv = p.emit("v", lambda v, o=nm, i=psum[bk][:, 0:16]:
                            v.reduce_max(o, i, axis=AX.X, negate=True),
                            waits=[("T", tl), ("V", p.cnt["V"])], inc="V")
                sv = p.emit("s", lambda s, o=blk, i=psum[bk][:, 0:16], b=nm,
                            ac=rows[:, 1:2]:
                            s.activation(o, i, AF.Exp, bias=b, accum_out=ac),
                            waits=[("V", vv), ("S", p.cnt["S"])], inc="S")
                vv = p.emit("v", lambda v, o=rows[:, 2:3], i=rows[:, 1:2]:
                            v.reciprocal(o, i), waits=[("S", sv)], inc="V")
                smw = p.emit("v", lambda v, o=blk, sc=rows[:, 2:3]:
                             v.tensor_scalar(o, o, sc, None, ALU.mult),
                             waits=[("V", vv)], inc="V")
                p.bank_guard[bk] = [("S", sv)]
                bkt = 2 + bk
                pvt = psum[bkt][0:16, 0:128]
                tvt = p.emit("t", lambda t, o=pvt, i=blk: t.transpose(o, i, cidf[:]),
                             waits=[("V", smw)] + p.bank_guard[bkt], inc="T")
                svt = p.emit("v", lambda v, o=SmT[:, tt * 128:(tt + 1) * 128], i=pvt:
                             v.tensor_copy(o, i), waits=[("T", tvt)], inc="V")
                p.bank_guard[bkt] = [("V", svt)]
                pump(4)
            wdone(0)
            stw = p.cnt["V"]
            dsm = dma("y", g["cc_s_in"], SmT, "DCU", waits=[("V", stw)] + ccbuf_guard.get("s", []))
            ccs = "CC%d" % ccq
            p.emit("p", lambda gp: gp.collective_compute(
                "AllGather", ALU.bypass, replica_groups=[list(range(NCORE))],
                ins=[g["cc_s_in"]], outs=[g["cc_s_out"]]),
                waits=[("DCU", dsm)], inc=ccs)
            d2 = dma("y", Sall, g["cc_s_out"], "DCG", waits=[(ccs, 1)])
            ccbuf_guard["s"] = [(ccs, 1), ("DCG", d2)]
            taucol = rows[:, 3:4]
            csum = rows[:, 4:5]
            locol = rows[:, 5:6]
            selcol = rows[:, 6:7]
            midcol = rows[:, 7:8]
            last = {"v": p.cnt["V"], "s": p.cnt["S"]}

            def _stepA(it):
                a = p.emit("v", lambda v, c=0.5 ** (it + 1): v.tensor_scalar(
                    midcol, locol, c, None, ALU.add),
                    waits=[("V", last["v"])], inc="V")
                a = p.emit("v", lambda v: v.tensor_scalar(cmpb[:], Sall, midcol, 0.0,
                                                          ALU.is_ge, op1=ALU.add,
                                                          accum_out=csum),
                           waits=[("V", a)] + ([("DCG", d2)] if it == 0 else []), inc="V")
                last["v"] = a

            def _stepB(it):
                tb = mm(psum[4][:, 0:1], cindf[:], csum, start=True, stop=True,
                        waits=[("V", last["v"])] + p.bank_guard[4]
                              + (CONST_DX if it == 0 else []),
                        inc="T")
                a = p.emit("v", lambda v: v.tensor_scalar(selcol, psum[4][:, 0:1],
                                                          float(KTOP), None, ALU.is_gt),
                           waits=[("T", tb), ("V", last["v"])], inc="V")
                p.bank_guard[4] = [("V", a)]
                a = p.emit("v", lambda v, c=0.5 ** (it + 1): v.tensor_scalar(
                    selcol, selcol, c, None, ALU.mult),
                    waits=[("V", a)], inc="V")
                a = p.emit("v", lambda v: v.tensor_tensor(locol, locol, selcol, ALU.add),
                           waits=[("V", a)], inc="V")
                last["v"] = a

            def _stepF():
                a = p.emit("v", lambda v, c=0.5 ** BISECT_ITERS: v.tensor_scalar(
                    taucol, locol, c, None, ALU.add),
                    waits=[("V", last["v"])], inc="V")
                a = p.emit("v", lambda v: v.tensor_scalar(cmpb[:], Sall, taucol, None,
                                                          ALU.is_ge),
                           waits=[("V", a)], inc="V")
                a = p.emit("v", lambda v: v.tensor_tensor(cmpb[:], cmpb[:], Sall, ALU.mult),
                           waits=[("V", a)], inc="V")
                last["v"] = a

            def _stepW(half):
                wmm = mm(psum[4][0:1, :], cst[:, 81:82], cmpb[:, half * 512:(half + 1) * 512],
                         start=True, stop=True,
                         waits=[("V", last["v"])] + p.bank_guard[4], inc="T")
                a = p.emit("v", lambda v, hh=half: v.tensor_copy(
                    rows[0:1, hh * 512:(hh + 1) * 512], psum[4][0:1, :]),
                    waits=[("T", wmm)], inc="V")
                p.bank_guard[4] = [("V", a)]
                wm2 = mm(psum[5 + half][:, :], cst[0:1, 96:224],
                         rows[0:1, half * 512:(half + 1) * 512],
                         start=True, stop=True, waits=[("V", a)] + p.bank_guard[5 + half],
                         inc="T")
                a = p.emit("v", lambda v, hh=half: v.tensor_copy(
                    wBsb[:, hh * 512:(hh + 1) * 512], psum[5 + hh][:, :]),
                    waits=[("T", wm2)], inc="V")
                p.bank_guard[5 + half] = [("V", a)]
                last["v"] = a
                if half == 1:
                    res["wv"] = a

            def _memsets():
                a = p.emit("v", lambda v: v.memset(locol, 0.0),
                           waits=[("V", last["v"])], inc="V")
                last["v"] = a
                _stepA(0)

            steps = [(4, _memsets), (200, lambda: _stepB(0))]
            for it in range(1, BISECT_ITERS):
                steps.append((3, lambda it=it: _stepA(it)))
                steps.append((3, lambda it=it: _stepB(it)))
            steps.append((3, _stepF))
            steps.append((3, lambda: _stepW(0)))
            steps.append((3, lambda: _stepW(1)))

            for gap, fn in steps:
                pump(gap)
                fn()
            pump(1 << 30)
            for ti in range(NPRE):
                for half in range(2):
                    tmp = cmpb[:, half * 512:(half + 1) * 512]
                    sl = slice(ti * TOK + half * 512, ti * TOK + (half + 1) * 512)
                    c1 = p.emit("v", lambda v, o=tmp,
                                a=scratch[ti][:, half * 512:(half + 1) * 512],
                                wb=wBsb[:, half * 512:(half + 1) * 512]:
                                v.tensor_tensor(o, a, wb, ALU.mult),
                                waits=[("S", evS[ti]), ("V", res["wv"])], inc="V")
                    p.emit("v", lambda v, o=xT[:, sl], i=tmp:
                           v.tensor_tensor(o, o, i, ALU.add),
                           waits=[("V", c1)] + xg(), inc="V")
            xT_guard[0] = [("V", p.cnt["V"])]

        def attn_b():
            ccq = ccn[0]
            ccn[0] += 1
            wsl_guard = [[], []]
            wTb_guard = [0, 0]
            totals_done = []
            xnw = layernorm(xnT, xT, xg())
            wslA = [[[], []], [[], []]]

            def stg(wi, par):
                return scr[:, wi * 2048 + par * 1024: wi * 2048 + par * 1024 + 1024].bitcast(BF16)

            for et in range(KT):
                par = et % 2
                for wi, wnm in ((0, "L0_attn_b_Wk"), (1, "L0_attn_b_Wv")):
                    if True:
                        bufb = stg(wi, par)
                        sd = "DW%d" % wi
                        d = dma("p", bufb, g[wnm][et * 128:(et + 1) * 128, :], sd,
                                waits=wslA[wi][par] + (scrhalf_guard[0] + scrhalf_guard[1] if et < 2 else []))
                        for k in range(KT):
                            tr128(wTb[:, wi * 2048 + k * 128: wi * 2048 + (k + 1) * 128],
                                  bufb[:, k * 128:(k + 1) * 128], 2 + (k % 2), True,
                                  t_waits=[(sd, d)],
                                  v_waits=[("T", wTb_guard[wi])] if wTb_guard[wi] else [])
                        wslA[wi][par] = [("T", p.cnt["T"])]
                    wtr = p.cnt["V"]
                    wtr_s = p.cnt["S"]
                    for half in range(2):
                        bk = 4 + wi * 2 + half
                        for k in range(KT):
                            mm(psum[bk][:, :],
                               wTb[:, wi * 2048 + k * 128: wi * 2048 + (k + 1) * 128],
                               xnT[:, k * TOK + half * 512: k * TOK + (half + 1) * 512],
                               start=(k == 0), stop=(k == KT - 1),
                               waits=([("V", wtr)] + list(xnw[half]) + p.bank_guard[bk]) if k == 0 else [])
                        if wi == 0:
                            ev = vcopy(kT[:, half * 512:(half + 1) * 512], psum[bk][:, :],
                                       waits=[("T", p.cnt["T"])], eng="s")
                            p.bank_guard[bk] = [("S", ev)]
                        else:
                            ev = vcopy(vtk[:, half * 512:(half + 1) * 512], psum[bk][:, :],
                                       waits=[("T", p.cnt["T"]), ("V", p.cnt["V"])], eng="s")
                            p.bank_guard[bk] = [("S", ev)]
                            p.emit("v", lambda v, o=abkv[:, half * 512:(half + 1) * 512],
                                   vv_=vtk[:, half * 512:(half + 1) * 512],
                                   kk=kT[:, half * 512:(half + 1) * 512]:
                                   v.tensor_tensor(o, vv_, kk, ALU.mult),
                                   waits=[("S", ev), ("V", p.cnt["V"])], inc="V")
                wTb_guard[0] = p.cnt["T"]
                wTb_guard[1] = p.cnt["T"]
                kvw = p.cnt["V"]
                tv = p.emit("v", lambda v, o=rows[:, et:et + 1], i=abkv[:, 0:1024]:
                            v.tensor_reduce(o, i, AX.X, ALU.add),
                            waits=[("V", kvw)], inc="V")
                sc = p.emit("v", lambda v, o=kvh[:, et * TOK:(et + 1) * TOK], i=abkv[:, 0:1024]:
                            v.tensor_tensor_scan(o, i, i, 0.0, ALU.add, ALU.bypass),
                            waits=[("V", kvw)], inc="V")
                totals_done = [("V", p.cnt["V"])]
            dcc = dma("y", g["cc_cs_in"], rows[:, 0:KT], "DCU", waits=totals_done)
            ccs = "CC%d" % ccq
            dq0 = [None, None]
            for et0 in range(2):
                dq0[et0] = dma("p", stg(0, et0), g["L0_attn_b_Wq"][et0 * 128:(et0 + 1) * 128, :],
                               "DW0", waits=wslA[0][et0])
            p.emit("p", lambda gp: gp.collective_compute(
                "AllGather", ALU.bypass, replica_groups=[list(range(NCORE))],
                ins=[g["cc_cs_in"]], outs=[g["cc_cs_out"]]),
                waits=[("DCU", dcc)], inc=ccs)
            d2 = dma("y", rows[:, 32:160].rearrange("p (r n) -> p r n", r=8),
                     g["cc_cs_out"].rearrange("(r p) n -> p r n", p=128), "DCG",
                     waits=[(ccs, 1)])
            cv = None
            for et in range(KT):
                par = et % 2
                bufq = stg(0, par)
                d = (dq0[et] if et < 2 else
                     dma("p", bufq, g["L0_attn_b_Wq"][et * 128:(et + 1) * 128, :], "DW0",
                         waits=wslA[0][par]))
                for k in range(KT):
                    tr128(wTb[:, k * 128:(k + 1) * 128], bufq[:, k * 128:(k + 1) * 128],
                          2 + (k % 2), True, t_waits=[("DW0", d)])
                wslA[0][par] = [("T", p.cnt["T"])]
                wtr = p.cnt["V"]
                if et == 0:
                    cv = p.emit("v", lambda v: v.memset(rows[:, 16:32], 0.0),
                                waits=[("DCG", d2)], inc="V")
                    for r in range(NCORE):
                        cv = p.emit("v", lambda v, rr=r: v.scalar_tensor_tensor(
                            rows[:, 16:32], rows[:, 32 + rr * 16:48 + rr * 16],
                            cst[:, 9 + rr:10 + rr],
                            rows[:, 16:32], ALU.mult, ALU.add),
                            waits=[("V", cv)], inc="V")
                for half in range(2):
                    bk = 4 + half
                    for k in range(KT):
                        mm(psum[bk][:, :], wTb[:, k * 128:(k + 1) * 128],
                           xnT[:, k * TOK + half * 512: k * TOK + (half + 1) * 512],
                           start=(k == 0), stop=(k == KT - 1),
                           waits=([("V", wtr)] + p.bank_guard[bk]) if k == 0 else [])
                    ev = vcopy(kT[:, half * 512:(half + 1) * 512], psum[bk][:, :],
                               waits=[("T", p.cnt["T"])], eng="s")
                    p.bank_guard[bk] = [("S", ev)]
                qw = p.cnt["S"]
                tv = p.emit("v", lambda v, o=abkv[:, 0:1024], i=kvh[:, et * TOK:(et + 1) * TOK],
                            cc=rows[:, 16 + et:17 + et]:
                            v.tensor_scalar(o, i, cc, DENOM_B, ALU.add, op1=ALU.mult),
                            waits=[("V", cv), ("V", p.cnt["V"]), ("S", qw)], inc="V")
                tv = p.emit("v", lambda v, o=kvh[:, et * TOK:(et + 1) * TOK],
                            a=abkv[:, 0:1024], b=kT[:, 0:1024]:
                            v.tensor_tensor(o, a, b, ALU.mult),
                            waits=[("V", tv)], inc="V")
            vhat_done = [("V", p.cnt["V"])]
            for ft in range(KT):
                par = ft % 2
                bufo = stg(0, par)
                d = dma("p", bufo, g["L0_attn_b_Wo"][ft * 128:(ft + 1) * 128, :], "DW0",
                        waits=wslA[0][par])
                for k in range(KT):
                    tr128(wTb[:, k * 128:(k + 1) * 128], bufo[:, k * 128:(k + 1) * 128],
                          2 + (k % 2), True, t_waits=[("DW0", d)])
                wslA[0][par] = [("T", p.cnt["T"])]
                wtr = p.cnt["V"]
                for half in range(2):
                    bk = 4 + half
                    for k in range(KT):
                        mm(psum[bk][:, :], wTb[:, k * 128:(k + 1) * 128],
                           kvh[:, k * TOK + half * 512: k * TOK + (half + 1) * 512],
                           start=(k == 0), stop=(k == KT - 1),
                           waits=([("V", wtr)] + vhat_done + p.bank_guard[bk]) if k == 0 else [])
                    sl = slice(ft * TOK + half * 512, ft * TOK + (half + 1) * 512)
                    rv = p.emit("v", lambda v, o=xT[:, sl], ps=psum[bk][:, :]:
                                v.tensor_tensor(o, o, ps, ALU.add),
                                waits=[("T", p.cnt["T"])] + xg(), inc="V")
                    p.bank_guard[bk] = [("V", rv)]
            xT_guard[0] = [("V", p.cnt["V"])]

        stage_in(g["x_in"])

        if upto == 0:
            stage_out(g["y_out"], waits=xg())
        elif tap == "ln_affine":
            w = layernorm(xnT, xT, xg(), gamma_col=lncol[:, 0:KT], beta_col=lncol[:, KT:2 * KT])["all"]
            for k in range(KT):
                vcopy(xT[:, k * TOK:(k + 1) * TOK], xnT[:, k * TOK:(k + 1) * TOK], waits=w)
            stage_out(g["y_out"], waits=[("V", p.cnt["V"])])
        elif tap == "ln_plain":
            w = layernorm(xnT, xT, xg())["all"]
            for k in range(KT):
                vcopy(xT[:, k * TOK:(k + 1) * TOK], xnT[:, k * TOK:(k + 1) * TOK], waits=w)
            stage_out(g["y_out"], waits=[("V", p.cnt["V"])])
        else:
            for c in range(4):
                dma("y", g["spill_x"][:, c * 4096:(c + 1) * 4096],
                    xT[:, c * 4096:(c + 1) * 4096], "DX0",
                    waits=xg() if c == 0 else ())
            spill_done = [("DX0", p.cnt["DX0"])]
            lnw = layernorm(xnT, xT, xg(), dst_f32=xT, dst_waits=spill_done)
            xT_guard[0] = lnw["all"]
            if tap == "msr0":
                xnw2 = layernorm(xnT, xT, xg(), gamma_col=lncol[:, 0:KT],
                                 beta_col=lncol[:, KT:2 * KT])
                wq, dq = wload(0, g["L0_attn_a_Wq"], 0, KT, 0, HD, layer=0)
                pend = {"q": (wq, dq, 0)}
                ztg = [[]]
                if NH < HEADS:
                    p.emit("v", lambda v: v.memset(big[:], 0.0), inc="V")
                fin2 = [None]
                for h in range(NH):
                    retention_head(0, h, xnw2, pend, ztg, fin2)
                fin2[0]()
                zv0 = p.emit("v", lambda v: v.memset(xT[:], 0.0),
                             waits=[("V", p.cnt["V"]), ("S", p.cnt["S"]),
                                    ("T", p.cnt["T"])], inc="V")
                xT_guard[0] = [("V", zv0)]
                wo_pass(0, ztg)
                stage_out(g["y_out"], waits=xg())
            else:
                for l in range(2):
                    retention_layer(l)
                for c in range(8):
                    hbuf = scr[:, (c % 2) * 2048:(c % 2) * 2048 + 2048]
                    dld = dma("y", hbuf, g["spill_x"][:, c * 2048:(c + 1) * 2048],
                              "DX1", waits=scrhalf_guard[c % 2] + (spill_done if c < 2 else []))
                    av = p.emit("v", lambda v, o=xT[:, c * 2048:(c + 1) * 2048], i=hbuf:
                                v.tensor_tensor(o, o, i, ALU.add),
                                waits=[("DX1", dld)] + xg(), inc="V")
                    scrhalf_guard[c % 2] = [("V", av)]
                xT_guard[0] = [("V", p.cnt["V"])]
                if upto == 1:
                    stage_out(g["y_out"], waits=xg())
                else:
                    moe()
                    if upto == 2:
                        stage_out(g["y_out"], waits=xg())
                    else:
                        attn_b()
                        if upto == 3:
                            stage_out(g["y_out"], waits=xg())
                        else:
                            moe()
                            stage_out(g["y_out"], waits=xg())

        p.finalize(E(nc.Block()))
    return nc


_CACHE = {}


def kernel(**inputs):
    upto = int(inputs.pop("_upto", 99))
    tap = inputs.pop("_tap", None)
    key = (upto, tap)
    if key not in _CACHE:
        _CACHE[key] = build_program(upto, tap)
    nc = _CACHE[key]
    x = np.ascontiguousarray(np.asarray(inputs["L0_attn_a_x"], dtype=np.float32).reshape(SEQ, H))
    cols = _host_cols(inputs)
    in_maps = []
    for c in range(NCORE):
        m = {"x_in": x[c * TOK:(c + 1) * TOK]}
        for k, v in inputs.items():
            if k == "L0_attn_a_x":
                continue
            m[k] = np.ascontiguousarray(np.asarray(v, dtype=np.float32))
        m.update(_host_consts(c))
        m.update(cols)
        in_maps.append(m)
    res = run_bass_kernel_spmd(nc, in_maps, core_ids=list(range(NCORE)))
    out = np.concatenate([r["y_out"] for r in res.results], axis=0)
    return out.reshape(1, SEQ, H)
